# version 12
# baseline (speedup 1.0000x reference)
import contextlib
import numpy as np
import ml_dtypes

import concourse.bass as bass
import concourse.mybir as mybir
from concourse.bass_types import AP
from concourse.bass_utils import run_bass_kernel_spmd

F32 = mybir.dt.float32
BF16 = mybir.dt.bfloat16
I8 = mybir.dt.int8
AF = mybir.ActivationFunctionType
ALU = mybir.AluOpType
AX = mybir.AxisListType

NCORES = 8
D = 2048
DC = 16
S = 4096
B = 2
NTOK = 1024
HID = 8192
QSC = 1.0 / float(np.sqrt(np.float32(128.0)))
ATANH_HALF = 0.5493061443340549
MAGIC = 12582912.0
NELW = {1: 8192 * 2048, 2: 8192 * 2048, 3: 2048 * 8192}


def mkap(t, offset, dims):
    base = t.ap()
    return AP(base.tensor, offset, [[st, cn] for (st, cn) in dims])


def build(debug=False, stop_after="full", dbg2=False):
    nc = bass.Bass("TRN2", target_bir_lowering=False,
                   detect_race_conditions=False)

    x_sl = nc.dram_tensor("x_sl", [NTOK, D], F32, kind="ExternalInput").ap()
    wqkv_sl = nc.dram_tensor("wqkv_sl", [D, 768], F32, kind="ExternalInput").ap()
    wo_sl = nc.dram_tensor("wo_sl", [256, D], F32, kind="ExternalInput").ap()
    wf_sl = nc.dram_tensor("wf_sl", [768, HID], F32, kind="ExternalInput").ap()
    g1b_in = nc.dram_tensor("g1b", [128, D], F32, kind="ExternalInput").ap()
    g2b_in = nc.dram_tensor("g2b", [128, D], F32, kind="ExternalInput").ap()
    g3b_in = nc.dram_tensor("g3b", [128, HID], F32, kind="ExternalInput").ap()
    ident_in = nc.dram_tensor("ident", [128, 128], BF16, kind="ExternalInput").ap()
    mask_in = nc.dram_tensor("maskt", [128, 128], F32, kind="ExternalInput").ap()
    onec_in = nc.dram_tensor("onec", [128, 1], F32, kind="ExternalInput").ap()
    oner_in = nc.dram_tensor("oner", [1, 128], F32, kind="ExternalInput").ap()
    onecb_in = nc.dram_tensor("onecb", [128, 1], BF16, kind="ExternalInput").ap()
    onerb_in = nc.dram_tensor("onerb", [1, 128], BF16, kind="ExternalInput").ap()
    onesq_in = nc.dram_tensor("onesq", [128, 128], BF16, kind="ExternalInput").ap()

    y_sl = nc.dram_tensor("y_sl", [NTOK, D], F32, kind="ExternalOutput").ap()

    dbg = {}
    if debug:
        for nm, shp, dt in [("xn", [NTOK, D], F32), ("q0", [128, S], BF16),
                            ("k0", [128, S], BF16), ("v0", [128, S], BF16),
                            ("afm0", [128, 2 * S], BF16), ("s", [NTOK, D], BF16),
                            ("bcs", [128, 16], F32), ("xq1", [128, 8 * D], BF16),
                            ("h0", [128, HID], BF16), ("negm", [128, 1], F32),
                            ("rc", [128, 24], F32)]:
            dbg[nm] = nc.dram_tensor("dbg_" + nm, shp, dt, kind="ExternalOutput").ap()

    if dbg2:
        dbg["d2_xn"] = nc.dram_tensor("d2_xn", [NCORES * D, NTOK], BF16,
                                      kind="ExternalOutput").ap()
        dbg["d2_s"] = nc.dram_tensor("d2_s", [NTOK, D], BF16,
                                     kind="ExternalOutput").ap()
        dbg["d2_rsin"] = nc.dram_tensor("d2_rsin", [B * S, D], BF16,
                                        kind="ExternalOutput").ap()
        dbg["d2_xq"] = nc.dram_tensor("d2_xq", [128, 8 * D], BF16,
                                      kind="ExternalOutput").ap()
        dbg["d2_tern"] = nc.dram_tensor("d2_tern", [768, HID], BF16,
                                        kind="ExternalOutput").ap()
        dbg["d2_ri0"] = nc.dram_tensor("d2_ri0", [B * S, D], BF16,
                                       kind="ExternalOutput").ap()
        dbg["d2_h16"] = nc.dram_tensor("d2_h16", [128, 4 * HID], BF16,
                                       kind="ExternalOutput").ap()
        dbg["d2_rc"] = nc.dram_tensor("d2_rc", [128, 24], F32,
                                      kind="ExternalOutput").ap()
    ag_xn_in = nc.dram_tensor("ag_xn_in", [D, NTOK], BF16)
    ag_xn_out = nc.dram_tensor("ag_xn_out", [NCORES * D, NTOK], BF16)
    ar_in = nc.dram_tensor("ar_in", [128, 4], F32)
    ar_out = nc.dram_tensor("ar_out", [128, 4], F32)
    ag_t_in = nc.dram_tensor("ag_t_in", [768, HID], BF16)
    ag_t_out = nc.dram_tensor("ag_t_out", [NCORES * 768, HID], BF16)
    rs_in = nc.dram_tensor("rs_in", [B * S, D], BF16)
    rs_out = nc.dram_tensor("rs_out", [NTOK, D], BF16)
    xqf_d = [nc.dram_tensor(f"xqf_d{m}", [128, 8 * D], BF16) for m in range(2)]

    import os as _os
    es = contextlib.ExitStack()
    SEMS, CNT = {}, {}
    with es:
        def _sem(name):
            if name not in SEMS:
                SEMS[name] = es.enter_context(nc.semaphore(name))
                CNT[name] = 0
            return SEMS[name]

        for snm in ["pe", "act", "dve"]:
            _sem(snm)

        def inc(instr, name, k=1):
            _sem(name)
            if name in ("dve", "act"):
                return CNT[name]
            step = 1 if (name in ("pe", "pool") or name.startswith("cc")) \
                else 16
            instr.then_inc(SEMS[name], k * step)
            CNT[name] += k * step
            return CNT[name]

        def wt(eng, name, val):
            _sem(name)
            assert val <= CNT[name], f"wait on future {name}: {val} > {CNT[name]}"
            if val > 0:
                eng.wait_ge(SEMS[name], val)

        def wtt(eng, tup):
            if tup:
                wt(eng, tup[0], tup[1])

        def cum(name):
            _sem(name)
            return (name, CNT[name])

        IDN = es.enter_context(nc.sbuf_tensor("IDN", [128, 128], BF16))
        MSK = es.enter_context(nc.sbuf_tensor("MSK", [128, 128], F32))
        ONEC = es.enter_context(nc.sbuf_tensor("ONEC", [128, 1], F32))
        ONER = es.enter_context(nc.sbuf_tensor("ONER", [1, 128], F32))
        ONECB = es.enter_context(nc.sbuf_tensor("ONECB", [128, 1], BF16))
        ONERB = es.enter_context(nc.sbuf_tensor("ONERB", [1, 128], BF16))
        ONESQ = es.enter_context(nc.sbuf_tensor("ONESQ", [128, 128], BF16))
        EPS5 = es.enter_context(nc.sbuf_tensor("EPS5", [128, 1], F32))
        EPS8 = es.enter_context(nc.sbuf_tensor("EPS8", [128, 1], F32))
        BCS = es.enter_context(nc.sbuf_tensor("BCS", [128, 16], F32))
        PWS = es.enter_context(nc.sbuf_tensor("PWS", [128, 4], F32))
        ARS = es.enter_context(nc.sbuf_tensor("ARS", [128, 4], F32))
        AM12 = es.enter_context(nc.sbuf_tensor("AM12", [128, 16], F32))
        RC1 = es.enter_context(nc.sbuf_tensor("RC1", [128, 8], F32))
        RC2 = es.enter_context(nc.sbuf_tensor("RC2", [128, 8], F32))
        RC3 = es.enter_context(nc.sbuf_tensor("RC3", [128, 8], F32))
        TOT = es.enter_context(nc.sbuf_tensor("TOT", [1, 16], F32))
        ROW = es.enter_context(nc.sbuf_tensor("ROW", [1, 16], F32))

        import os as _os2
        _PRE = _os2.environ.get("KPRE", "cc")
        if _PRE == "all":
            for _nm in ["ccA", "ccB", "ccC", "ccD",
                        "sCONST", "sXT0", "sXT1", "sWFT0", "sWFT1", "sAGI",
                        "sARI", "sARO", "sG12", "sXQS", "sTRN0", "sTRN1", "sW2",
                        "sRHS0", "sRHS1", "sRSB0", "sRSB1", "sG3", "sXQQ",
                        "sW12_0", "sW12_1", "sW3_0", "sW3_1", "sFIN", "sOO0",
                        "sOO1", "oodbg"]:
                _sem(_nm)
        elif _PRE == "cc":
            for _nm in ["ccA", "ccB", "ccC", "ccD", "pool"]:
                _sem(_nm)

        block = es.enter_context(nc.Block())

        class Chain:
            def __init__(self, eng, name):
                self._e = eng
                self._n = name

            def wait_ge(self, sem, val):
                return self._e.wait_ge(sem, val)

            def __getattr__(self, attr):
                real = getattr(self._e, attr)
                if not callable(real):
                    return real
                nm = self._n

                def call(*a, **k):
                    if CNT[nm] > 0:
                        self._e.wait_ge(SEMS[nm], CNT[nm])
                    instr = real(*a, **k)
                    if instr is not None and hasattr(instr, "then_inc"):
                        instr.then_inc(SEMS[nm], 1)
                        CNT[nm] += 1
                    return instr
                return call

        def V_(f):
            block.vector(lambda v: f(Chain(v, "dve")))

        def A_(f):
            block.scalar(lambda a: f(Chain(a, "act")))

        S_, T_, G_ = block.sync, block.tensor, block.gpsimd

        def _finish_early():
            def f(gpsimd):
                for nm in list(SEMS):
                    wt(gpsimd, nm, CNT[nm])
            G_(f)


        esXN = contextlib.ExitStack()
        XN = esXN.enter_context(nc.sbuf_tensor("XN", [128, 8 * D], F32))
        esLN = contextlib.ExitStack()
        XT = [esLN.enter_context(nc.sbuf_tensor(f"XT{i}", [128, D], F32))
              for i in range(2)]
        XBF = esLN.enter_context(nc.sbuf_tensor("XBF", [128, D], BF16))
        SCR = esLN.enter_context(nc.sbuf_tensor("SCR1", [128, D], F32))
        ST1 = esLN.enter_context(nc.sbuf_tensor("ST1", [128, 1], F32))
        ST2 = esLN.enter_context(nc.sbuf_tensor("ST2", [128, 1], F32))
        ST3 = esLN.enter_context(nc.sbuf_tensor("ST3", [128, 1], F32))
        esTF = contextlib.ExitStack()
        XTF = esTF.enter_context(nc.sbuf_tensor("XTF", [128, DC * NTOK], BF16))

        if int(_os.environ.get("KCONSTS", "1")):
            def f(sync):
                inc(sync.dma_start(out=IDN[:], in_=ident_in), "sCONST")
                inc(sync.dma_start(out=MSK[:], in_=mask_in), "sCONST")
                inc(sync.dma_start(out=ONEC[:], in_=onec_in), "sCONST")
                inc(sync.dma_start(out=ONER[:], in_=oner_in), "sCONST")
                inc(sync.dma_start(out=ONECB[:], in_=onecb_in), "sCONST")
                inc(sync.dma_start(out=ONERB[:], in_=onerb_in), "sCONST")
                inc(sync.dma_start(out=ONESQ[:], in_=onesq_in), "sCONST")
            S_(f)
        d_consts = cum("sCONST")

        def f(vector):
            wtt(vector, d_consts)
            vector.memset(EPS5[:], 1e-5)
            vector.memset(EPS8[:], 1e-8)
            inc(vector.memset(PWS[:], 0.0), "dve")
        V_(f)

        esP = contextlib.ExitStack()
        PTB = esP.enter_context(nc.psum_tensor("PTB1", [128, 128], BF16))
        _NTILES = int(_os.environ.get("KNT", "8"))
        ln_done = [0] * 8
        tr_evac = [0]
        ln_tr_pe = [0]
        xdma = [None] * 8
        for i in range(_NTILES):
            def f(sync, i=i):
                if i >= 2:
                    wt(sync, "dve", ln_done[i - 2])
                inc(sync.dma_start(out=XT[i % 2][:],
                                   in_=x_sl[i * 128:(i + 1) * 128, :]), f"sXT{i % 2}")
                xdma[i] = cum(f"sXT{i % 2}")
            S_(f)

            def f(scalar, i=i):
                wtt(scalar, xdma[i])
                if i > 0:
                    wt(scalar, "dve", ln_done[i - 1])
                inc(scalar.activation(SCR[:], XT[i % 2][:], AF.Square,
                                      bias=0.0, scale=1.0, accum_out=ST2[:]), "act")
            A_(f)
            a_sq = CNT["act"]

            def f(vector, i=i, a_sq=a_sq):
                wtt(vector, xdma[i])
                inc(vector.reduce_sum(ST1[:], XT[i % 2][:], AX.X), "dve")
                inc(vector.tensor_scalar_mul(ST1[:], ST1[:], -1.0 / D), "dve")
                wt(vector, "act", a_sq)
                inc(vector.tensor_mul(ST3[:], ST1[:], ST1[:]), "dve")
                inc(vector.scalar_tensor_tensor(ST3[:], ST2[:], 1.0 / D, ST3[:],
                                                ALU.mult, ALU.subtract), "dve")
            V_(f)
            dve_var = CNT["dve"]

            def f(scalar, dve_var=dve_var):
                wt(scalar, "dve", dve_var)
                inc(scalar.activation(ST2[:], ST3[:], AF.Sqrt, bias=EPS5[:],
                                      scale=1.0), "act")
            A_(f)
            a_rstd = CNT["act"]

            def f(vector, i=i, a_rstd=a_rstd, ltp=ln_tr_pe[0]):
                wt(vector, "act", a_rstd)
                inc(vector.reciprocal(ST2[:], ST2[:]), "dve")
                inc(vector.tensor_scalar(XN[:, i * D:(i + 1) * D], XT[i % 2][:],
                                         ST1[:], ST2[:], ALU.add, ALU.mult), "dve")
                if ltp:
                    wt(vector, "pe", ltp)
                inc(vector.tensor_copy(XBF[:], XN[:, i * D:(i + 1) * D]), "dve")
            V_(f)
            ln_done[i] = CNT["dve"]

            for ic in range(DC):
                def f(tensor, i=i, ic=ic, g=ln_done[i], te=tr_evac[-1]):
                    wt(tensor, "dve", g)
                    wt(tensor, "dve", te)
                    wtt(tensor, d_consts)
                    inc(tensor.transpose(PTB[:], XBF[:, ic * 128:(ic + 1) * 128],
                                         IDN[:]), "pe")
                T_(f)
                pe_tr = CNT["pe"]

                def f(vector, i=i, ic=ic, pe_tr=pe_tr):
                    wt(vector, "pe", pe_tr)
                    inc(vector.tensor_copy(
                        XTF[:, ic * NTOK + i * 128: ic * NTOK + (i + 1) * 128],
                        PTB[:]), "dve")
                V_(f)
                tr_evac.append(CNT["dve"])
            ln_tr_pe[0] = CNT["pe"]
        act_xtf = CNT["act"]
        dve_xtf = CNT["dve"]

        _kag = int(_os.environ.get("KAG", "1"))
        if _kag:
            def f(gpsimd):
                wt(gpsimd, "dve", dve_xtf)
                if _kag in (1, 2):
                    for part in range(4):
                        inc(gpsimd.dma_start(
                            out=ag_xn_in.ap()[part * 512:(part + 1) * 512, :]
                            .rearrange("(i p) t -> p i t", p=128),
                            in_=XTF[:, part * 4 * NTOK:(part + 1) * 4 * NTOK]
                            .rearrange("p (i t) -> p i t", i=4)), "sAGI")
                    wt(gpsimd, "sAGI", CNT["sAGI"])
                if _kag in (1, 3):
                    inc(gpsimd.collective_compute(
                        "AllGather", ALU.bypass,
                        replica_groups=[list(range(NCORES))],
                        ins=[ag_xn_in.ap().opt()], outs=[ag_xn_out.ap().opt()]),
                        "ccA")
            G_(f)
        cc_agxn = cum("ccA")
        dG_agxn = cum("sAGI")
        esTF.close()
        if stop_after == "ln":
            _finish_early()
            esLN.close(); esXN.close(); esP.close()
            return nc

        if debug:
            def f(sync):
                wt(sync, "dve", ln_done[7])
                inc(sync.dma_start(out=dbg["xn"].rearrange("(i p) c -> p i c", p=128),
                                   in_=XN[:].rearrange("p (i c) -> p i c", i=8)), "oodbg")
            S_(f)

        esG = contextlib.ExitStack()
        G1B = esG.enter_context(nc.sbuf_tensor("G1B", [128, D], F32))
        G2B = esG.enter_context(nc.sbuf_tensor("G2B", [128, D], F32))
        YT = esG.enter_context(nc.sbuf_tensor("YT", [128, D], F32))
        YG = esG.enter_context(nc.sbuf_tensor("YGg", [128, D], F32))
        XQ8 = esG.enter_context(nc.sbuf_tensor("XQ8g", [128, D], F32))
        XQB = esG.enter_context(nc.sbuf_tensor("XQBg", [128, D], BF16))
        XSTG = esG.enter_context(nc.sbuf_tensor("XSTG", [128, D], BF16))
        AMX = esG.enter_context(nc.sbuf_tensor("AMXg", [128, 1], F32))
        SCLg = esG.enter_context(nc.sbuf_tensor("SCLg", [128, 1], F32))
        SSQ = esG.enter_context(nc.sbuf_tensor("SSQg", [128, 1], F32))
        RSTD = esG.enter_context(nc.sbuf_tensor("RSTDg", [128, 1], F32))
        esGP = contextlib.ExitStack()
        PTG = esGP.enter_context(nc.psum_tensor("PTG", [128, 128], BF16))

        def f(sync):
            wtt(sync, dG_agxn)
            inc(sync.dma_start(out=G1B[:], in_=g1b_in), "sG12")
            inc(sync.dma_start(out=G2B[:], in_=g2b_in), "sG12")
        S_(f)
        d_g12 = cum("sG12")

        xstg_dma = [None]
        last_tr_pe = [0]
        prev_dve_y = [0]
        for i in range(8):
            def f(scalar, i=i, pdy=prev_dve_y[0]):
                wt(scalar, "dve", ln_done[i])
                if i == 0:
                    wtt(scalar, d_g12)
                if pdy:
                    wt(scalar, "dve", pdy)
                inc(scalar.activation(SCR[:], XN[:, i * D:(i + 1) * D], AF.Square,
                                      bias=0.0, scale=1.0, accum_out=SSQ[:]), "act")
                inc(scalar.activation(RSTD[:], SSQ[:], AF.Sqrt, bias=EPS8[:],
                                      scale=1.0 / D), "act")
            A_(f)
            a_rst = CNT["act"]

            def f(vector, i=i, a_rst=a_rst):
                wt(vector, "act", a_rst)
                inc(vector.reciprocal(RSTD[:], RSTD[:]), "dve")
                inc(vector.tensor_scalar_mul(YT[:], XN[:, i * D:(i + 1) * D], RSTD[:]),
                    "dve")
            V_(f)
            dve_y = CNT["dve"]
            prev_dve_y[0] = dve_y

            for mi, GB in enumerate((G1B, G2B)):
                def f(vector, mi=mi, GB=GB, i=i, dve_y=dve_y):
                    wt(vector, "dve", dve_y)
                    inc(vector.tensor_mul(YG[:], YT[:], GB[:]), "dve")
                    inc(vector.reduce_max(AMX[:], YG[:], AX.X,
                                          apply_absolute_value=True), "dve")
                    inc(vector.tensor_scalar_max(AMX[:], AMX[:], 1e-4), "dve")
                    inc(vector.tensor_copy(AM12[:, mi * 8 + i: mi * 8 + i + 1],
                                           AMX[:]), "dve")
                V_(f)
                dve_am = CNT["dve"]

                def f(vector, dve_am=dve_am, ltr=last_tr_pe[0]):
                    wt(vector, "dve", dve_am)
                    if ltr:
                        wt(vector, "pe", ltr)
                    inc(vector.reciprocal(SCLg[:], AMX[:]), "dve")
                    inc(vector.tensor_scalar(XQ8[:], YG[:], SCLg[:], 127.0,
                                             ALU.mult, ALU.mult), "dve")
                    inc(vector.tensor_scalar(YG[:], XQ8[:], MAGIC, MAGIC,
                                             ALU.add, ALU.subtract), "dve")
                    inc(vector.tensor_scalar(XQB[:], YG[:], 127.0, -128.0,
                                             ALU.min, ALU.max), "dve")
                V_(f)
                dve_xq = CNT["dve"]

                stg_evac = [0]
                for ic in range(DC):
                    def f(tensor, ic=ic, dve_xq=dve_xq, se=stg_evac[-1],
                          xd=xstg_dma[-1]):
                        wt(tensor, "dve", dve_xq)
                        wt(tensor, "dve", se)
                        if ic == 0:
                            wtt(tensor, xd)
                        inc(tensor.transpose(PTG[:], XQB[:, ic * 128:(ic + 1) * 128],
                                             IDN[:]), "pe")
                    T_(f)
                    pe_t = CNT["pe"]

                    def f(vector, ic=ic, pe_t=pe_t):
                        wt(vector, "pe", pe_t)
                        inc(vector.tensor_copy(XSTG[:, ic * 128:(ic + 1) * 128],
                                               PTG[:]), "dve")
                    V_(f)
                    stg_evac.append(CNT["dve"])
                last_tr_pe[0] = CNT["pe"]

                def f(sync, mi=mi, i=i, se=stg_evac[-1]):
                    wt(sync, "dve", se)
                    inc(sync.dma_start(out=xqf_d[mi].ap()[:, i * D:(i + 1) * D],
                                       in_=XSTG[:]), "sXQS")
                    xstg_dma.append(cum("sXQS"))
                S_(f)
        act_xqf = CNT["act"]
        dve_g_end = CNT["dve"]
        dS_xqf = cum("sXQS")

        if debug:
            def f(sync):
                wtt(sync, dS_xqf)
                inc(sync.dma_start(out=dbg["xq1"], in_=xqf_d[0].ap()), "oodbg")
                wt(sync, "dve", CNT["dve"])
                inc(sync.dma_start(out=dbg["rc"][:, 0:16], in_=AM12[:]), "oodbg")
            S_(f)
        esGP.close()
        esG.close()
        if stop_after == "g":
            _finish_early()
            esLN.close(); esXN.close(); esP.close()
            return nc

        esP.close()
        esLN.close()
        esXN.close()

        es2 = contextlib.ExitStack()
        WFT = [es2.enter_context(nc.sbuf_tensor(f"WFT{i}", [128, 2048], F32))
               for i in range(2)]
        SG1 = es2.enter_context(nc.sbuf_tensor("SG1", [128, 2048], BF16))
        SG2 = es2.enter_context(nc.sbuf_tensor("SG2", [128, 2048], BF16))
        TRN = [es2.enter_context(nc.sbuf_tensor(f"TRN{i}", [128, 2048], BF16))
               for i in range(2)]
        ST3C = es2.enter_context(nc.sbuf_tensor("ST3C", [128, 1], F32))
        WQB = es2.enter_context(nc.sbuf_tensor("WQB", [128, DC * 768], BF16))
        WOB = es2.enter_context(nc.sbuf_tensor("WOB", [128, 2 * D], BF16))
        esWQ = contextlib.ExitStack()
        WQF = esWQ.enter_context(nc.sbuf_tensor("WQF", [128, DC * 768], F32))
        WOF = esWQ.enter_context(nc.sbuf_tensor("WOF", [128, 2 * D], F32))

        def f(sync):
            wt(sync, "act", act_xqf)
            wt(sync, "dve", dve_g_end)
            wtt(sync, dG_agxn)
            wtt(sync, dS_xqf)
            for part in range(4):
                inc(sync.dma_start(
                    out=WQF[:, part * 4 * 768:(part + 1) * 4 * 768]
                    .rearrange("p (i c) -> p i c", i=4),
                    in_=wqkv_sl[part * 512:(part + 1) * 512, :]
                    .rearrange("(i p) c -> p i c", p=128)), "sW2")
            inc(sync.dma_start(out=WOF[:].rearrange("p (i c) -> p i c", i=2),
                               in_=wo_sl.rearrange("(i p) c -> p i c", p=128)), "sW2")
        S_(f)
        d_w2 = cum("sW2")

        def f(vector):
            wtt(vector, d_w2)
            inc(vector.tensor_copy(WQB[:], WQF[:]), "dve")
            inc(vector.tensor_copy(WOB[:], WOF[:]), "dve")
        V_(f)
        dve_wcast = CNT["dve"]
        esWQ.close()

        QF = [es2.enter_context(nc.sbuf_tensor(f"QF{h}", [128, S], BF16))
              for h in range(2)]
        KF = [es2.enter_context(nc.sbuf_tensor(f"KF{h}", [128, S], BF16))
              for h in range(2)]
        VF = [es2.enter_context(nc.sbuf_tensor(f"VF{h}", [128, S], BF16))
              for h in range(2)]
        RHS = [es2.enter_context(nc.sbuf_tensor(f"RHS{i}", [128, DC * 512], BF16))
               for i in range(2)]
        AFM = es2.enter_context(nc.sbuf_tensor("AFM", [128, 2 * S], BF16))
        VT = es2.enter_context(nc.sbuf_tensor("VT", [128, 32 * 129], BF16))
        SQT = es2.enter_context(nc.sbuf_tensor("SQT", [128, S], BF16))
        MROW = es2.enter_context(nc.sbuf_tensor("MROW", [128, 32], F32))
        MNEG = es2.enter_context(nc.sbuf_tensor("MNEG", [128, 32], F32))
        MTMP = es2.enter_context(nc.sbuf_tensor("MTMP", [128, 1], F32))
        DB = es2.enter_context(nc.sbuf_tensor("DB", [128, 512], BF16))
        ATM = [es2.enter_context(nc.sbuf_tensor(f"ATM{i}", [128, 128], BF16))
               for i in range(2)]
        RCT = [es2.enter_context(nc.sbuf_tensor(f"RCT{i}", [128, 1], F32))
               for i in range(2)]
        PRB = [es2.enter_context(nc.sbuf_tensor(f"PRB{i}", [128, 512], BF16))
               for i in range(3)]
        RSB = [es2.enter_context(nc.sbuf_tensor(f"RSB{i}", [128, D], BF16))
               for i in range(2)]
        NRM = es2.enter_context(nc.sbuf_tensor("NRM", [1, 4], F32))

        def qkv_phase(b, psum_act=None, psum_dve=0):
            esQ = contextlib.ExitStack()
            act_pre = CNT["act"] if psum_act is None else psum_act
            dve_pre0 = psum_dve
            PQ = [esQ.enter_context(nc.psum_tensor(f"PQ{b}_{ob}", [128, 512], F32))
                  for ob in range(6)]
            evac_acts = [0] * 6
            rhs_free = [0] * 8
            rhs_dma = [None] * 8
            for tg in range(8):
                def f(sync, tg=tg, b=b, ap_=act_pre):
                    if tg == 0:
                        wtt(sync, cc_agxn)
                        wt(sync, "dve", dve_wcast)
                    if tg < 2 and b == 1:
                        wt(sync, "act", ap_)
                    if tg >= 2:
                        wt(sync, "act", rhs_free[tg - 2])
                    r = 4 * b + tg // 2
                    c0 = (tg % 2) * 512
                    for part in range(4):
                        src = ag_xn_out.ap()[r * D + part * 512:
                                             r * D + (part + 1) * 512, c0:c0 + 512]
                        inc(sync.dma_start(
                            out=RHS[tg % 2][:, part * 4 * 512:(part + 1) * 4 * 512]
                            .rearrange("p (i t) -> p i t", i=4),
                            in_=src.rearrange("(i p) t -> p i t", p=128)),
                            f"sRHS{tg % 2}")
                    rhs_dma[tg] = cum(f"sRHS{tg % 2}")
                S_(f)

                def f(tensor, tg=tg, ea=list(evac_acts), ap_=act_pre,
                      dp_=dve_pre0):
                    wtt(tensor, rhs_dma[tg])
                    if tg == 0:
                        wt(tensor, "act", ap_)
                        if dp_:
                            wt(tensor, "dve", dp_)
                    for ob in range(6):
                        if ea[ob]:
                            wt(tensor, "act", ea[ob])
                        for ic in range(DC):
                            inc(tensor.matmul(
                                PQ[ob][:],
                                WQB[:, ic * 768 + ob * 128: ic * 768 + (ob + 1) * 128],
                                RHS[tg % 2][:, ic * 512:(ic + 1) * 512],
                                start=(ic == 0), stop=(ic == DC - 1)), "pe")
                T_(f)
                pe_tg = CNT["pe"]

                def f(scalar, tg=tg, pe_tg=pe_tg):
                    wt(scalar, "pe", pe_tg)
                    dsts = [(QF[0], QSC), (KF[0], None), (VF[0], None),
                            (QF[1], QSC), (KF[1], None), (VF[1], None)]
                    for ob, (dstt, sc) in enumerate(dsts):
                        dst = dstt[:, tg * 512:(tg + 1) * 512]
                        if sc is not None:
                            inc(scalar.activation(dst, PQ[ob][:], AF.Copy,
                                                  bias=0.0, scale=sc), "act")
                        else:
                            inc(scalar.copy(dst, PQ[ob][:]), "act")
                        evac_acts[ob] = CNT["act"]
                    rhs_free[tg] = CNT["act"]
                A_(f)
            esQ.close()
            return CNT["act"]

        PRO = {}
        tiles_c = []
        for m in (1, 2, 3):
            r0 = {1: 0, 2: 256, 3: 512}[m]
            for rr in range(r0, r0 + 256, 128):
                for cc_ in range(0, HID, 2048):
                    tiles_c.append((m, rr, cc_))

        def emit_c1():
            c1_red = [0] * len(tiles_c)
            wf1_dma = [None] * len(tiles_c)
            for t, (m, rr, cc_) in enumerate(tiles_c):
                def f(sync, t=t, rr=rr, cc_=cc_):
                    if t >= 2:
                        wt(sync, "dve", c1_red[t - 2])
                    inc(sync.dma_start(out=WFT[t % 2][:],
                                       in_=wf_sl[rr:rr + 128, cc_:cc_ + 2048]),
                        f"sWFT{t % 2}")
                    wf1_dma[t] = cum(f"sWFT{t % 2}")
                S_(f)

                def f(vector, t=t, m=m):
                    wtt(vector, wf1_dma[t])
                    inc(vector.reduce_sum(ST3C[:], WFT[t % 2][:], AX.X,
                                          apply_absolute_value=True), "dve")
                    inc(vector.tensor_add(PWS[:, m - 1:m], PWS[:, m - 1:m],
                                          ST3C[:]), "dve")
                    c1_red[t] = CNT["dve"]
                V_(f)
            PRO["dve_pw"] = CNT["dve"]
            PRO["c1_red"] = c1_red

        def emit_ar():
            def f(gpsimd, dve_pw=PRO["dve_pw"]):
                wt(gpsimd, "dve", dve_pw)
                inc(gpsimd.dma_start(out=ar_in.ap(), in_=PWS[:]), "sARI")
                wt(gpsimd, "sARI", CNT["sARI"])
                inc(gpsimd.collective_compute(
                    "AllReduce", ALU.add, replica_groups=[list(range(NCORES))],
                    ins=[ar_in.ap().opt()], outs=[ar_out.ap().opt()]), "ccB")
                wt(gpsimd, "ccB", CNT["ccB"])
                inc(gpsimd.dma_start(out=ARS[:], in_=ar_out.ap()), "sARO")
            G_(f)
            PRO["dG_ars"] = cum("sARO")

        def emit_c2(psum_act):
            esC = contextlib.ExitStack()
            PT1 = esC.enter_context(nc.psum_tensor("PT1", [1, 4], F32))
            PT2 = esC.enter_context(nc.psum_tensor("PT2", [128, 16], F32))

            def f(tensor, pa=psum_act):
                wtt(tensor, PRO["dG_ars"])
                wt(tensor, "act", pa)
                inc(tensor.matmul(PT1[:], ONEC[:], ARS[:], start=True, stop=True),
                    "pe")
            T_(f)
            pe_tot = CNT["pe"]

            def f(vector, pe_tot=pe_tot):
                wt(vector, "pe", pe_tot)
                vector.memset(ROW[:], 0.0)
                vector.memset(TOT[:], 0.0)
                inc(vector.tensor_copy(TOT[0:1, 0:4], PT1[:]), "dve")
                for m in (1, 2, 3):
                    inc(vector.tensor_scalar_mul(TOT[0:1, m - 1:m],
                                                 TOT[0:1, m - 1:m],
                                                 1.0 / NELW[m]), "dve")
                inc(vector.tensor_scalar(ROW[0:1, 0:3], TOT[0:1, 0:3], 1e-8,
                                         ATANH_HALF, ALU.add, ALU.mult), "dve")
                inc(vector.tensor_scalar_mul(ROW[0:1, 3:6], ROW[0:1, 0:3], -1.0),
                    "dve")
                inc(vector.tensor_scalar_add(ROW[0:1, 9:12], TOT[0:1, 0:3], 1.0),
                    "dve")
                inc(vector.tensor_scalar(ROW[0:1, 12:15], TOT[0:1, 0:3], -1.0, 1.0,
                                         ALU.mult, ALU.add), "dve")
                inc(vector.reciprocal(ROW[0:1, 12:15], ROW[0:1, 12:15]), "dve")
                inc(vector.tensor_mul(ROW[0:1, 9:12], ROW[0:1, 9:12],
                                      ROW[0:1, 12:15]), "dve")
            V_(f)
            dve_row = CNT["dve"]

            def f(scalar, dve_row=dve_row):
                wt(scalar, "dve", dve_row)
                inc(scalar.activation(ROW[0:1, 6:9], ROW[0:1, 9:12], AF.Ln,
                                      bias=0.0, scale=1.0), "act")
            A_(f)
            a_ln = CNT["act"]

            def f(vector, a_ln=a_ln):
                wt(vector, "act", a_ln)
                inc(vector.tensor_scalar_mul(ROW[0:1, 6:9], ROW[0:1, 6:9],
                                             0.25 / 127.0), "dve")
            V_(f)
            dve_row2 = CNT["dve"]

            def f(tensor, dve_row2=dve_row2):
                wt(tensor, "dve", dve_row2)
                inc(tensor.matmul(PT2[:], ONER[:], ROW[0:1, :], start=True,
                                  stop=True), "pe")
            T_(f)
            pe_bcs = CNT["pe"]

            def f(vector, pe_bcs=pe_bcs):
                wt(vector, "pe", pe_bcs)
                inc(vector.tensor_copy(BCS[:], PT2[:]), "dve")
                inc(vector.tensor_scalar_mul(RC1[:], AM12[:, 0:8], BCS[:, 6:7]),
                    "dve")
                inc(vector.tensor_scalar_mul(RC2[:], AM12[:, 8:16], BCS[:, 7:8]),
                    "dve")
            V_(f)
            PRO["dve_bcs"] = CNT["dve"]
            esC.close()

        c2b_state = {"wf2": [None] * len(tiles_c), "sgn": [0] * len(tiles_c),
                     "add": [0] * len(tiles_c), "tdma": [None] * len(tiles_c),
                     "next": 0, "ag": False}

        def _emit_c2b_tile(t):
            m, rr, cc_ = tiles_c[t]
            st = c2b_state

            def f(sync, t=t, rr=rr, cc_=cc_):
                if t == 0:
                    wt(sync, "dve", PRO["dve_bcs"])
                if t >= 2:
                    wt(sync, "act", st["sgn"][t - 2])
                inc(sync.dma_start(out=WFT[t % 2][:],
                                   in_=wf_sl[rr:rr + 128, cc_:cc_ + 2048]),
                    f"sWFT{t % 2}")
                st["wf2"][t] = cum(f"sWFT{t % 2}")
            S_(f)

            def f(scalar, t=t, m=m):
                wtt(scalar, st["wf2"][t])
                if t >= 2:
                    wt(scalar, "dve", st["add"][t - 2])
                elif t == 0:
                    wt(scalar, "dve", PRO["dve_bcs"])
                inc(scalar.activation(SG1[:], WFT[t % 2][:], AF.Sign,
                                      bias=BCS[:, 2 + m:3 + m], scale=1.0), "act")
                inc(scalar.activation(SG2[:], WFT[t % 2][:], AF.Sign,
                                      bias=BCS[:, m - 1:m], scale=1.0), "act")
                st["sgn"][t] = CNT["act"]
            A_(f)

            def f(vector, t=t):
                wt(vector, "act", st["sgn"][t])
                if t >= 2:
                    wtt(vector, st["tdma"][t - 2])
                inc(vector.tensor_add(TRN[t % 2][:], SG1[:], SG2[:]), "dve")
                st["add"][t] = CNT["dve"]
            V_(f)

            def f(gpsimd, t=t, rr=rr, cc_=cc_):
                wt(gpsimd, "dve", st["add"][t])
                inc(gpsimd.dma_start(out=ag_t_in.ap()[rr:rr + 128, cc_:cc_ + 2048],
                                     in_=TRN[t % 2][:]), f"sTRN{t % 2}")
                st["tdma"][t] = cum(f"sTRN{t % 2}")
            G_(f)

        def _emit_ag_tern():
            st = c2b_state

            def f(gpsimd):
                wtt(gpsimd, st["tdma"][-1])
                wtt(gpsimd, st["tdma"][-2])
                inc(gpsimd.collective_compute(
                    "AllGather", ALU.bypass, replica_groups=[list(range(NCORES))],
                    ins=[ag_t_in.ap().opt()], outs=[ag_t_out.ap().opt()]), "ccC")
            G_(f)
            PRO["cc_tern"] = cum("ccC")
            st["ag"] = True

        def c2b_filler(n=4):
            st = c2b_state
            if "dve_bcs" not in PRO:
                return
            while n > 0 and st["next"] < len(tiles_c):
                _emit_c2b_tile(st["next"])
                st["next"] += 1
                n -= 1
            if st["next"] == len(tiles_c) and not st["ag"]:
                _emit_ag_tern()

        def c2b_drain():
            c2b_filler(n=len(tiles_c))

        prev_sco_pe = [0]

        def attn_phase(b, qkv_done, filler=None):
            last_afm = 0
            for h in range(2):
                dve0 = CNT["dve"]
                act0 = qkv_done
                esA = contextlib.ExitStack()
                PTBv = esA.enter_context(nc.psum_tensor(f"PTv{b}{h}", [128, 128], BF16))
                PN = esA.enter_context(nc.psum_tensor(f"PN{b}{h}", [1, 512], F32))

                def f(vector, psp=prev_sco_pe[0]):
                    if psp:
                        wt(vector, "pe", psp)
                    inc(vector.memset(
                        VT[:].rearrange("p (k c) -> p k c", c=129)[:, :, 128:129],
                        1.0), "dve")
                V_(f)
                v_evac = [CNT["dve"]]
                for ki in range(32):
                    def f(tensor, ki=ki, h=h, act0=act0, ve=v_evac[-1]):
                        wt(tensor, "act", act0)
                        wt(tensor, "dve", ve)
                        inc(tensor.transpose(PTBv[:],
                                             VF[h][:, ki * 128:(ki + 1) * 128],
                                             IDN[:]), "pe")
                    T_(f)
                    pe_v = CNT["pe"]

                    def f(vector, ki=ki, pe_v=pe_v):
                        wt(vector, "pe", pe_v)
                        inc(vector.tensor_copy(VT[:, ki * 129:ki * 129 + 128],
                                               PTBv[:]), "dve")
                    V_(f)
                    v_evac.append(CNT["dve"])

                PMX = [esA.enter_context(
                    nc.psum_tensor(f"PMX{b}{h}{i}", [128, 512], F32))
                    for i in range(2)]
                mx_red = {0: 0, 1: 0}
                for qt in range(32):
                    nfull = qt // 4
                    plen = (qt % 4) * 128 + 128
                    chunks = [(kc * 512, 512, False) for kc in range(nfull)]
                    chunks.append((nfull * 512, plen, True))
                    for idx, (k0, klen, partial) in enumerate(chunks):
                        par = idx % 2

                        def f(tensor, qt=qt, k0=k0, klen=klen, par=par, h=h,
                              act0=act0, mr=mx_red[par], dve0=dve0):
                            wt(tensor, "act", act0)
                            wt(tensor, "dve", mr if mr else dve0)
                            inc(tensor.matmul(
                                PMX[par][:, 0:klen],
                                QF[h][:, qt * 128:(qt + 1) * 128],
                                KF[h][:, k0:k0 + klen],
                                start=True, stop=True), "pe")
                        T_(f)
                        pe_mx = CNT["pe"]

                        def f(vector, qt=qt, klen=klen, par=par, idx=idx,
                              partial=partial, pe_mx=pe_mx):
                            wt(vector, "pe", pe_mx)
                            if partial:
                                inc(vector.tensor_add(
                                    PMX[par][:, klen - 128:klen],
                                    PMX[par][:, klen - 128:klen], MSK[:]), "dve")
                            if idx == 0:
                                inc(vector.reduce_max(MROW[:, qt:qt + 1],
                                                      PMX[par][:, 0:klen], AX.X),
                                    "dve")
                            else:
                                inc(vector.reduce_max(MTMP[:],
                                                      PMX[par][:, 0:klen], AX.X),
                                    "dve")
                                inc(vector.tensor_max(MROW[:, qt:qt + 1],
                                                      MROW[:, qt:qt + 1], MTMP[:]),
                                    "dve")
                            mx_red[par] = CNT["dve"]
                        V_(f)

                def f(vector):
                    inc(vector.tensor_scalar(MNEG[:], MROW[:], -1.0, -2.0,
                                             ALU.mult, ALU.add), "dve")
                V_(f)
                dve_negm = CNT["dve"]
                esA.close()


                esA2 = contextlib.ExitStack()
                PSS = [esA2.enter_context(
                    nc.psum_tensor(f"PS{b}{h}{i}", [128, 512], F32))
                    for i in range(3)]
                PSA = [esA2.enter_context(
                    nc.psum_tensor(f"PA{b}{h}{i}", [128, 129], F32))
                    for i in range(4)]
                PTR = esA2.enter_context(nc.psum_tensor(f"PR{b}{h}", [128, 128], BF16))

                exp_last = {0: 0, 1: 0, 2: 0}
                pv_last = {0: 0, 1: 0, 2: 0}
                resc_dve = {}
                tr_done = {0: prev_sco_pe[0], 1: prev_sco_pe[0]}
                afm_acts = [0]
                db_pe = [prev_sco_pe[0]]
                prev_resc0 = CNT["dve"]
                for qg in range(8):
                    def f(vector, qg=qg, dn=dve_negm, dp=db_pe[0]):
                        wt(vector, "dve", dn)
                        if dp:
                            wt(vector, "pe", dp)
                        for j in range(4):
                            qt = 4 * qg + j
                            inc(vector.tensor_scalar_mul(
                                DB[:, j * 128:(j + 1) * 128], IDN[:],
                                MNEG[:, qt:qt + 1]), "dve")
                    V_(f)
                    dve_db = CNT["dve"]

                    nki = 4 * qg + 4
                    for ki in range(nki):
                        par = ki % 3
                        diag = ki - 4 * qg

                        def f(tensor, qg=qg, ki=ki, par=par, h=h,
                              dve_db=dve_db, vl=v_evac[-1], el=exp_last[par]):
                            if el:
                                wt(tensor, "act", el)
                            else:
                                wt(tensor, "dve", vl)
                            wt(tensor, "dve", dve_db)
                            inc(tensor.matmul(PSS[par][:],
                                              KF[h][:, ki * 128:(ki + 1) * 128],
                                              QF[h][:, qg * 512:(qg + 1) * 512],
                                              start=True, stop=False), "pe")
                            inc(tensor.matmul(PSS[par][:], ONESQ[:], DB[:],
                                              start=False, stop=True), "pe")
                        T_(f)
                        pe_sco = CNT["pe"]

                        dve_pre = 0
                        if diag >= 0:
                            def f(vector, par=par, diag=diag, pe_sco=pe_sco,
                                  el=exp_last[par], pl=pv_last[par]):
                                wt(vector, "pe", pe_sco)
                                if el:
                                    wt(vector, "act", el)
                                if pl:
                                    wt(vector, "pe", pl)
                                inc(vector.tensor_add(
                                    PSS[par][:, diag * 128:(diag + 1) * 128],
                                    PSS[par][:, diag * 128:(diag + 1) * 128],
                                    MSK[:]), "dve")
                            V_(f)
                            dve_pre = CNT["dve"]

                        def f(scalar, par=par, diag=diag, pe_sco=pe_sco,
                              dve_pre=dve_pre, pl=pv_last[par]):
                            wt(scalar, "pe", pe_sco)
                            if dve_pre:
                                wt(scalar, "dve", dve_pre)
                            if pl:
                                wt(scalar, "pe", pl)
                            vf = max(0, diag) * 128
                            inc(scalar.activation(PRB[par][:, vf:512],
                                                  PSS[par][:, vf:512],
                                                  AF.Exp, bias=0.0, scale=1.0),
                                "act")
                        A_(f)
                        exp_last[par] = CNT["act"]

                        def f(tensor, qg=qg, ki=ki, par=par, diag=diag,
                              ea=exp_last[par], pr0=prev_resc0):
                            wt(tensor, "act", ea)
                            for qi in range(max(0, diag), 4):
                                if ki == 0:
                                    wt(tensor, "dve", resc_dve.get(qi, pr0))
                                inc(tensor.matmul(
                                    PSA[qi][:],
                                    PRB[par][:, qi * 128:(qi + 1) * 128],
                                    VT[:, ki * 129:(ki + 1) * 129],
                                    start=(ki == 0),
                                    stop=(ki == 4 * qg + qi)), "pe")
                        T_(f)
                        pv_last[par] = CNT["pe"]
                        pv_done_ki = CNT["pe"]
                        db_pe[0] = pe_sco

                        if diag >= 0:
                            qi = diag
                            qt = 4 * qg + qi
                            apar = qt % 2

                            def f(vector, qi=qi, apar=apar, pvd=pv_done_ki,
                                  td=tr_done.get(apar, 0)):
                                wt(vector, "pe", pvd)
                                if td:
                                    wt(vector, "pe", td)
                                inc(vector.reciprocal(RCT[apar][:],
                                                      PSA[qi][:, 128:129]), "dve")
                                inc(vector.tensor_scalar_mul(ATM[apar][:],
                                                             PSA[qi][:, 0:128],
                                                             RCT[apar][:]), "dve")
                            V_(f)
                            resc_dve[qi] = CNT["dve"]

                            def f(tensor, qi=qi, apar=apar, rd=resc_dve[qi],
                                  aa=afm_acts[-1]):
                                wt(tensor, "dve", rd)
                                wt(tensor, "act", aa)
                                inc(tensor.transpose(PTR[:], ATM[apar][:], IDN[:]),
                                    "pe")
                            T_(f)
                            tr_done[apar] = CNT["pe"]

                            def f(scalar, qt=qt, h=h, td=tr_done[apar]):
                                wt(scalar, "pe", td)
                                inc(scalar.copy(
                                    AFM[:, h * S + qt * 128: h * S + (qt + 1) * 128],
                                    PTR[:]), "act")
                            A_(f)
                            afm_acts.append(CNT["act"])
                    if filler is not None:
                        filler()
                esA2.close()
                prev_sco_pe[0] = CNT["pe"]
                last_afm = afm_acts[-1]
            return last_afm

        def wo_phase(b, afm_done, prev_rs=None):
            esW = contextlib.ExitStack()
            PWO = [esW.enter_context(nc.psum_tensor(f"PW{b}{i}", [128, D], F32))
                   for i in range(2)]
            evac_for = [0] * 32
            rs_dma = [None] * 32
            for tt in range(32):
                buf = tt % 2

                def f(tensor, tt=tt, buf=buf, afm_done=afm_done,
                      ef=(evac_for[tt - 2] if tt >= 2 else 0)):
                    wt(tensor, "act", afm_done)
                    if ef:
                        wt(tensor, "act", ef)
                    for hc in range(2):
                        for oc in range(4):
                            inc(tensor.matmul(
                                PWO[buf][:, oc * 512:(oc + 1) * 512],
                                AFM[:, hc * S + tt * 128: hc * S + (tt + 1) * 128],
                                WOB[:, hc * D + oc * 512: hc * D + (oc + 1) * 512],
                                start=(hc == 0), stop=(hc == 1)), "pe")
                T_(f)
                pe_wo = CNT["pe"]

                def f(scalar, tt=tt, buf=buf, pe_wo=pe_wo, prev_rs=prev_rs,
                      rg=(rs_dma[tt - 2] if tt >= 2 else None)):
                    wt(scalar, "pe", pe_wo)
                    if rg:
                        wtt(scalar, rg)
                    elif prev_rs:
                        wtt(scalar, prev_rs[buf])
                    inc(scalar.copy(RSB[buf][:], PWO[buf][:]), "act")
                    evac_for[tt] = CNT["act"]
                A_(f)

                def f(gpsimd, tt=tt, buf=buf, b=b):
                    wt(gpsimd, "act", evac_for[tt])
                    inc(gpsimd.dma_start(
                        out=rs_in.ap()[b * S + tt * 128: b * S + (tt + 1) * 128, :],
                        in_=RSB[buf][:]), f"sRSB{buf}")
                    rs_dma[tt] = cum(f"sRSB{buf}")
                G_(f)
            esW.close()
            return (rs_dma[30], rs_dma[31])

        act_qkv0 = qkv_phase(0, psum_act=act_xqf, psum_dve=dve_g_end)
        emit_c1()
        emit_ar()
        emit_c2(act_qkv0)
        if stop_after == "qkv0":
            _finish_early()
            es2.close()
            return nc
        if debug:
            def f(sync):
                wt(sync, "act", act_qkv0)
                inc(sync.dma_start(out=dbg["q0"], in_=QF[0][:]), "oodbg")
                inc(sync.dma_start(out=dbg["k0"], in_=KF[0][:]), "oodbg")
                inc(sync.dma_start(out=dbg["v0"], in_=VF[0][:]), "oodbg")
            S_(f)
        afm0 = attn_phase(0, act_qkv0, filler=c2b_filler)
        if stop_after == "attn0":
            _finish_early()
            es2.close()
            return nc
        if debug:
            def f(sync, afm0=afm0):
                wt(sync, "act", afm0)
                inc(sync.dma_start(out=dbg["afm0"], in_=AFM[:]), "oodbg")
            S_(f)
        wo0 = wo_phase(0, afm0)
        if stop_after == "wo0":
            _finish_early()
            es2.close()
            return nc
        act_qkv1 = qkv_phase(1)
        if stop_after == "qkv1":
            _finish_early()
            es2.close()
            return nc
        afm1 = attn_phase(1, act_qkv1, filler=c2b_filler)
        if stop_after == "attn1":
            _finish_early()
            es2.close()
            return nc
        c2b_drain()
        wo1 = wo_phase(1, afm1, prev_rs=wo0)
        stage2_act = CNT["act"]
        cc_tern = PRO["cc_tern"]
        if stop_after == "wo1":
            _finish_early()
            es2.close()
            return nc

        def f(gpsimd, wo1=wo1):
            wtt(gpsimd, wo1[0])
            wtt(gpsimd, wo1[1])
            if dbg2:
                inc(gpsimd.dma_start(out=dbg["d2_ri0"], in_=rs_in.ap()), "oodbg")
                wt(gpsimd, "oodbg", CNT["oodbg"])
            inc(gpsimd.collective_compute(
                "ReduceScatter", ALU.add, replica_groups=[list(range(NCORES))],
                ins=[rs_in.ap().opt()], outs=[rs_out.ap().opt()]), "ccD")
        G_(f)
        cc_rs = cum("ccD")
        if stop_after == "rs":
            _finish_early()
            es2.close()
            return nc

        if debug:
            def f(sync):
                wtt(sync, cc_rs)
                inc(sync.dma_start(out=dbg["s"], in_=rs_out.ap()), "oodbg")
            S_(f)

        es2.close()

        esH = contextlib.ExitStack()
        G3B = esH.enter_context(nc.sbuf_tensor("G3B", [128, HID], F32))

        def f(sync):
            wt(sync, "act", stage2_act)
            wtt(sync, wo1[0])
            wtt(sync, wo1[1])
            inc(sync.dma_start(out=G3B[:], in_=g3b_in), "sG3")
        S_(f)
        d_g3 = cum("sG3")

        def w12_src_ap(m, parity, hc, uh):
            m_off = 0 if m == 0 else 256
            off = (m_off + parity * 128 + uh * 4 * 768) * HID + hc * 512
            return mkap(ag_t_out, off, [(HID, 128), (768 * HID, 4), (1, 512)])

        out_dma = []
        PRV = {"mm_last": 0, "w3pe": 0, "h_act": 0, "ob_last": 0,
               "xq_used": 0}
        for qd in range(2):
            esQ = contextlib.ExitStack()
            H16 = esQ.enter_context(nc.sbuf_tensor(f"H16_{qd}",
                                                   [128, 4 * HID], BF16))

            def HT(th):
                return H16[:, th * HID:(th + 1) * HID]

            def XQ3F(th, j):
                return H16[:, th * HID + j * 128: th * HID + (j + 1) * 128]

            esW12 = contextlib.ExitStack()
            XQ12Q = [esW12.enter_context(
                nc.sbuf_tensor(f"XQQ{qd}{m}", [128, 4 * D], BF16))
                for m in range(2)]
            P12 = [[esW12.enter_context(
                nc.psum_tensor(f"P12_{qd}_{m}_{th}", [128, 512], F32))
                for th in range(4)] for m in range(2)]
            W12B = [[esW12.enter_context(
                nc.sbuf_tensor(f"W12B{qd}{m}{i}", [128, 16 * 512], BF16))
                for i in range(2)] for m in range(2)]
            SW = esW12.enter_context(nc.sbuf_tensor(f"SW{qd}", [128, 512], BF16))
            XV = esW12.enter_context(nc.sbuf_tensor(f"XV{qd}", [128, 512], BF16))

            def f(sync, qd=qd, xu=PRV["xq_used"]):
                if qd == 0:
                    wtt(sync, dS_xqf)
                    wt(sync, "act", stage2_act)
                    wtt(sync, wo1[0])
                    wtt(sync, wo1[1])
                if xu:
                    wt(sync, "pe", xu)
                for m in range(2):
                    for hf in range(2):
                        inc(sync.dma_start(
                            out=XQ12Q[m][:, hf * 2 * D:(hf + 1) * 2 * D],
                            in_=xqf_d[m].ap()[:, (qd * 4 + hf * 2) * D:
                                              (qd * 4 + hf * 2 + 2) * D]),
                            "sXQQ")
            S_(f)
            d_xqq = cum("sXQQ")
            if stop_after == "h_xq" and qd == 0:
                _finish_early()
                esW12.close(); esQ.close(); esH.close()
                return nc

            w12_dma = [None] * 16
            h_done = [[0] * 16 for _ in range(4)]
            mm_done = [[0] * 16 for _ in range(4)]
            for hc in range(16):
                def f(sync, qd=qd, hc=hc, pmm=PRV["mm_last"],
                      ob=PRV["ob_last"]):
                    if qd == 0 and hc == 0:
                        wtt(sync, cc_tern)
                    if qd == 0 and hc < 2:
                        wt(sync, "act", stage2_act)
                        wtt(sync, wo1[0])
                        wtt(sync, wo1[1])
                    if hc >= 2:
                        wt(sync, "pe", mm_done[3][hc - 2])
                    elif pmm:
                        wt(sync, "pe", pmm)
                        if ob:
                            wt(sync, "dve", ob)
                    for m in range(2):
                        for parity in range(2):
                            for uh in range(2):
                                outap = W12B[m][hc % 2][:].rearrange(
                                    "p (u q c) -> p u q c", q=2, c=512)[
                                    :, uh * 4:(uh + 1) * 4, parity, :]
                                inc(sync.dma_start(
                                    out=outap,
                                    in_=w12_src_ap(m, parity, hc, uh)),
                                    f"sW12_{hc % 2}")
                    w12_dma[hc] = cum(f"sW12_{hc % 2}")
                S_(f)

                for th in range(4):
                    for m in range(2):
                        def f(tensor, qd=qd, hc=hc, th=th, m=m,
                              hg=(h_done[th][hc - 2] if hc >= 2 else 0), dq=d_xqq):
                            wtt(tensor, w12_dma[hc])
                            if hc == 0 and th == 0 and m == 0:
                                wtt(tensor, dq)
                            if hg:
                                wt(tensor, "dve", hg)
                            for ic in range(DC):
                                inc(tensor.matmul(
                                    P12[m][th][:],
                                    XQ12Q[m][:, (th * DC + ic) * 128:
                                             (th * DC + ic + 1) * 128],
                                    W12B[m][hc % 2][:, ic * 512:(ic + 1) * 512],
                                    start=(ic == 0), stop=(ic == DC - 1)), "pe")
                        T_(f)
                        mm_done[th][hc] = CNT["pe"]

                    def f(scalar, qd=qd, hc=hc, th=th, mmd=mm_done[th][hc],
                          hprev=PRV["h_act"]):
                        wt(scalar, "pe", mmd)
                        if hprev:
                            wt(scalar, "dve", hprev)
                        inc(scalar.activation(
                            SW[:], P12[0][th][:], AF.Silu, bias=0.0,
                            scale=RC1[:, qd * 4 + th:qd * 4 + th + 1]), "act")
                    A_(f)
                    a_sw = CNT["act"]

                    def f(vector, qd=qd, hc=hc, th=th, a_sw=a_sw,
                          mmd=mm_done[th][hc]):
                        wt(vector, "pe", mmd)
                        inc(vector.tensor_scalar_mul(
                            XV[:], P12[1][th][:],
                            RC2[:, qd * 4 + th:qd * 4 + th + 1]), "dve")
                        wt(vector, "act", a_sw)
                        inc(vector.tensor_mul(HT(th)[:, hc * 512:(hc + 1) * 512],
                                              SW[:], XV[:]), "dve")
                        h_done[th][hc] = CNT["dve"]
                        PRV["h_act"] = CNT["dve"]
                    V_(f)
            PRV["mm_last"] = mm_done[3][15]
            esW12.close()
            if stop_after == "h_w12" and qd == 0:
                _finish_early()
                esQ.close(); esH.close()
                return nc

            if debug and qd == 0:
                def f(sync, hd=h_done[0][15]):
                    wt(sync, "dve", hd)
                    inc(sync.dma_start(out=dbg["h0"], in_=HT(0)[:]), "oodbg")
                S_(f)

            esQu = contextlib.ExitStack()
            YGH = esQu.enter_context(nc.sbuf_tensor(f"YGH{qd}", [128, HID], F32))
            YH2 = esQu.enter_context(nc.sbuf_tensor(f"YH2{qd}", [128, HID], F32))
            XQ8H = esQu.enter_context(nc.sbuf_tensor(f"XQ8H{qd}", [128, HID], BF16))
            AMX3 = esQu.enter_context(nc.sbuf_tensor(f"AMX3{qd}", [128, 1], F32))
            SSQ3 = esQu.enter_context(nc.sbuf_tensor(f"SSQ3{qd}", [128, 1], F32))
            RST3 = esQu.enter_context(nc.sbuf_tensor(f"RST3{qd}", [128, 1], F32))
            SCL3 = esQu.enter_context(nc.sbuf_tensor(f"SCL3{qd}", [128, 1], F32))
            esTP = contextlib.ExitStack()
            PTH = esTP.enter_context(nc.psum_tensor(f"PTH{qd}", [128, 128], BF16))

            q3_evac = [0]
            q3_done = [0] * 4
            prev_q8 = [0]
            for th in range(4):
                def f(scalar, th=th, hd=h_done[th][15], dg3=d_g3, qd=qd,
                      pq8=prev_q8[0]):
                    wt(scalar, "dve", hd)
                    if pq8:
                        wt(scalar, "dve", pq8)
                    if th == 0 and qd == 0:
                        wtt(scalar, dg3)
                    inc(scalar.activation(YGH[:], HT(th)[:], AF.Square,
                                          bias=0.0, scale=1.0, accum_out=SSQ3[:]),
                        "act")
                    inc(scalar.activation(RST3[:], SSQ3[:], AF.Sqrt, bias=EPS8[:],
                                          scale=1.0 / HID), "act")
                A_(f)
                a_r3 = CNT["act"]

                def f(vector, th=th, a_r3=a_r3, qd=qd):
                    wt(vector, "act", a_r3)
                    inc(vector.reciprocal(RST3[:], RST3[:]), "dve")
                    inc(vector.scalar_tensor_tensor(YGH[:], HT(th)[:], RST3[:],
                                                    G3B[:], ALU.mult, ALU.mult),
                        "dve")
                    inc(vector.reduce_max(AMX3[:], YGH[:], AX.X,
                                          apply_absolute_value=True), "dve")
                    inc(vector.tensor_scalar_max(AMX3[:], AMX3[:], 1e-4), "dve")
                    inc(vector.tensor_mul(RC3[:, qd * 4 + th:qd * 4 + th + 1],
                                          AMX3[:], BCS[:, 8:9]), "dve")
                V_(f)
                dve_a3 = CNT["dve"]

                def f(vector, dve_a3=dve_a3):
                    wt(vector, "dve", dve_a3)
                    inc(vector.reciprocal(SCL3[:], AMX3[:]), "dve")
                    inc(vector.tensor_scalar(YH2[:], YGH[:], SCL3[:], 127.0,
                                             ALU.mult, ALU.mult), "dve")
                    inc(vector.tensor_scalar(YGH[:], YH2[:], MAGIC, MAGIC,
                                             ALU.add, ALU.subtract), "dve")
                    inc(vector.tensor_scalar(XQ8H[:], YGH[:], 127.0, -128.0,
                                             ALU.min, ALU.max), "dve")
                V_(f)
                dve_q8 = CNT["dve"]
                prev_q8[0] = dve_q8

                for j in range(64):
                    def f(tensor, j=j, dve_q8=dve_q8, qe=q3_evac[-1]):
                        wt(tensor, "dve", dve_q8)
                        wt(tensor, "dve", qe)
                        inc(tensor.transpose(PTH[:],
                                             XQ8H[:, j * 128:(j + 1) * 128],
                                             IDN[:]), "pe")
                    T_(f)
                    pe_th = CNT["pe"]

                    def f(vector, th=th, j=j, pt=pe_th):
                        wt(vector, "pe", pt)
                        inc(vector.tensor_copy(XQ3F(th, j), PTH[:]), "dve")
                    V_(f)
                    q3_evac.append(CNT["dve"])
                q3_done[th] = CNT["dve"]
            if dbg2 and qd == 0:
                def f(sync, q3=q3_done[3]):
                    wt(sync, "dve", q3)
                    for pp in range(4):
                        inc(sync.dma_start(
                            out=dbg["d2_h16"][:, pp * HID:(pp + 1) * HID],
                            in_=H16[:, pp * HID:(pp + 1) * HID]), "oodbg")
                    inc(sync.dma_start(out=dbg["d2_rc"][:, 0:8], in_=RC1[:]),
                        "oodbg")
                    inc(sync.dma_start(out=dbg["d2_rc"][:, 8:16], in_=RC2[:]),
                        "oodbg")
                    inc(sync.dma_start(out=dbg["d2_rc"][:, 16:24], in_=RC3[:]),
                        "oodbg")
                S_(f)
            esTP.close()
            esQu.close()
            if stop_after == "h_q3" and qd == 0:
                _finish_early()
                esQ.close(); esH.close()
                return nc

            esW3 = contextlib.ExitStack()
            PO = [esW3.enter_context(nc.psum_tensor(f"PO{qd}{th}", [128, 1024], F32))
                  for th in range(4)]
            W3B = [esW3.enter_context(
                nc.sbuf_tensor(f"W3B{qd}{i}", [128, 4 * 1024], BF16))
                for i in range(2)]
            FO4 = [esW3.enter_context(nc.sbuf_tensor(f"FO4_{qd}{th}", [128, D], F32))
                   for th in range(4)]
            GT = esW3.enter_context(nc.sbuf_tensor(f"GT{qd}", [128, D], F32))
            SBF = esW3.enter_context(nc.sbuf_tensor(f"SBF{qd}", [128, D], BF16))
            SF32 = esW3.enter_context(nc.sbuf_tensor(f"SF32{qd}", [128, D], F32))
            XT2 = esW3.enter_context(nc.sbuf_tensor(f"XT2{qd}", [128, D], F32))
            OBUF = [esW3.enter_context(nc.sbuf_tensor(f"OB{qd}{i}", [128, D], F32))
                    for i in range(2)]

            w3_dma = [None] * 32
            w3_used = {}
            po_free = [0]
            for och in range(2):
                for rh in range(16):
                    t3 = och * 16 + rh
                    r, hh = rh // 2, rh % 2

                    def f(gpsimd, qd=qd, t3=t3, r=r, hh=hh, och=och,
                          q3=q3_done[3], g=PRV["w3pe"],
                          wg=(w3_used.get(t3 - 2, 0) if t3 >= 2 else 0)):
                        if t3 == 0:
                            wt(gpsimd, "dve", q3)
                        if wg:
                            wt(gpsimd, "pe", wg)
                        elif g:
                            wt(gpsimd, "pe", g)
                        for jh in range(2):
                            off = ((r * 768 + 512) * HID
                                   + (hh * 4 + jh * 2) * 32 * HID + och * 1024)
                            src = mkap(ag_t_out, off,
                                       [(HID, 32), (D, 4), (32 * HID, 2), (1, 1024)])
                            inc(gpsimd.dma_start(
                                out=W3B[t3 % 2][:, jh * 2 * 1024:(jh + 1) * 2 * 1024]
                                .rearrange("p (j c) -> p j c", j=2),
                                in_=src), f"sW3_{t3 % 2}")
                        w3_dma[t3] = cum(f"sW3_{t3 % 2}")
                    G_(f)

                    def f(tensor, qd=qd, t3=t3, rh=rh, r=r, hh=hh, och=och,
                          pf=po_free[0]):
                        wtt(tensor, w3_dma[t3])
                        if rh == 0 and pf:
                            wt(tensor, "dve", pf)
                        for js in range(4):
                            j = r * 8 + hh * 4 + js
                            for th in range(4):
                                for oc in range(2):
                                    inc(tensor.matmul(
                                        PO[th][:, oc * 512:(oc + 1) * 512],
                                        XQ3F(th, j),
                                        W3B[t3 % 2][:, js * 1024 + oc * 512:
                                                    js * 1024 + (oc + 1) * 512],
                                        start=(j == 0), stop=(j == 63)), "pe")
                    T_(f)
                    w3_used[t3] = CNT["pe"]
                pe_w3o = CNT["pe"]

                for th in range(4):
                    def f(vector, th=th, och=och, qd=qd, pw=pe_w3o):
                        wt(vector, "pe", pw)
                        inc(vector.tensor_scalar_mul(
                            FO4[th][:, och * 1024:(och + 1) * 1024], PO[th][:],
                            RC3[:, qd * 4 + th:qd * 4 + th + 1]), "dve")
                    V_(f)
                po_free[0] = CNT["dve"]
            pe_w3 = CNT["pe"]
            PRV["w3pe"] = pe_w3
            dve_fo4 = po_free[0]
            if stop_after == "h_w3" and qd == 0:
                _finish_early()
                esW3.close(); esQ.close(); esH.close()
                return nc

            for th in range(4):
                ti = qd * 4 + th

                def f(sync, ti=ti, g=PRV["ob_last"], q3=q3_done[3]):
                    wtt(sync, cc_rs)
                    wt(sync, "dve", q3)
                    if g:
                        wt(sync, "dve", g)
                    inc(sync.dma_start(
                        out=SBF[:], in_=rs_out.ap()[ti * 128:(ti + 1) * 128, :]),
                        "sFIN")
                    inc(sync.dma_start(out=XT2[:],
                                       in_=x_sl[ti * 128:(ti + 1) * 128, :]), "sFIN")
                S_(f)
                d_fin = cum("sFIN")

                def f(scalar, th=th, dve_fo4=dve_fo4, g=PRV["ob_last"]):
                    wt(scalar, "dve", dve_fo4)
                    if g:
                        wt(scalar, "dve", g)
                    inc(scalar.activation(GT[:], FO4[th][:], AF.Silu, bias=0.0,
                                          scale=1.0),
                        "act")
                A_(f)
                a_g = CNT["act"]

                def f(vector, ti=ti, th=th, a_g=a_g, d_fin=d_fin,
                      g=(out_dma[ti - 2] if ti >= 2 else None)):
                    wt(vector, "act", a_g)
                    wtt(vector, d_fin)
                    if g:
                        wtt(vector, g)
                    inc(vector.tensor_copy(SF32[:], SBF[:]), "dve")
                    inc(vector.tensor_mul(FO4[th][:], GT[:], SF32[:]), "dve")
                    inc(vector.tensor_add(OBUF[ti % 2][:], FO4[th][:], XT2[:]),
                        "dve")
                V_(f)
                ob_last = CNT["dve"]
                PRV["ob_last"] = ob_last

                def f(gpsimd, ti=ti, ou=ob_last):
                    wt(gpsimd, "dve", ou)
                    inc(gpsimd.dma_start(out=y_sl[ti * 128:(ti + 1) * 128, :],
                                         in_=OBUF[ti % 2][:]), f"sOO{ti % 2}")
                    out_dma.append(cum(f"sOO{ti % 2}"))
                G_(f)
            PRV["xq_used"] = pe_w3
            esW3.close()
            esQ.close()
            if stop_after == "h_fin" and qd == 0:
                _finish_early()
                esH.close()
                return nc
        esH.close()

        if dbg2:
            def f(sync):
                for nm in list(SEMS):
                    wt(sync, nm, CNT[nm])
                inc(sync.dma_start(out=dbg["d2_xn"], in_=ag_xn_out.ap()), "oodbg")
                inc(sync.dma_start(out=dbg["d2_s"], in_=rs_out.ap()), "oodbg")
                inc(sync.dma_start(out=dbg["d2_rsin"], in_=rs_in.ap()), "oodbg")
                inc(sync.dma_start(out=dbg["d2_xq"], in_=xqf_d[0].ap()), "oodbg")
                inc(sync.dma_start(out=dbg["d2_tern"],
                                   in_=ag_t_out.ap()[0:768, :]), "oodbg")
            S_(f)

        def f(gpsimd):
            for nm in list(SEMS):
                if nm.startswith("sOO") or nm == "oodbg":
                    wt(gpsimd, nm, CNT[nm])
        G_(f)

    return nc



_CACHED = {}


def _get_nc(debug=False):
    key = bool(debug)
    if key not in _CACHED:
        _CACHED[key] = build(debug=key)
    return _CACHED[key]


def make_in_maps(L0_attn_hidden_states, L0_attn_w_qkv, L0_attn_w_o,
                 L0_ffn_w1, L0_ffn_g1, L0_ffn_w2, L0_ffn_g2, L0_ffn_w3, L0_ffn_g3):
    bf = ml_dtypes.bfloat16
    x = np.ascontiguousarray(np.asarray(L0_attn_hidden_states, np.float32)
                             .reshape(B * S, D))
    wqkvT = np.ascontiguousarray(np.asarray(L0_attn_w_qkv, np.float32).T)
    woT = np.ascontiguousarray(np.asarray(L0_attn_w_o, np.float32).T)
    w1T = np.ascontiguousarray(np.asarray(L0_ffn_w1, np.float32).T)
    w2T = np.ascontiguousarray(np.asarray(L0_ffn_w2, np.float32).T)
    w3T = np.ascontiguousarray(np.asarray(L0_ffn_w3, np.float32).T)

    g1b = np.ascontiguousarray(np.tile(np.asarray(L0_ffn_g1, np.float32)[None, :],
                                       (128, 1)))
    g2b = np.ascontiguousarray(np.tile(np.asarray(L0_ffn_g2, np.float32)[None, :],
                                       (128, 1)))
    g3b = np.ascontiguousarray(np.tile(np.asarray(L0_ffn_g3, np.float32)[None, :],
                                       (128, 1)))
    ident = np.eye(128, dtype=np.float32).astype(bf)
    kk, qq = np.meshgrid(np.arange(128), np.arange(128), indexing="ij")
    maskt = np.where(kk <= qq, 0.0, -1e9).astype(np.float32)
    onec = np.ones((128, 1), np.float32)
    oner = np.ones((1, 128), np.float32)
    onecb = np.ones((128, 1), np.float32).astype(bf)
    onerb = np.ones((1, 128), np.float32).astype(bf)
    onesq = np.ones((128, 128), np.float32).astype(bf)

    in_maps = []
    for c in range(NCORES):
        h0 = 2 * c
        cols = []
        for h in (h0, h0 + 1):
            for off in (0, D, 2 * D):
                cols.append(wqkvT[:, off + h * 128: off + (h + 1) * 128])
        wqkv_sl_a = np.ascontiguousarray(np.concatenate(cols, axis=1))
        wo_sl_a = np.ascontiguousarray(woT[h0 * 128: h0 * 128 + 256, :])
        wf_sl_a = np.ascontiguousarray(np.concatenate([
            w1T[c * 256:(c + 1) * 256, :],
            w2T[c * 256:(c + 1) * 256, :],
            w3T[c * NTOK:(c + 1) * NTOK, :].reshape(256, HID)], axis=0))
        in_maps.append({
            "x_sl": np.ascontiguousarray(x[c * NTOK:(c + 1) * NTOK, :]),
            "wqkv_sl": wqkv_sl_a,
            "wo_sl": wo_sl_a,
            "wf_sl": wf_sl_a,
            "g1b": g1b, "g2b": g2b, "g3b": g3b,
            "ident": ident, "maskt": maskt,
            "onec": onec, "oner": oner, "onecb": onecb, "onerb": onerb,
            "onesq": onesq,
        })
    return in_maps


def kernel(L0_attn_hidden_states, L0_attn_w_qkv, L0_attn_w_o,
           L0_ffn_w1, L0_ffn_g1, L0_ffn_w2, L0_ffn_g2, L0_ffn_w3, L0_ffn_g3):
    in_maps = make_in_maps(L0_attn_hidden_states, L0_attn_w_qkv, L0_attn_w_o,
                           L0_ffn_w1, L0_ffn_g1, L0_ffn_w2, L0_ffn_g2,
                           L0_ffn_w3, L0_ffn_g3)
    nc = _get_nc(debug=False)
    res = run_bass_kernel_spmd(nc, in_maps, core_ids=list(range(NCORES)))
    out = np.concatenate([res.results[c]["y_sl"] for c in range(NCORES)], axis=0)
    return out.reshape(B, S, D)


# version 13
# speedup vs baseline: 1.0268x; 1.0268x over previous
import contextlib
import numpy as np
import ml_dtypes

import concourse.bass as bass
import concourse.mybir as mybir
from concourse.bass_types import AP
from concourse.bass_utils import run_bass_kernel_spmd

F32 = mybir.dt.float32
BF16 = mybir.dt.bfloat16
I8 = mybir.dt.int8
AF = mybir.ActivationFunctionType
ALU = mybir.AluOpType
AX = mybir.AxisListType

NCORES = 8
D = 2048
DC = 16
S = 4096
B = 2
NTOK = 1024
HID = 8192
QSC = 1.0 / float(np.sqrt(np.float32(128.0)))
ATANH_HALF = 0.5493061443340549
MAGIC = 12582912.0
NELW = {1: 8192 * 2048, 2: 8192 * 2048, 3: 2048 * 8192}


def mkap(t, offset, dims):
    base = t.ap()
    return AP(base.tensor, offset, [[st, cn] for (st, cn) in dims])


def build(debug=False, stop_after="full", dbg2=False):
    nc = bass.Bass("TRN2", target_bir_lowering=False,
                   detect_race_conditions=False)

    x_sl = nc.dram_tensor("x_sl", [NTOK, D], F32, kind="ExternalInput").ap()
    wqkv_sl = nc.dram_tensor("wqkv_sl", [D, 768], F32, kind="ExternalInput").ap()
    wo_sl = nc.dram_tensor("wo_sl", [256, D], F32, kind="ExternalInput").ap()
    wf_sl = nc.dram_tensor("wf_sl", [768, HID], F32, kind="ExternalInput").ap()
    g1b_in = nc.dram_tensor("g1b", [128, D], F32, kind="ExternalInput").ap()
    g2b_in = nc.dram_tensor("g2b", [128, D], F32, kind="ExternalInput").ap()
    g3b_in = nc.dram_tensor("g3b", [128, HID], F32, kind="ExternalInput").ap()
    ident_in = nc.dram_tensor("ident", [128, 128], BF16, kind="ExternalInput").ap()
    mask_in = nc.dram_tensor("maskt", [128, 128], F32, kind="ExternalInput").ap()
    onec_in = nc.dram_tensor("onec", [128, 1], F32, kind="ExternalInput").ap()
    oner_in = nc.dram_tensor("oner", [1, 128], F32, kind="ExternalInput").ap()
    onecb_in = nc.dram_tensor("onecb", [128, 1], BF16, kind="ExternalInput").ap()
    onerb_in = nc.dram_tensor("onerb", [1, 128], BF16, kind="ExternalInput").ap()
    onesq_in = nc.dram_tensor("onesq", [128, 128], BF16, kind="ExternalInput").ap()

    y_sl = nc.dram_tensor("y_sl", [NTOK, D], F32, kind="ExternalOutput").ap()

    dbg = {}
    if debug:
        for nm, shp, dt in [("xn", [NTOK, D], F32), ("q0", [128, S], BF16),
                            ("k0", [128, S], BF16), ("v0", [128, S], BF16),
                            ("afm0", [128, 2 * S], BF16), ("s", [NTOK, D], BF16),
                            ("bcs", [128, 16], F32), ("xq1", [128, 8 * D], BF16),
                            ("h0", [128, HID], BF16), ("negm", [128, 1], F32),
                            ("rc", [128, 24], F32)]:
            dbg[nm] = nc.dram_tensor("dbg_" + nm, shp, dt, kind="ExternalOutput").ap()

    if dbg2:
        dbg["d2_xn"] = nc.dram_tensor("d2_xn", [NCORES * D, NTOK], BF16,
                                      kind="ExternalOutput").ap()
        dbg["d2_s"] = nc.dram_tensor("d2_s", [NTOK, D], BF16,
                                     kind="ExternalOutput").ap()
        dbg["d2_rsin"] = nc.dram_tensor("d2_rsin", [B * S, D], BF16,
                                        kind="ExternalOutput").ap()
        dbg["d2_xq"] = nc.dram_tensor("d2_xq", [128, 8 * D], BF16,
                                      kind="ExternalOutput").ap()
        dbg["d2_tern"] = nc.dram_tensor("d2_tern", [768, HID], BF16,
                                        kind="ExternalOutput").ap()
        dbg["d2_ri0"] = nc.dram_tensor("d2_ri0", [B * S, D], BF16,
                                       kind="ExternalOutput").ap()
        dbg["d2_h16"] = nc.dram_tensor("d2_h16", [128, 4 * HID], BF16,
                                       kind="ExternalOutput").ap()
        dbg["d2_rc"] = nc.dram_tensor("d2_rc", [128, 24], F32,
                                      kind="ExternalOutput").ap()
    ag_xn_in = nc.dram_tensor("ag_xn_in", [D, NTOK], BF16)
    ag_xn_out = nc.dram_tensor("ag_xn_out", [NCORES * D, NTOK], BF16)
    ar_in = nc.dram_tensor("ar_in", [128, 4], F32)
    ar_out = nc.dram_tensor("ar_out", [128, 4], F32)
    ag_t_in = nc.dram_tensor("ag_t_in", [768, HID], BF16)
    ag_t_out = nc.dram_tensor("ag_t_out", [NCORES * 768, HID], BF16)
    rs_in = nc.dram_tensor("rs_in", [B * S, D], BF16)
    rs_out = nc.dram_tensor("rs_out", [NTOK, D], BF16)
    xqf_d = [nc.dram_tensor(f"xqf_d{m}", [128, 8 * D], BF16) for m in range(2)]

    import os as _os
    es = contextlib.ExitStack()
    SEMS, CNT = {}, {}
    with es:
        def _sem(name):
            if name not in SEMS:
                SEMS[name] = es.enter_context(nc.semaphore(name))
                CNT[name] = 0
            return SEMS[name]

        for snm in ["pe", "act", "dve"]:
            _sem(snm)

        def inc(instr, name, k=1):
            _sem(name)
            if name in ("dve", "act"):
                return CNT[name]
            step = 1 if (name in ("pe", "pool") or name.startswith("cc")) \
                else 16
            instr.then_inc(SEMS[name], k * step)
            CNT[name] += k * step
            return CNT[name]

        def wt(eng, name, val):
            _sem(name)
            assert val <= CNT[name], f"wait on future {name}: {val} > {CNT[name]}"
            if val > 0:
                eng.wait_ge(SEMS[name], val)

        def wtt(eng, tup):
            if tup:
                wt(eng, tup[0], tup[1])

        def cum(name):
            _sem(name)
            return (name, CNT[name])

        IDN = es.enter_context(nc.sbuf_tensor("IDN", [128, 128], BF16))
        MSK = es.enter_context(nc.sbuf_tensor("MSK", [128, 128], F32))
        ONEC = es.enter_context(nc.sbuf_tensor("ONEC", [128, 1], F32))
        ONER = es.enter_context(nc.sbuf_tensor("ONER", [1, 128], F32))
        ONECB = es.enter_context(nc.sbuf_tensor("ONECB", [128, 1], BF16))
        ONERB = es.enter_context(nc.sbuf_tensor("ONERB", [1, 128], BF16))
        ONESQ = es.enter_context(nc.sbuf_tensor("ONESQ", [128, 128], BF16))
        EPS5 = es.enter_context(nc.sbuf_tensor("EPS5", [128, 1], F32))
        EPS8 = es.enter_context(nc.sbuf_tensor("EPS8", [128, 1], F32))
        BCS = es.enter_context(nc.sbuf_tensor("BCS", [128, 16], F32))
        PWS = es.enter_context(nc.sbuf_tensor("PWS", [128, 4], F32))
        ARS = es.enter_context(nc.sbuf_tensor("ARS", [128, 4], F32))
        AM12 = es.enter_context(nc.sbuf_tensor("AM12", [128, 16], F32))
        RC1 = es.enter_context(nc.sbuf_tensor("RC1", [128, 8], F32))
        RC2 = es.enter_context(nc.sbuf_tensor("RC2", [128, 8], F32))
        RC3 = es.enter_context(nc.sbuf_tensor("RC3", [128, 8], F32))
        TOT = es.enter_context(nc.sbuf_tensor("TOT", [1, 16], F32))
        ROW = es.enter_context(nc.sbuf_tensor("ROW", [1, 16], F32))

        import os as _os2
        _PRE = _os2.environ.get("KPRE", "cc")
        if _PRE == "all":
            for _nm in ["ccA", "ccB", "ccC", "ccD",
                        "sCONST", "sXT0", "sXT1", "sWFT0", "sWFT1", "sAGI",
                        "sARI", "sARO", "sG12", "sXQS", "sTRN0", "sTRN1", "sW2",
                        "sRHS0", "sRHS1", "sRSB0", "sRSB1", "sG3", "sXQQ",
                        "sW12_0", "sW12_1", "sW3_0", "sW3_1", "sFIN", "sOO0",
                        "sOO1", "oodbg"]:
                _sem(_nm)
        elif _PRE == "cc":
            for _nm in ["ccA", "ccB", "ccC", "ccD", "pool"]:
                _sem(_nm)

        block = es.enter_context(nc.Block())

        class Chain:
            def __init__(self, eng, name):
                self._e = eng
                self._n = name

            def wait_ge(self, sem, val):
                return self._e.wait_ge(sem, val)

            def __getattr__(self, attr):
                real = getattr(self._e, attr)
                if not callable(real):
                    return real
                nm = self._n

                def call(*a, **k):
                    if CNT[nm] > 0:
                        self._e.wait_ge(SEMS[nm], CNT[nm])
                    instr = real(*a, **k)
                    if instr is not None and hasattr(instr, "then_inc"):
                        instr.then_inc(SEMS[nm], 1)
                        CNT[nm] += 1
                    return instr
                return call

        def V_(f):
            block.vector(lambda v: f(Chain(v, "dve")))

        def A_(f):
            block.scalar(lambda a: f(Chain(a, "act")))

        S_, T_, G_ = block.sync, block.tensor, block.gpsimd

        def _finish_early():
            def f(gpsimd):
                for nm in list(SEMS):
                    wt(gpsimd, nm, CNT[nm])
            G_(f)


        esXN = contextlib.ExitStack()
        XN = esXN.enter_context(nc.sbuf_tensor("XN", [128, 8 * D], F32))
        esLN = contextlib.ExitStack()
        XT = [esLN.enter_context(nc.sbuf_tensor(f"XT{i}", [128, D], F32))
              for i in range(2)]
        XBF = esLN.enter_context(nc.sbuf_tensor("XBF", [128, D], BF16))
        SCR = esLN.enter_context(nc.sbuf_tensor("SCR1", [128, D], F32))
        ST1 = esLN.enter_context(nc.sbuf_tensor("ST1", [128, 1], F32))
        ST2 = esLN.enter_context(nc.sbuf_tensor("ST2", [128, 1], F32))
        ST3 = esLN.enter_context(nc.sbuf_tensor("ST3", [128, 1], F32))
        esTF = contextlib.ExitStack()
        XTF = esTF.enter_context(nc.sbuf_tensor("XTF", [128, DC * NTOK], BF16))

        if int(_os.environ.get("KCONSTS", "1")):
            def f(sync):
                inc(sync.dma_start(out=IDN[:], in_=ident_in), "sCONST")
                inc(sync.dma_start(out=MSK[:], in_=mask_in), "sCONST")
                inc(sync.dma_start(out=ONEC[:], in_=onec_in), "sCONST")
                inc(sync.dma_start(out=ONER[:], in_=oner_in), "sCONST")
                inc(sync.dma_start(out=ONECB[:], in_=onecb_in), "sCONST")
                inc(sync.dma_start(out=ONERB[:], in_=onerb_in), "sCONST")
                inc(sync.dma_start(out=ONESQ[:], in_=onesq_in), "sCONST")
            S_(f)
        d_consts = cum("sCONST")

        def f(vector):
            wtt(vector, d_consts)
            vector.memset(EPS5[:], 1e-5)
            vector.memset(EPS8[:], 1e-8)
            inc(vector.memset(PWS[:], 0.0), "dve")
        V_(f)

        esP = contextlib.ExitStack()
        PTB = esP.enter_context(nc.psum_tensor("PTB1", [128, 128], BF16))
        _NTILES = int(_os.environ.get("KNT", "8"))
        ln_done = [0] * 8
        tr_evac = [0]
        ln_tr_pe = [0]
        xdma = [None] * 8
        for i in range(_NTILES):
            def f(sync, i=i):
                if i >= 2:
                    wt(sync, "dve", ln_done[i - 2])
                inc(sync.dma_start(out=XT[i % 2][:],
                                   in_=x_sl[i * 128:(i + 1) * 128, :]), f"sXT{i % 2}")
                xdma[i] = cum(f"sXT{i % 2}")
            S_(f)

            def f(scalar, i=i):
                wtt(scalar, xdma[i])
                if i > 0:
                    wt(scalar, "dve", ln_done[i - 1])
                inc(scalar.activation(SCR[:], XT[i % 2][:], AF.Square,
                                      bias=0.0, scale=1.0, accum_out=ST2[:]), "act")
            A_(f)
            a_sq = CNT["act"]

            def f(vector, i=i, a_sq=a_sq):
                wtt(vector, xdma[i])
                inc(vector.reduce_sum(ST1[:], XT[i % 2][:], AX.X), "dve")
                inc(vector.tensor_scalar_mul(ST1[:], ST1[:], -1.0 / D), "dve")
                wt(vector, "act", a_sq)
                inc(vector.tensor_mul(ST3[:], ST1[:], ST1[:]), "dve")
                inc(vector.scalar_tensor_tensor(ST3[:], ST2[:], 1.0 / D, ST3[:],
                                                ALU.mult, ALU.subtract), "dve")
            V_(f)
            dve_var = CNT["dve"]

            def f(scalar, dve_var=dve_var):
                wt(scalar, "dve", dve_var)
                inc(scalar.activation(ST2[:], ST3[:], AF.Sqrt, bias=EPS5[:],
                                      scale=1.0), "act")
            A_(f)
            a_rstd = CNT["act"]

            def f(vector, i=i, a_rstd=a_rstd, ltp=ln_tr_pe[0]):
                wt(vector, "act", a_rstd)
                inc(vector.reciprocal(ST2[:], ST2[:]), "dve")
                inc(vector.tensor_scalar(XN[:, i * D:(i + 1) * D], XT[i % 2][:],
                                         ST1[:], ST2[:], ALU.add, ALU.mult), "dve")
                if ltp:
                    wt(vector, "pe", ltp)
                inc(vector.tensor_copy(XBF[:], XN[:, i * D:(i + 1) * D]), "dve")
            V_(f)
            ln_done[i] = CNT["dve"]

            for ic in range(DC):
                def f(tensor, i=i, ic=ic, g=ln_done[i], te=tr_evac[-1]):
                    wt(tensor, "dve", g)
                    wt(tensor, "dve", te)
                    wtt(tensor, d_consts)
                    inc(tensor.transpose(PTB[:], XBF[:, ic * 128:(ic + 1) * 128],
                                         IDN[:]), "pe")
                T_(f)
                pe_tr = CNT["pe"]

                def f(vector, i=i, ic=ic, pe_tr=pe_tr):
                    wt(vector, "pe", pe_tr)
                    inc(vector.tensor_copy(
                        XTF[:, ic * NTOK + i * 128: ic * NTOK + (i + 1) * 128],
                        PTB[:]), "dve")
                V_(f)
                tr_evac.append(CNT["dve"])
            ln_tr_pe[0] = CNT["pe"]
        act_xtf = CNT["act"]
        dve_xtf = CNT["dve"]

        _kag = int(_os.environ.get("KAG", "1"))
        if _kag:
            def f(gpsimd):
                wt(gpsimd, "dve", dve_xtf)
                if _kag in (1, 2):
                    for part in range(4):
                        inc(gpsimd.dma_start(
                            out=ag_xn_in.ap()[part * 512:(part + 1) * 512, :]
                            .rearrange("(i p) t -> p i t", p=128),
                            in_=XTF[:, part * 4 * NTOK:(part + 1) * 4 * NTOK]
                            .rearrange("p (i t) -> p i t", i=4)), "sAGI")
                    wt(gpsimd, "sAGI", CNT["sAGI"])
                if _kag in (1, 3):
                    inc(gpsimd.collective_compute(
                        "AllGather", ALU.bypass,
                        replica_groups=[list(range(NCORES))],
                        ins=[ag_xn_in.ap().opt()], outs=[ag_xn_out.ap().opt()]),
                        "ccA")
            G_(f)
        cc_agxn = cum("ccA")
        dG_agxn = cum("sAGI")
        esTF.close()
        if stop_after == "ln":
            _finish_early()
            esLN.close(); esXN.close(); esP.close()
            return nc

        if debug:
            def f(sync):
                wt(sync, "dve", ln_done[7])
                inc(sync.dma_start(out=dbg["xn"].rearrange("(i p) c -> p i c", p=128),
                                   in_=XN[:].rearrange("p (i c) -> p i c", i=8)), "oodbg")
            S_(f)

        esG = contextlib.ExitStack()
        G1B = esG.enter_context(nc.sbuf_tensor("G1B", [128, D], F32))
        G2B = esG.enter_context(nc.sbuf_tensor("G2B", [128, D], F32))
        YT = esG.enter_context(nc.sbuf_tensor("YT", [128, D], F32))
        YG = esG.enter_context(nc.sbuf_tensor("YGg", [128, D], F32))
        XQ8 = esG.enter_context(nc.sbuf_tensor("XQ8g", [128, D], F32))
        XQB = esG.enter_context(nc.sbuf_tensor("XQBg", [128, D], BF16))
        XSTG = esG.enter_context(nc.sbuf_tensor("XSTG", [128, D], BF16))
        AMX = esG.enter_context(nc.sbuf_tensor("AMXg", [128, 1], F32))
        SCLg = esG.enter_context(nc.sbuf_tensor("SCLg", [128, 1], F32))
        SSQ = esG.enter_context(nc.sbuf_tensor("SSQg", [128, 1], F32))
        RSTD = esG.enter_context(nc.sbuf_tensor("RSTDg", [128, 1], F32))
        esGP = contextlib.ExitStack()
        PTG = esGP.enter_context(nc.psum_tensor("PTG", [128, 128], BF16))

        def f(sync):
            wtt(sync, dG_agxn)
            inc(sync.dma_start(out=G1B[:], in_=g1b_in), "sG12")
            inc(sync.dma_start(out=G2B[:], in_=g2b_in), "sG12")
        S_(f)
        d_g12 = cum("sG12")

        xstg_dma = [None]
        last_tr_pe = [0]
        prev_dve_y = [0]
        for i in range(8):
            def f(scalar, i=i, pdy=prev_dve_y[0]):
                wt(scalar, "dve", ln_done[i])
                if i == 0:
                    wtt(scalar, d_g12)
                if pdy:
                    wt(scalar, "dve", pdy)
                inc(scalar.activation(SCR[:], XN[:, i * D:(i + 1) * D], AF.Square,
                                      bias=0.0, scale=1.0, accum_out=SSQ[:]), "act")
                inc(scalar.activation(RSTD[:], SSQ[:], AF.Sqrt, bias=EPS8[:],
                                      scale=1.0 / D), "act")
            A_(f)
            a_rst = CNT["act"]

            def f(vector, i=i, a_rst=a_rst):
                wt(vector, "act", a_rst)
                inc(vector.reciprocal(RSTD[:], RSTD[:]), "dve")
                inc(vector.tensor_scalar_mul(YT[:], XN[:, i * D:(i + 1) * D], RSTD[:]),
                    "dve")
            V_(f)
            dve_y = CNT["dve"]
            prev_dve_y[0] = dve_y

            for mi, GB in enumerate((G1B, G2B)):
                def f(vector, mi=mi, GB=GB, i=i, dve_y=dve_y):
                    wt(vector, "dve", dve_y)
                    inc(vector.tensor_mul(YG[:], YT[:], GB[:]), "dve")
                    inc(vector.reduce_max(AMX[:], YG[:], AX.X,
                                          apply_absolute_value=True), "dve")
                    inc(vector.tensor_scalar_max(AMX[:], AMX[:], 1e-4), "dve")
                    inc(vector.tensor_copy(AM12[:, mi * 8 + i: mi * 8 + i + 1],
                                           AMX[:]), "dve")
                V_(f)
                dve_am = CNT["dve"]

                def f(vector, dve_am=dve_am, ltr=last_tr_pe[0]):
                    wt(vector, "dve", dve_am)
                    if ltr:
                        wt(vector, "pe", ltr)
                    inc(vector.reciprocal(SCLg[:], AMX[:]), "dve")
                    inc(vector.tensor_scalar(XQ8[:], YG[:], SCLg[:], 127.0,
                                             ALU.mult, ALU.mult), "dve")
                    inc(vector.tensor_scalar(YG[:], XQ8[:], MAGIC, MAGIC,
                                             ALU.add, ALU.subtract), "dve")
                    inc(vector.tensor_scalar(XQB[:], YG[:], 127.0, -128.0,
                                             ALU.min, ALU.max), "dve")
                V_(f)
                dve_xq = CNT["dve"]

                stg_evac = [0]
                for ic in range(DC):
                    def f(tensor, ic=ic, dve_xq=dve_xq, se=stg_evac[-1],
                          xd=xstg_dma[-1]):
                        wt(tensor, "dve", dve_xq)
                        wt(tensor, "dve", se)
                        if ic == 0:
                            wtt(tensor, xd)
                        inc(tensor.transpose(PTG[:], XQB[:, ic * 128:(ic + 1) * 128],
                                             IDN[:]), "pe")
                    T_(f)
                    pe_t = CNT["pe"]

                    def f(vector, ic=ic, pe_t=pe_t):
                        wt(vector, "pe", pe_t)
                        inc(vector.tensor_copy(XSTG[:, ic * 128:(ic + 1) * 128],
                                               PTG[:]), "dve")
                    V_(f)
                    stg_evac.append(CNT["dve"])
                last_tr_pe[0] = CNT["pe"]

                def f(sync, mi=mi, i=i, se=stg_evac[-1]):
                    wt(sync, "dve", se)
                    inc(sync.dma_start(out=xqf_d[mi].ap()[:, i * D:(i + 1) * D],
                                       in_=XSTG[:]), "sXQS")
                    xstg_dma.append(cum("sXQS"))
                S_(f)
        act_xqf = CNT["act"]
        dve_g_end = CNT["dve"]
        dS_xqf = cum("sXQS")

        if debug:
            def f(sync):
                wtt(sync, dS_xqf)
                inc(sync.dma_start(out=dbg["xq1"], in_=xqf_d[0].ap()), "oodbg")
                wt(sync, "dve", CNT["dve"])
                inc(sync.dma_start(out=dbg["rc"][:, 0:16], in_=AM12[:]), "oodbg")
            S_(f)
        esGP.close()
        esG.close()
        if stop_after == "g":
            _finish_early()
            esLN.close(); esXN.close(); esP.close()
            return nc

        esP.close()
        esLN.close()
        esXN.close()

        es2 = contextlib.ExitStack()
        WFT = [es2.enter_context(nc.sbuf_tensor(f"WFT{i}", [128, 2048], F32))
               for i in range(2)]
        SG1 = es2.enter_context(nc.sbuf_tensor("SG1", [128, 2048], BF16))
        SG2 = es2.enter_context(nc.sbuf_tensor("SG2", [128, 2048], BF16))
        TRN = [es2.enter_context(nc.sbuf_tensor(f"TRN{i}", [128, 2048], BF16))
               for i in range(2)]
        ST3C = es2.enter_context(nc.sbuf_tensor("ST3C", [128, 1], F32))
        WQB = es2.enter_context(nc.sbuf_tensor("WQB", [128, DC * 768], BF16))
        WOB = es2.enter_context(nc.sbuf_tensor("WOB", [128, 2 * D], BF16))
        esWQ = contextlib.ExitStack()
        WQF = esWQ.enter_context(nc.sbuf_tensor("WQF", [128, DC * 768], F32))
        WOF = esWQ.enter_context(nc.sbuf_tensor("WOF", [128, 2 * D], F32))

        def f(sync):
            wt(sync, "act", act_xqf)
            wt(sync, "dve", dve_g_end)
            wtt(sync, dG_agxn)
            wtt(sync, dS_xqf)
            for part in range(4):
                inc(sync.dma_start(
                    out=WQF[:, part * 4 * 768:(part + 1) * 4 * 768]
                    .rearrange("p (i c) -> p i c", i=4),
                    in_=wqkv_sl[part * 512:(part + 1) * 512, :]
                    .rearrange("(i p) c -> p i c", p=128)), "sW2")
            inc(sync.dma_start(out=WOF[:].rearrange("p (i c) -> p i c", i=2),
                               in_=wo_sl.rearrange("(i p) c -> p i c", p=128)), "sW2")
        S_(f)
        d_w2 = cum("sW2")

        def f(vector):
            wtt(vector, d_w2)
            inc(vector.tensor_copy(WQB[:], WQF[:]), "dve")
            inc(vector.tensor_copy(WOB[:], WOF[:]), "dve")
        V_(f)
        dve_wcast = CNT["dve"]
        esWQ.close()

        QF = [es2.enter_context(nc.sbuf_tensor(f"QF{h}", [128, S], BF16))
              for h in range(2)]
        KF = [es2.enter_context(nc.sbuf_tensor(f"KF{h}", [128, S], BF16))
              for h in range(2)]
        VF = [es2.enter_context(nc.sbuf_tensor(f"VF{h}", [128, S], BF16))
              for h in range(2)]
        RHS = [es2.enter_context(nc.sbuf_tensor(f"RHS{i}", [128, DC * 512], BF16))
               for i in range(2)]
        AFM = es2.enter_context(nc.sbuf_tensor("AFM", [128, 2 * S], BF16))
        VT = es2.enter_context(nc.sbuf_tensor("VT", [128, 32 * 129], BF16))
        SQT = es2.enter_context(nc.sbuf_tensor("SQT", [128, S], BF16))
        MROW = es2.enter_context(nc.sbuf_tensor("MROW", [128, 32], F32))
        MCH = es2.enter_context(nc.sbuf_tensor("MCH", [128, 9], F32))
        MNEG = es2.enter_context(nc.sbuf_tensor("MNEG", [128, 32], F32))
        MTMP = es2.enter_context(nc.sbuf_tensor("MTMP", [128, 1], F32))
        DB = es2.enter_context(nc.sbuf_tensor("DB", [128, 512], BF16))
        ATM = [es2.enter_context(nc.sbuf_tensor(f"ATM{i}", [128, 128], BF16))
               for i in range(2)]
        RCT = [es2.enter_context(nc.sbuf_tensor(f"RCT{i}", [128, 1], F32))
               for i in range(2)]
        PRB = [es2.enter_context(nc.sbuf_tensor(f"PRB{i}", [128, 512], BF16))
               for i in range(3)]
        RSB = [es2.enter_context(nc.sbuf_tensor(f"RSB{i}", [128, D], BF16))
               for i in range(2)]
        NRM = es2.enter_context(nc.sbuf_tensor("NRM", [1, 4], F32))

        def qkv_phase(b, psum_act=None, psum_dve=0):
            esQ = contextlib.ExitStack()
            act_pre = CNT["act"] if psum_act is None else psum_act
            dve_pre0 = psum_dve
            PQ = [esQ.enter_context(nc.psum_tensor(f"PQ{b}_{ob}", [128, 512], F32))
                  for ob in range(6)]
            evac_acts = [0] * 6
            rhs_free = [0] * 8
            rhs_dma = [None] * 8
            for tg in range(8):
                def f(sync, tg=tg, b=b, ap_=act_pre):
                    if tg == 0:
                        wtt(sync, cc_agxn)
                        wt(sync, "dve", dve_wcast)
                    if tg < 2 and b == 1:
                        wt(sync, "act", ap_)
                    if tg >= 2:
                        wt(sync, "act", rhs_free[tg - 2])
                    r = 4 * b + tg // 2
                    c0 = (tg % 2) * 512
                    for part in range(4):
                        src = ag_xn_out.ap()[r * D + part * 512:
                                             r * D + (part + 1) * 512, c0:c0 + 512]
                        inc(sync.dma_start(
                            out=RHS[tg % 2][:, part * 4 * 512:(part + 1) * 4 * 512]
                            .rearrange("p (i t) -> p i t", i=4),
                            in_=src.rearrange("(i p) t -> p i t", p=128)),
                            f"sRHS{tg % 2}")
                    rhs_dma[tg] = cum(f"sRHS{tg % 2}")
                S_(f)

                def f(tensor, tg=tg, ea=list(evac_acts), ap_=act_pre,
                      dp_=dve_pre0):
                    wtt(tensor, rhs_dma[tg])
                    if tg == 0:
                        wt(tensor, "act", ap_)
                        if dp_:
                            wt(tensor, "dve", dp_)
                    for ob in range(6):
                        if ea[ob]:
                            wt(tensor, "act", ea[ob])
                        for ic in range(DC):
                            inc(tensor.matmul(
                                PQ[ob][:],
                                WQB[:, ic * 768 + ob * 128: ic * 768 + (ob + 1) * 128],
                                RHS[tg % 2][:, ic * 512:(ic + 1) * 512],
                                start=(ic == 0), stop=(ic == DC - 1)), "pe")
                T_(f)
                pe_tg = CNT["pe"]

                def f(scalar, tg=tg, pe_tg=pe_tg):
                    wt(scalar, "pe", pe_tg)
                    dsts = [(QF[0], QSC), (KF[0], None), (VF[0], None),
                            (QF[1], QSC), (KF[1], None), (VF[1], None)]
                    for ob, (dstt, sc) in enumerate(dsts):
                        dst = dstt[:, tg * 512:(tg + 1) * 512]
                        if sc is not None:
                            inc(scalar.activation(dst, PQ[ob][:], AF.Copy,
                                                  bias=0.0, scale=sc), "act")
                        else:
                            inc(scalar.copy(dst, PQ[ob][:]), "act")
                        evac_acts[ob] = CNT["act"]
                    rhs_free[tg] = CNT["act"]
                A_(f)
            esQ.close()
            return CNT["act"]

        PRO = {}
        tiles_c = []
        for m in (1, 2, 3):
            r0 = {1: 0, 2: 256, 3: 512}[m]
            for rr in range(r0, r0 + 256, 128):
                for cc_ in range(0, HID, 2048):
                    tiles_c.append((m, rr, cc_))

        def emit_c1():
            c1_red = [0] * len(tiles_c)
            wf1_dma = [None] * len(tiles_c)
            for t, (m, rr, cc_) in enumerate(tiles_c):
                def f(sync, t=t, rr=rr, cc_=cc_):
                    if t >= 2:
                        wt(sync, "dve", c1_red[t - 2])
                    inc(sync.dma_start(out=WFT[t % 2][:],
                                       in_=wf_sl[rr:rr + 128, cc_:cc_ + 2048]),
                        f"sWFT{t % 2}")
                    wf1_dma[t] = cum(f"sWFT{t % 2}")
                S_(f)

                def f(vector, t=t, m=m):
                    wtt(vector, wf1_dma[t])
                    inc(vector.reduce_sum(ST3C[:], WFT[t % 2][:], AX.X,
                                          apply_absolute_value=True), "dve")
                    inc(vector.tensor_add(PWS[:, m - 1:m], PWS[:, m - 1:m],
                                          ST3C[:]), "dve")
                    c1_red[t] = CNT["dve"]
                V_(f)
            PRO["dve_pw"] = CNT["dve"]
            PRO["c1_red"] = c1_red

        def emit_ar():
            def f(gpsimd, dve_pw=PRO["dve_pw"]):
                wt(gpsimd, "dve", dve_pw)
                inc(gpsimd.dma_start(out=ar_in.ap(), in_=PWS[:]), "sARI")
                wt(gpsimd, "sARI", CNT["sARI"])
                inc(gpsimd.collective_compute(
                    "AllReduce", ALU.add, replica_groups=[list(range(NCORES))],
                    ins=[ar_in.ap().opt()], outs=[ar_out.ap().opt()]), "ccB")
                wt(gpsimd, "ccB", CNT["ccB"])
                inc(gpsimd.dma_start(out=ARS[:], in_=ar_out.ap()), "sARO")
            G_(f)
            PRO["dG_ars"] = cum("sARO")

        def emit_c2(psum_act):
            esC = contextlib.ExitStack()
            PT1 = esC.enter_context(nc.psum_tensor("PT1", [1, 4], F32))
            PT2 = esC.enter_context(nc.psum_tensor("PT2", [128, 16], F32))

            def f(tensor, pa=psum_act):
                wtt(tensor, PRO["dG_ars"])
                wt(tensor, "act", pa)
                inc(tensor.matmul(PT1[:], ONEC[:], ARS[:], start=True, stop=True),
                    "pe")
            T_(f)
            pe_tot = CNT["pe"]

            def f(vector, pe_tot=pe_tot):
                wt(vector, "pe", pe_tot)
                vector.memset(ROW[:], 0.0)
                vector.memset(TOT[:], 0.0)
                inc(vector.tensor_copy(TOT[0:1, 0:4], PT1[:]), "dve")
                for m in (1, 2, 3):
                    inc(vector.tensor_scalar_mul(TOT[0:1, m - 1:m],
                                                 TOT[0:1, m - 1:m],
                                                 1.0 / NELW[m]), "dve")
                inc(vector.tensor_scalar(ROW[0:1, 0:3], TOT[0:1, 0:3], 1e-8,
                                         ATANH_HALF, ALU.add, ALU.mult), "dve")
                inc(vector.tensor_scalar_mul(ROW[0:1, 3:6], ROW[0:1, 0:3], -1.0),
                    "dve")
                inc(vector.tensor_scalar_add(ROW[0:1, 9:12], TOT[0:1, 0:3], 1.0),
                    "dve")
                inc(vector.tensor_scalar(ROW[0:1, 12:15], TOT[0:1, 0:3], -1.0, 1.0,
                                         ALU.mult, ALU.add), "dve")
                inc(vector.reciprocal(ROW[0:1, 12:15], ROW[0:1, 12:15]), "dve")
                inc(vector.tensor_mul(ROW[0:1, 9:12], ROW[0:1, 9:12],
                                      ROW[0:1, 12:15]), "dve")
            V_(f)
            dve_row = CNT["dve"]

            def f(scalar, dve_row=dve_row):
                wt(scalar, "dve", dve_row)
                inc(scalar.activation(ROW[0:1, 6:9], ROW[0:1, 9:12], AF.Ln,
                                      bias=0.0, scale=1.0), "act")
            A_(f)
            a_ln = CNT["act"]

            def f(vector, a_ln=a_ln):
                wt(vector, "act", a_ln)
                inc(vector.tensor_scalar_mul(ROW[0:1, 6:9], ROW[0:1, 6:9],
                                             0.25 / 127.0), "dve")
            V_(f)
            dve_row2 = CNT["dve"]

            def f(tensor, dve_row2=dve_row2):
                wt(tensor, "dve", dve_row2)
                inc(tensor.matmul(PT2[:], ONER[:], ROW[0:1, :], start=True,
                                  stop=True), "pe")
            T_(f)
            pe_bcs = CNT["pe"]

            def f(vector, pe_bcs=pe_bcs):
                wt(vector, "pe", pe_bcs)
                inc(vector.tensor_copy(BCS[:], PT2[:]), "dve")
                inc(vector.tensor_scalar_mul(RC1[:], AM12[:, 0:8], BCS[:, 6:7]),
                    "dve")
                inc(vector.tensor_scalar_mul(RC2[:], AM12[:, 8:16], BCS[:, 7:8]),
                    "dve")
            V_(f)
            PRO["dve_bcs"] = CNT["dve"]
            esC.close()

        c2b_state = {"wf2": [None] * len(tiles_c), "sgn": [0] * len(tiles_c),
                     "add": [0] * len(tiles_c), "tdma": [None] * len(tiles_c),
                     "next": 0, "ag": False}

        def _emit_c2b_tile(t):
            m, rr, cc_ = tiles_c[t]
            st = c2b_state

            def f(sync, t=t, rr=rr, cc_=cc_):
                if t == 0:
                    wt(sync, "dve", PRO["dve_bcs"])
                if t >= 2:
                    wt(sync, "act", st["sgn"][t - 2])
                inc(sync.dma_start(out=WFT[t % 2][:],
                                   in_=wf_sl[rr:rr + 128, cc_:cc_ + 2048]),
                    f"sWFT{t % 2}")
                st["wf2"][t] = cum(f"sWFT{t % 2}")
            S_(f)

            def f(scalar, t=t, m=m):
                wtt(scalar, st["wf2"][t])
                if t >= 2:
                    wt(scalar, "dve", st["add"][t - 2])
                elif t == 0:
                    wt(scalar, "dve", PRO["dve_bcs"])
                inc(scalar.activation(SG1[:], WFT[t % 2][:], AF.Sign,
                                      bias=BCS[:, 2 + m:3 + m], scale=1.0), "act")
                inc(scalar.activation(SG2[:], WFT[t % 2][:], AF.Sign,
                                      bias=BCS[:, m - 1:m], scale=1.0), "act")
                st["sgn"][t] = CNT["act"]
            A_(f)

            def f(vector, t=t):
                wt(vector, "act", st["sgn"][t])
                if t >= 2:
                    wtt(vector, st["tdma"][t - 2])
                inc(vector.tensor_add(TRN[t % 2][:], SG1[:], SG2[:]), "dve")
                st["add"][t] = CNT["dve"]
            V_(f)

            def f(gpsimd, t=t, rr=rr, cc_=cc_):
                wt(gpsimd, "dve", st["add"][t])
                inc(gpsimd.dma_start(out=ag_t_in.ap()[rr:rr + 128, cc_:cc_ + 2048],
                                     in_=TRN[t % 2][:]), f"sTRN{t % 2}")
                st["tdma"][t] = cum(f"sTRN{t % 2}")
            G_(f)

        def _emit_ag_tern():
            st = c2b_state

            def f(gpsimd):
                wtt(gpsimd, st["tdma"][-1])
                wtt(gpsimd, st["tdma"][-2])
                inc(gpsimd.collective_compute(
                    "AllGather", ALU.bypass, replica_groups=[list(range(NCORES))],
                    ins=[ag_t_in.ap().opt()], outs=[ag_t_out.ap().opt()]), "ccC")
            G_(f)
            PRO["cc_tern"] = cum("ccC")
            st["ag"] = True

        def c2b_filler(n=4):
            st = c2b_state
            if "dve_bcs" not in PRO:
                return
            while n > 0 and st["next"] < len(tiles_c):
                _emit_c2b_tile(st["next"])
                st["next"] += 1
                n -= 1
            if st["next"] == len(tiles_c) and not st["ag"]:
                _emit_ag_tern()

        def c2b_drain():
            c2b_filler(n=len(tiles_c))

        prev_sco_pe = [0]

        def attn_phase(b, qkv_done, filler=None):
            last_afm = 0
            for h in range(2):
                dve0 = CNT["dve"]
                act0 = qkv_done
                esA = contextlib.ExitStack()
                PTBv = esA.enter_context(nc.psum_tensor(f"PTv{b}{h}", [128, 128], BF16))
                PN = esA.enter_context(nc.psum_tensor(f"PN{b}{h}", [1, 512], F32))

                def f(vector, psp=prev_sco_pe[0]):
                    if psp:
                        wt(vector, "pe", psp)
                    inc(vector.memset(
                        VT[:].rearrange("p (k c) -> p k c", c=129)[:, :, 128:129],
                        1.0), "dve")
                V_(f)
                v_evac = [CNT["dve"]]
                for ki in range(32):
                    def f(tensor, ki=ki, h=h, act0=act0, ve=v_evac[-1]):
                        wt(tensor, "act", act0)
                        wt(tensor, "dve", ve)
                        inc(tensor.transpose(PTBv[:],
                                             VF[h][:, ki * 128:(ki + 1) * 128],
                                             IDN[:]), "pe")
                    T_(f)
                    pe_v = CNT["pe"]

                    def f(vector, ki=ki, pe_v=pe_v):
                        wt(vector, "pe", pe_v)
                        inc(vector.tensor_copy(VT[:, ki * 129:ki * 129 + 128],
                                               PTBv[:]), "dve")
                    V_(f)
                    v_evac.append(CNT["dve"])

                PMX = [esA.enter_context(
                    nc.psum_tensor(f"PMX{b}{h}{i}", [128, 512], F32))
                    for i in range(2)]
                mx_red = {0: 0, 1: 0}
                for qt in range(32):
                    nfull = qt // 4
                    plen = (qt % 4) * 128 + 128
                    chunks = [(kc * 512, 512, False) for kc in range(nfull)]
                    chunks.append((nfull * 512, plen, True))
                    for idx, (k0, klen, partial) in enumerate(chunks):
                        par = idx % 2

                        def f(tensor, qt=qt, k0=k0, klen=klen, par=par, h=h,
                              act0=act0, mr=mx_red[par], dve0=dve0):
                            wt(tensor, "act", act0)
                            wt(tensor, "dve", mr if mr else dve0)
                            inc(tensor.matmul(
                                PMX[par][:, 0:klen],
                                QF[h][:, qt * 128:(qt + 1) * 128],
                                KF[h][:, k0:k0 + klen],
                                start=True, stop=True), "pe")
                        T_(f)
                        pe_mx = CNT["pe"]

                        def f(vector, qt=qt, klen=klen, par=par, idx=idx,
                              partial=partial, pe_mx=pe_mx, nch=len(chunks)):
                            wt(vector, "pe", pe_mx)
                            if partial:
                                inc(vector.tensor_add(
                                    PMX[par][:, klen - 128:klen],
                                    PMX[par][:, klen - 128:klen], MSK[:]), "dve")
                            inc(vector.reduce_max(MCH[:, idx:idx + 1],
                                                  PMX[par][:, 0:klen], AX.X),
                                "dve")
                            mx_red[par] = CNT["dve"]
                            if idx == nch - 1:
                                inc(vector.reduce_max(MROW[:, qt:qt + 1],
                                                      MCH[:, 0:nch], AX.X), "dve")
                        V_(f)

                def f(vector):
                    inc(vector.tensor_scalar(MNEG[:], MROW[:], -1.0, -2.0,
                                             ALU.mult, ALU.add), "dve")
                V_(f)
                dve_negm = CNT["dve"]
                esA.close()


                esA2 = contextlib.ExitStack()
                PSS = [esA2.enter_context(
                    nc.psum_tensor(f"PS{b}{h}{i}", [128, 512], F32))
                    for i in range(3)]
                PSA = [esA2.enter_context(
                    nc.psum_tensor(f"PA{b}{h}{i}", [128, 129], F32))
                    for i in range(4)]
                PTR = esA2.enter_context(nc.psum_tensor(f"PR{b}{h}", [128, 128], BF16))

                exp_last = {0: 0, 1: 0, 2: 0}
                pv_last = {0: 0, 1: 0, 2: 0}
                resc_dve = {}
                tr_done = {0: prev_sco_pe[0], 1: prev_sco_pe[0]}
                afm_acts = [0]
                db_pe = [prev_sco_pe[0]]
                prev_resc0 = CNT["dve"]
                for qg in range(8):
                    def f(vector, qg=qg, dn=dve_negm, dp=db_pe[0]):
                        wt(vector, "dve", dn)
                        if dp:
                            wt(vector, "pe", dp)
                        for j in range(4):
                            qt = 4 * qg + j
                            inc(vector.tensor_scalar_mul(
                                DB[:, j * 128:(j + 1) * 128], IDN[:],
                                MNEG[:, qt:qt + 1]), "dve")
                    V_(f)
                    dve_db = CNT["dve"]

                    nki = 4 * qg + 4
                    for ki in range(nki):
                        par = ki % 3
                        diag = ki - 4 * qg

                        def f(tensor, qg=qg, ki=ki, par=par, h=h,
                              dve_db=dve_db, vl=v_evac[-1], el=exp_last[par]):
                            if el:
                                wt(tensor, "act", el)
                            else:
                                wt(tensor, "dve", vl)
                            wt(tensor, "dve", dve_db)
                            inc(tensor.matmul(PSS[par][:],
                                              KF[h][:, ki * 128:(ki + 1) * 128],
                                              QF[h][:, qg * 512:(qg + 1) * 512],
                                              start=True, stop=False), "pe")
                            inc(tensor.matmul(PSS[par][:], ONESQ[:], DB[:],
                                              start=False, stop=True), "pe")
                        T_(f)
                        pe_sco = CNT["pe"]

                        dve_pre = 0
                        if diag >= 0:
                            def f(vector, par=par, diag=diag, pe_sco=pe_sco,
                                  el=exp_last[par], pl=pv_last[par]):
                                wt(vector, "pe", pe_sco)
                                if el:
                                    wt(vector, "act", el)
                                if pl:
                                    wt(vector, "pe", pl)
                                inc(vector.tensor_add(
                                    PSS[par][:, diag * 128:(diag + 1) * 128],
                                    PSS[par][:, diag * 128:(diag + 1) * 128],
                                    MSK[:]), "dve")
                            V_(f)
                            dve_pre = CNT["dve"]

                        def f(scalar, par=par, diag=diag, pe_sco=pe_sco,
                              dve_pre=dve_pre, pl=pv_last[par]):
                            wt(scalar, "pe", pe_sco)
                            if dve_pre:
                                wt(scalar, "dve", dve_pre)
                            if pl:
                                wt(scalar, "pe", pl)
                            vf = max(0, diag) * 128
                            inc(scalar.activation(PRB[par][:, vf:512],
                                                  PSS[par][:, vf:512],
                                                  AF.Exp, bias=0.0, scale=1.0),
                                "act")
                        A_(f)
                        exp_last[par] = CNT["act"]

                        def f(tensor, qg=qg, ki=ki, par=par, diag=diag,
                              ea=exp_last[par], pr0=prev_resc0):
                            wt(tensor, "act", ea)
                            for qi in range(max(0, diag), 4):
                                if ki == 0:
                                    wt(tensor, "dve", resc_dve.get(qi, pr0))
                                inc(tensor.matmul(
                                    PSA[qi][:],
                                    PRB[par][:, qi * 128:(qi + 1) * 128],
                                    VT[:, ki * 129:(ki + 1) * 129],
                                    start=(ki == 0),
                                    stop=(ki == 4 * qg + qi)), "pe")
                        T_(f)
                        pv_last[par] = CNT["pe"]
                        pv_done_ki = CNT["pe"]
                        db_pe[0] = pe_sco

                        if diag >= 0:
                            qi = diag
                            qt = 4 * qg + qi
                            apar = qt % 2

                            def f(vector, qi=qi, apar=apar, pvd=pv_done_ki,
                                  td=tr_done.get(apar, 0)):
                                wt(vector, "pe", pvd)
                                if td:
                                    wt(vector, "pe", td)
                                inc(vector.reciprocal(RCT[apar][:],
                                                      PSA[qi][:, 128:129]), "dve")
                                inc(vector.tensor_scalar_mul(ATM[apar][:],
                                                             PSA[qi][:, 0:128],
                                                             RCT[apar][:]), "dve")
                            V_(f)
                            resc_dve[qi] = CNT["dve"]

                            def f(tensor, qi=qi, apar=apar, rd=resc_dve[qi],
                                  aa=afm_acts[-1]):
                                wt(tensor, "dve", rd)
                                wt(tensor, "dve", aa)
                                inc(tensor.transpose(PTR[:], ATM[apar][:], IDN[:]),
                                    "pe")
                            T_(f)
                            tr_done[apar] = CNT["pe"]

                            def f(vector, qt=qt, h=h, td=tr_done[apar]):
                                wt(vector, "pe", td)
                                inc(vector.tensor_copy(
                                    AFM[:, h * S + qt * 128: h * S + (qt + 1) * 128],
                                    PTR[:]), "dve")
                            V_(f)
                            afm_acts.append(CNT["dve"])
                    if filler is not None:
                        filler()
                esA2.close()
                prev_sco_pe[0] = CNT["pe"]
                last_afm = afm_acts[-1]
            return last_afm

        def wo_phase(b, afm_done, prev_rs=None):
            esW = contextlib.ExitStack()
            PWO = [esW.enter_context(nc.psum_tensor(f"PW{b}{i}", [128, D], F32))
                   for i in range(2)]
            evac_for = [0] * 32
            rs_dma = [None] * 32
            for tt in range(32):
                buf = tt % 2

                def f(tensor, tt=tt, buf=buf, afm_done=afm_done,
                      ef=(evac_for[tt - 2] if tt >= 2 else 0)):
                    wt(tensor, "dve", afm_done)
                    if ef:
                        wt(tensor, "act", ef)
                    for hc in range(2):
                        for oc in range(4):
                            inc(tensor.matmul(
                                PWO[buf][:, oc * 512:(oc + 1) * 512],
                                AFM[:, hc * S + tt * 128: hc * S + (tt + 1) * 128],
                                WOB[:, hc * D + oc * 512: hc * D + (oc + 1) * 512],
                                start=(hc == 0), stop=(hc == 1)), "pe")
                T_(f)
                pe_wo = CNT["pe"]

                def f(scalar, tt=tt, buf=buf, pe_wo=pe_wo, prev_rs=prev_rs,
                      rg=(rs_dma[tt - 2] if tt >= 2 else None)):
                    wt(scalar, "pe", pe_wo)
                    if rg:
                        wtt(scalar, rg)
                    elif prev_rs:
                        wtt(scalar, prev_rs[buf])
                    inc(scalar.copy(RSB[buf][:], PWO[buf][:]), "act")
                    evac_for[tt] = CNT["act"]
                A_(f)

                def f(gpsimd, tt=tt, buf=buf, b=b):
                    wt(gpsimd, "act", evac_for[tt])
                    inc(gpsimd.dma_start(
                        out=rs_in.ap()[b * S + tt * 128: b * S + (tt + 1) * 128, :],
                        in_=RSB[buf][:]), f"sRSB{buf}")
                    rs_dma[tt] = cum(f"sRSB{buf}")
                G_(f)
            esW.close()
            return (rs_dma[30], rs_dma[31])

        act_qkv0 = qkv_phase(0, psum_act=act_xqf, psum_dve=dve_g_end)
        emit_c1()
        emit_ar()
        emit_c2(act_qkv0)
        if stop_after == "qkv0":
            _finish_early()
            es2.close()
            return nc
        if debug:
            def f(sync):
                wt(sync, "act", act_qkv0)
                inc(sync.dma_start(out=dbg["q0"], in_=QF[0][:]), "oodbg")
                inc(sync.dma_start(out=dbg["k0"], in_=KF[0][:]), "oodbg")
                inc(sync.dma_start(out=dbg["v0"], in_=VF[0][:]), "oodbg")
            S_(f)
        afm0 = attn_phase(0, act_qkv0, filler=c2b_filler)
        if stop_after == "attn0":
            _finish_early()
            es2.close()
            return nc
        if debug:
            def f(sync, afm0=afm0):
                wt(sync, "dve", afm0)
                inc(sync.dma_start(out=dbg["afm0"], in_=AFM[:]), "oodbg")
            S_(f)
        wo0 = wo_phase(0, afm0)
        if stop_after == "wo0":
            _finish_early()
            es2.close()
            return nc
        act_qkv1 = qkv_phase(1)
        if stop_after == "qkv1":
            _finish_early()
            es2.close()
            return nc
        afm1 = attn_phase(1, act_qkv1, filler=c2b_filler)
        if stop_after == "attn1":
            _finish_early()
            es2.close()
            return nc
        c2b_drain()
        wo1 = wo_phase(1, afm1, prev_rs=wo0)
        stage2_act = CNT["act"]
        cc_tern = PRO["cc_tern"]
        if stop_after == "wo1":
            _finish_early()
            es2.close()
            return nc

        def f(gpsimd, wo1=wo1):
            wtt(gpsimd, wo1[0])
            wtt(gpsimd, wo1[1])
            if dbg2:
                inc(gpsimd.dma_start(out=dbg["d2_ri0"], in_=rs_in.ap()), "oodbg")
                wt(gpsimd, "oodbg", CNT["oodbg"])
            inc(gpsimd.collective_compute(
                "ReduceScatter", ALU.add, replica_groups=[list(range(NCORES))],
                ins=[rs_in.ap().opt()], outs=[rs_out.ap().opt()]), "ccD")
        G_(f)
        cc_rs = cum("ccD")
        if stop_after == "rs":
            _finish_early()
            es2.close()
            return nc

        if debug:
            def f(sync):
                wtt(sync, cc_rs)
                inc(sync.dma_start(out=dbg["s"], in_=rs_out.ap()), "oodbg")
            S_(f)

        es2.close()

        esH = contextlib.ExitStack()
        G3B = esH.enter_context(nc.sbuf_tensor("G3B", [128, HID], F32))

        def f(sync):
            wt(sync, "act", stage2_act)
            wtt(sync, wo1[0])
            wtt(sync, wo1[1])
            inc(sync.dma_start(out=G3B[:], in_=g3b_in), "sG3")
        S_(f)
        d_g3 = cum("sG3")

        def w12_src_ap(m, parity, hc, uh):
            m_off = 0 if m == 0 else 256
            off = (m_off + parity * 128 + uh * 4 * 768) * HID + hc * 512
            return mkap(ag_t_out, off, [(HID, 128), (768 * HID, 4), (1, 512)])

        out_dma = []
        PRV = {"mm_last": 0, "w3pe": 0, "h_act": 0, "ob_last": 0,
               "xq_used": 0}
        for qd in range(2):
            esQ = contextlib.ExitStack()
            H16 = esQ.enter_context(nc.sbuf_tensor(f"H16_{qd}",
                                                   [128, 4 * HID], BF16))

            def HT(th):
                return H16[:, th * HID:(th + 1) * HID]

            def XQ3F(th, j):
                return H16[:, th * HID + j * 128: th * HID + (j + 1) * 128]

            esW12 = contextlib.ExitStack()
            XQ12Q = [esW12.enter_context(
                nc.sbuf_tensor(f"XQQ{qd}{m}", [128, 4 * D], BF16))
                for m in range(2)]
            P12 = [[esW12.enter_context(
                nc.psum_tensor(f"P12_{qd}_{m}_{th}", [128, 512], F32))
                for th in range(4)] for m in range(2)]
            W12B = [[esW12.enter_context(
                nc.sbuf_tensor(f"W12B{qd}{m}{i}", [128, 16 * 512], BF16))
                for i in range(2)] for m in range(2)]
            SW = esW12.enter_context(nc.sbuf_tensor(f"SW{qd}", [128, 512], BF16))
            XV = esW12.enter_context(nc.sbuf_tensor(f"XV{qd}", [128, 512], BF16))

            def f(sync, qd=qd, xu=PRV["xq_used"]):
                if qd == 0:
                    wtt(sync, dS_xqf)
                    wt(sync, "act", stage2_act)
                    wtt(sync, wo1[0])
                    wtt(sync, wo1[1])
                if xu:
                    wt(sync, "pe", xu)
                for m in range(2):
                    for hf in range(2):
                        inc(sync.dma_start(
                            out=XQ12Q[m][:, hf * 2 * D:(hf + 1) * 2 * D],
                            in_=xqf_d[m].ap()[:, (qd * 4 + hf * 2) * D:
                                              (qd * 4 + hf * 2 + 2) * D]),
                            "sXQQ")
            S_(f)
            d_xqq = cum("sXQQ")
            if stop_after == "h_xq" and qd == 0:
                _finish_early()
                esW12.close(); esQ.close(); esH.close()
                return nc

            w12_dma = [None] * 16
            h_done = [[0] * 16 for _ in range(4)]
            mm_done = [[0] * 16 for _ in range(4)]
            for hc in range(16):
                def f(sync, qd=qd, hc=hc, pmm=PRV["mm_last"],
                      ob=PRV["ob_last"]):
                    if qd == 0 and hc == 0:
                        wtt(sync, cc_tern)
                    if qd == 0 and hc < 2:
                        wt(sync, "act", stage2_act)
                        wtt(sync, wo1[0])
                        wtt(sync, wo1[1])
                    if hc >= 2:
                        wt(sync, "pe", mm_done[3][hc - 2])
                    elif pmm:
                        wt(sync, "pe", pmm)
                        if ob:
                            wt(sync, "dve", ob)
                    for m in range(2):
                        for parity in range(2):
                            for uh in range(2):
                                outap = W12B[m][hc % 2][:].rearrange(
                                    "p (u q c) -> p u q c", q=2, c=512)[
                                    :, uh * 4:(uh + 1) * 4, parity, :]
                                inc(sync.dma_start(
                                    out=outap,
                                    in_=w12_src_ap(m, parity, hc, uh)),
                                    f"sW12_{hc % 2}")
                    w12_dma[hc] = cum(f"sW12_{hc % 2}")
                S_(f)

                for th in range(4):
                    for m in range(2):
                        def f(tensor, qd=qd, hc=hc, th=th, m=m,
                              hg=(h_done[th][hc - 2] if hc >= 2 else 0), dq=d_xqq):
                            wtt(tensor, w12_dma[hc])
                            if hc == 0 and th == 0 and m == 0:
                                wtt(tensor, dq)
                            if hg:
                                wt(tensor, "dve", hg)
                            for ic in range(DC):
                                inc(tensor.matmul(
                                    P12[m][th][:],
                                    XQ12Q[m][:, (th * DC + ic) * 128:
                                             (th * DC + ic + 1) * 128],
                                    W12B[m][hc % 2][:, ic * 512:(ic + 1) * 512],
                                    start=(ic == 0), stop=(ic == DC - 1)), "pe")
                        T_(f)
                        mm_done[th][hc] = CNT["pe"]

                    def f(scalar, qd=qd, hc=hc, th=th, mmd=mm_done[th][hc],
                          hprev=PRV["h_act"]):
                        wt(scalar, "pe", mmd)
                        if hprev:
                            wt(scalar, "dve", hprev)
                        inc(scalar.activation(
                            SW[:], P12[0][th][:], AF.Silu, bias=0.0,
                            scale=RC1[:, qd * 4 + th:qd * 4 + th + 1]), "act")
                    A_(f)
                    a_sw = CNT["act"]

                    def f(vector, qd=qd, hc=hc, th=th, a_sw=a_sw,
                          mmd=mm_done[th][hc]):
                        wt(vector, "pe", mmd)
                        inc(vector.tensor_scalar_mul(
                            XV[:], P12[1][th][:],
                            RC2[:, qd * 4 + th:qd * 4 + th + 1]), "dve")
                        wt(vector, "act", a_sw)
                        inc(vector.tensor_mul(HT(th)[:, hc * 512:(hc + 1) * 512],
                                              SW[:], XV[:]), "dve")
                        h_done[th][hc] = CNT["dve"]
                        PRV["h_act"] = CNT["dve"]
                    V_(f)
            PRV["mm_last"] = mm_done[3][15]
            esW12.close()
            if stop_after == "h_w12" and qd == 0:
                _finish_early()
                esQ.close(); esH.close()
                return nc

            if debug and qd == 0:
                def f(sync, hd=h_done[0][15]):
                    wt(sync, "dve", hd)
                    inc(sync.dma_start(out=dbg["h0"], in_=HT(0)[:]), "oodbg")
                S_(f)

            esQu = contextlib.ExitStack()
            YGH = esQu.enter_context(nc.sbuf_tensor(f"YGH{qd}", [128, HID], F32))
            YH2 = esQu.enter_context(nc.sbuf_tensor(f"YH2{qd}", [128, HID], F32))
            XQ8H = esQu.enter_context(nc.sbuf_tensor(f"XQ8H{qd}", [128, HID], BF16))
            AMX3 = esQu.enter_context(nc.sbuf_tensor(f"AMX3{qd}", [128, 1], F32))
            SSQ3 = esQu.enter_context(nc.sbuf_tensor(f"SSQ3{qd}", [128, 1], F32))
            RST3 = esQu.enter_context(nc.sbuf_tensor(f"RST3{qd}", [128, 1], F32))
            SCL3 = esQu.enter_context(nc.sbuf_tensor(f"SCL3{qd}", [128, 1], F32))
            esTP = contextlib.ExitStack()
            PTH = esTP.enter_context(nc.psum_tensor(f"PTH{qd}", [128, 128], BF16))

            q3_evac = [0]
            q3_done = [0] * 4
            prev_q8 = [0]
            for th in range(4):
                def f(scalar, th=th, hd=h_done[th][15], dg3=d_g3, qd=qd,
                      pq8=prev_q8[0]):
                    wt(scalar, "dve", hd)
                    if pq8:
                        wt(scalar, "dve", pq8)
                    if th == 0 and qd == 0:
                        wtt(scalar, dg3)
                    inc(scalar.activation(YGH[:], HT(th)[:], AF.Square,
                                          bias=0.0, scale=1.0, accum_out=SSQ3[:]),
                        "act")
                    inc(scalar.activation(RST3[:], SSQ3[:], AF.Sqrt, bias=EPS8[:],
                                          scale=1.0 / HID), "act")
                A_(f)
                a_r3 = CNT["act"]

                def f(vector, th=th, a_r3=a_r3, qd=qd):
                    wt(vector, "act", a_r3)
                    inc(vector.reciprocal(RST3[:], RST3[:]), "dve")
                    inc(vector.scalar_tensor_tensor(YGH[:], HT(th)[:], RST3[:],
                                                    G3B[:], ALU.mult, ALU.mult),
                        "dve")
                    inc(vector.reduce_max(AMX3[:], YGH[:], AX.X,
                                          apply_absolute_value=True), "dve")
                    inc(vector.tensor_scalar_max(AMX3[:], AMX3[:], 1e-4), "dve")
                    inc(vector.tensor_mul(RC3[:, qd * 4 + th:qd * 4 + th + 1],
                                          AMX3[:], BCS[:, 8:9]), "dve")
                V_(f)
                dve_a3 = CNT["dve"]

                def f(vector, dve_a3=dve_a3):
                    wt(vector, "dve", dve_a3)
                    inc(vector.reciprocal(SCL3[:], AMX3[:]), "dve")
                    inc(vector.tensor_scalar(YH2[:], YGH[:], SCL3[:], 127.0,
                                             ALU.mult, ALU.mult), "dve")
                    inc(vector.tensor_scalar(YGH[:], YH2[:], MAGIC, MAGIC,
                                             ALU.add, ALU.subtract), "dve")
                    inc(vector.tensor_scalar(XQ8H[:], YGH[:], 127.0, -128.0,
                                             ALU.min, ALU.max), "dve")
                V_(f)
                dve_q8 = CNT["dve"]
                prev_q8[0] = dve_q8

                for j in range(64):
                    def f(tensor, j=j, dve_q8=dve_q8, qe=q3_evac[-1]):
                        wt(tensor, "dve", dve_q8)
                        wt(tensor, "dve", qe)
                        inc(tensor.transpose(PTH[:],
                                             XQ8H[:, j * 128:(j + 1) * 128],
                                             IDN[:]), "pe")
                    T_(f)
                    pe_th = CNT["pe"]

                    def f(vector, th=th, j=j, pt=pe_th):
                        wt(vector, "pe", pt)
                        inc(vector.tensor_copy(XQ3F(th, j), PTH[:]), "dve")
                    V_(f)
                    q3_evac.append(CNT["dve"])
                q3_done[th] = CNT["dve"]
            if dbg2 and qd == 0:
                def f(sync, q3=q3_done[3]):
                    wt(sync, "dve", q3)
                    for pp in range(4):
                        inc(sync.dma_start(
                            out=dbg["d2_h16"][:, pp * HID:(pp + 1) * HID],
                            in_=H16[:, pp * HID:(pp + 1) * HID]), "oodbg")
                    inc(sync.dma_start(out=dbg["d2_rc"][:, 0:8], in_=RC1[:]),
                        "oodbg")
                    inc(sync.dma_start(out=dbg["d2_rc"][:, 8:16], in_=RC2[:]),
                        "oodbg")
                    inc(sync.dma_start(out=dbg["d2_rc"][:, 16:24], in_=RC3[:]),
                        "oodbg")
                S_(f)
            esTP.close()
            esQu.close()
            if stop_after == "h_q3" and qd == 0:
                _finish_early()
                esQ.close(); esH.close()
                return nc

            esW3 = contextlib.ExitStack()
            PO = [esW3.enter_context(nc.psum_tensor(f"PO{qd}{th}", [128, 1024], F32))
                  for th in range(4)]
            W3B = [esW3.enter_context(
                nc.sbuf_tensor(f"W3B{qd}{i}", [128, 4 * 1024], BF16))
                for i in range(2)]
            FO4 = [esW3.enter_context(nc.sbuf_tensor(f"FO4_{qd}{th}", [128, D], F32))
                   for th in range(4)]
            GT = esW3.enter_context(nc.sbuf_tensor(f"GT{qd}", [128, D], F32))
            SBF = esW3.enter_context(nc.sbuf_tensor(f"SBF{qd}", [128, D], BF16))
            SF32 = esW3.enter_context(nc.sbuf_tensor(f"SF32{qd}", [128, D], F32))
            XT2 = esW3.enter_context(nc.sbuf_tensor(f"XT2{qd}", [128, D], F32))
            OBUF = [esW3.enter_context(nc.sbuf_tensor(f"OB{qd}{i}", [128, D], F32))
                    for i in range(2)]

            w3_dma = [None] * 32
            w3_used = {}
            po_free = [0]
            for och in range(2):
                for rh in range(16):
                    t3 = och * 16 + rh
                    r, hh = rh // 2, rh % 2

                    def f(gpsimd, qd=qd, t3=t3, r=r, hh=hh, och=och,
                          q3=q3_done[3], g=PRV["w3pe"],
                          wg=(w3_used.get(t3 - 2, 0) if t3 >= 2 else 0)):
                        if t3 == 0:
                            wt(gpsimd, "dve", q3)
                        if wg:
                            wt(gpsimd, "pe", wg)
                        elif g:
                            wt(gpsimd, "pe", g)
                        for jh in range(2):
                            off = ((r * 768 + 512) * HID
                                   + (hh * 4 + jh * 2) * 32 * HID + och * 1024)
                            src = mkap(ag_t_out, off,
                                       [(HID, 32), (D, 4), (32 * HID, 2), (1, 1024)])
                            inc(gpsimd.dma_start(
                                out=W3B[t3 % 2][:, jh * 2 * 1024:(jh + 1) * 2 * 1024]
                                .rearrange("p (j c) -> p j c", j=2),
                                in_=src), f"sW3_{t3 % 2}")
                        w3_dma[t3] = cum(f"sW3_{t3 % 2}")
                    G_(f)

                    def f(tensor, qd=qd, t3=t3, rh=rh, r=r, hh=hh, och=och,
                          pf=po_free[0]):
                        wtt(tensor, w3_dma[t3])
                        if rh == 0 and pf:
                            wt(tensor, "dve", pf)
                        for js in range(4):
                            j = r * 8 + hh * 4 + js
                            for th in range(4):
                                for oc in range(2):
                                    inc(tensor.matmul(
                                        PO[th][:, oc * 512:(oc + 1) * 512],
                                        XQ3F(th, j),
                                        W3B[t3 % 2][:, js * 1024 + oc * 512:
                                                    js * 1024 + (oc + 1) * 512],
                                        start=(j == 0), stop=(j == 63)), "pe")
                    T_(f)
                    w3_used[t3] = CNT["pe"]
                pe_w3o = CNT["pe"]

                for th in range(4):
                    def f(vector, th=th, och=och, qd=qd, pw=pe_w3o):
                        wt(vector, "pe", pw)
                        inc(vector.tensor_scalar_mul(
                            FO4[th][:, och * 1024:(och + 1) * 1024], PO[th][:],
                            RC3[:, qd * 4 + th:qd * 4 + th + 1]), "dve")
                    V_(f)
                po_free[0] = CNT["dve"]
            pe_w3 = CNT["pe"]
            PRV["w3pe"] = pe_w3
            dve_fo4 = po_free[0]
            if stop_after == "h_w3" and qd == 0:
                _finish_early()
                esW3.close(); esQ.close(); esH.close()
                return nc

            for th in range(4):
                ti = qd * 4 + th

                def f(sync, ti=ti, g=PRV["ob_last"], q3=q3_done[3]):
                    wtt(sync, cc_rs)
                    wt(sync, "dve", q3)
                    if g:
                        wt(sync, "dve", g)
                    inc(sync.dma_start(
                        out=SBF[:], in_=rs_out.ap()[ti * 128:(ti + 1) * 128, :]),
                        "sFIN")
                    inc(sync.dma_start(out=XT2[:],
                                       in_=x_sl[ti * 128:(ti + 1) * 128, :]), "sFIN")
                S_(f)
                d_fin = cum("sFIN")

                def f(scalar, th=th, dve_fo4=dve_fo4, g=PRV["ob_last"]):
                    wt(scalar, "dve", dve_fo4)
                    if g:
                        wt(scalar, "dve", g)
                    inc(scalar.activation(GT[:], FO4[th][:], AF.Silu, bias=0.0,
                                          scale=1.0),
                        "act")
                A_(f)
                a_g = CNT["act"]

                def f(vector, ti=ti, th=th, a_g=a_g, d_fin=d_fin,
                      g=(out_dma[ti - 2] if ti >= 2 else None)):
                    wt(vector, "act", a_g)
                    wtt(vector, d_fin)
                    if g:
                        wtt(vector, g)
                    inc(vector.tensor_copy(SF32[:], SBF[:]), "dve")
                    inc(vector.tensor_mul(FO4[th][:], GT[:], SF32[:]), "dve")
                    inc(vector.tensor_add(OBUF[ti % 2][:], FO4[th][:], XT2[:]),
                        "dve")
                V_(f)
                ob_last = CNT["dve"]
                PRV["ob_last"] = ob_last

                def f(gpsimd, ti=ti, ou=ob_last):
                    wt(gpsimd, "dve", ou)
                    inc(gpsimd.dma_start(out=y_sl[ti * 128:(ti + 1) * 128, :],
                                         in_=OBUF[ti % 2][:]), f"sOO{ti % 2}")
                    out_dma.append(cum(f"sOO{ti % 2}"))
                G_(f)
            PRV["xq_used"] = pe_w3
            esW3.close()
            esQ.close()
            if stop_after == "h_fin" and qd == 0:
                _finish_early()
                esH.close()
                return nc
        esH.close()

        if dbg2:
            def f(sync):
                for nm in list(SEMS):
                    wt(sync, nm, CNT[nm])
                inc(sync.dma_start(out=dbg["d2_xn"], in_=ag_xn_out.ap()), "oodbg")
                inc(sync.dma_start(out=dbg["d2_s"], in_=rs_out.ap()), "oodbg")
                inc(sync.dma_start(out=dbg["d2_rsin"], in_=rs_in.ap()), "oodbg")
                inc(sync.dma_start(out=dbg["d2_xq"], in_=xqf_d[0].ap()), "oodbg")
                inc(sync.dma_start(out=dbg["d2_tern"],
                                   in_=ag_t_out.ap()[0:768, :]), "oodbg")
            S_(f)

        def f(gpsimd):
            for nm in list(SEMS):
                if nm.startswith("sOO") or nm == "oodbg":
                    wt(gpsimd, nm, CNT[nm])
        G_(f)

    return nc



_CACHED = {}


def _get_nc(debug=False):
    key = bool(debug)
    if key not in _CACHED:
        _CACHED[key] = build(debug=key)
    return _CACHED[key]


def make_in_maps(L0_attn_hidden_states, L0_attn_w_qkv, L0_attn_w_o,
                 L0_ffn_w1, L0_ffn_g1, L0_ffn_w2, L0_ffn_g2, L0_ffn_w3, L0_ffn_g3):
    bf = ml_dtypes.bfloat16
    x = np.ascontiguousarray(np.asarray(L0_attn_hidden_states, np.float32)
                             .reshape(B * S, D))
    wqkvT = np.ascontiguousarray(np.asarray(L0_attn_w_qkv, np.float32).T)
    woT = np.ascontiguousarray(np.asarray(L0_attn_w_o, np.float32).T)
    w1T = np.ascontiguousarray(np.asarray(L0_ffn_w1, np.float32).T)
    w2T = np.ascontiguousarray(np.asarray(L0_ffn_w2, np.float32).T)
    w3T = np.ascontiguousarray(np.asarray(L0_ffn_w3, np.float32).T)

    g1b = np.ascontiguousarray(np.tile(np.asarray(L0_ffn_g1, np.float32)[None, :],
                                       (128, 1)))
    g2b = np.ascontiguousarray(np.tile(np.asarray(L0_ffn_g2, np.float32)[None, :],
                                       (128, 1)))
    g3b = np.ascontiguousarray(np.tile(np.asarray(L0_ffn_g3, np.float32)[None, :],
                                       (128, 1)))
    ident = np.eye(128, dtype=np.float32).astype(bf)
    kk, qq = np.meshgrid(np.arange(128), np.arange(128), indexing="ij")
    maskt = np.where(kk <= qq, 0.0, -1e9).astype(np.float32)
    onec = np.ones((128, 1), np.float32)
    oner = np.ones((1, 128), np.float32)
    onecb = np.ones((128, 1), np.float32).astype(bf)
    onerb = np.ones((1, 128), np.float32).astype(bf)
    onesq = np.ones((128, 128), np.float32).astype(bf)

    in_maps = []
    for c in range(NCORES):
        h0 = 2 * c
        cols = []
        for h in (h0, h0 + 1):
            for off in (0, D, 2 * D):
                cols.append(wqkvT[:, off + h * 128: off + (h + 1) * 128])
        wqkv_sl_a = np.ascontiguousarray(np.concatenate(cols, axis=1))
        wo_sl_a = np.ascontiguousarray(woT[h0 * 128: h0 * 128 + 256, :])
        wf_sl_a = np.ascontiguousarray(np.concatenate([
            w1T[c * 256:(c + 1) * 256, :],
            w2T[c * 256:(c + 1) * 256, :],
            w3T[c * NTOK:(c + 1) * NTOK, :].reshape(256, HID)], axis=0))
        in_maps.append({
            "x_sl": np.ascontiguousarray(x[c * NTOK:(c + 1) * NTOK, :]),
            "wqkv_sl": wqkv_sl_a,
            "wo_sl": wo_sl_a,
            "wf_sl": wf_sl_a,
            "g1b": g1b, "g2b": g2b, "g3b": g3b,
            "ident": ident, "maskt": maskt,
            "onec": onec, "oner": oner, "onecb": onecb, "onerb": onerb,
            "onesq": onesq,
        })
    return in_maps


def kernel(L0_attn_hidden_states, L0_attn_w_qkv, L0_attn_w_o,
           L0_ffn_w1, L0_ffn_g1, L0_ffn_w2, L0_ffn_g2, L0_ffn_w3, L0_ffn_g3):
    in_maps = make_in_maps(L0_attn_hidden_states, L0_attn_w_qkv, L0_attn_w_o,
                           L0_ffn_w1, L0_ffn_g1, L0_ffn_w2, L0_ffn_g2,
                           L0_ffn_w3, L0_ffn_g3)
    nc = _get_nc(debug=False)
    res = run_bass_kernel_spmd(nc, in_maps, core_ids=list(range(NCORES)))
    out = np.concatenate([res.results[c]["y_sl"] for c in range(NCORES)], axis=0)
    return out.reshape(B, S, D)


# version 14 (final)
# speedup vs baseline: 1.0321x; 1.0052x over previous
import contextlib
import numpy as np
import ml_dtypes

import concourse.bass as bass
import concourse.mybir as mybir
from concourse.bass_types import AP
from concourse.bass_utils import run_bass_kernel_spmd

F32 = mybir.dt.float32
BF16 = mybir.dt.bfloat16
I8 = mybir.dt.int8
AF = mybir.ActivationFunctionType
ALU = mybir.AluOpType
AX = mybir.AxisListType

NCORES = 8
D = 2048
DC = 16
S = 4096
B = 2
NTOK = 1024
HID = 8192
QSC = 1.0 / float(np.sqrt(np.float32(128.0)))
ATANH_HALF = 0.5493061443340549
MAGIC = 12582912.0
NELW = {1: 8192 * 2048, 2: 8192 * 2048, 3: 2048 * 8192}


def mkap(t, offset, dims):
    base = t.ap()
    return AP(base.tensor, offset, [[st, cn] for (st, cn) in dims])


def build(debug=False, stop_after="full", dbg2=False):
    nc = bass.Bass("TRN2", target_bir_lowering=False,
                   detect_race_conditions=False)

    x_sl = nc.dram_tensor("x_sl", [NTOK, D], F32, kind="ExternalInput").ap()
    wqkv_sl = nc.dram_tensor("wqkv_sl", [D, 768], F32, kind="ExternalInput").ap()
    wo_sl = nc.dram_tensor("wo_sl", [256, D], F32, kind="ExternalInput").ap()
    wf_sl = nc.dram_tensor("wf_sl", [768, HID], F32, kind="ExternalInput").ap()
    g1b_in = nc.dram_tensor("g1b", [128, D], F32, kind="ExternalInput").ap()
    g2b_in = nc.dram_tensor("g2b", [128, D], F32, kind="ExternalInput").ap()
    g3b_in = nc.dram_tensor("g3b", [128, HID], F32, kind="ExternalInput").ap()
    ident_in = nc.dram_tensor("ident", [128, 128], BF16, kind="ExternalInput").ap()
    mask_in = nc.dram_tensor("maskt", [128, 128], F32, kind="ExternalInput").ap()
    onec_in = nc.dram_tensor("onec", [128, 1], F32, kind="ExternalInput").ap()
    oner_in = nc.dram_tensor("oner", [1, 128], F32, kind="ExternalInput").ap()
    onecb_in = nc.dram_tensor("onecb", [128, 1], BF16, kind="ExternalInput").ap()
    onerb_in = nc.dram_tensor("onerb", [1, 128], BF16, kind="ExternalInput").ap()
    onesq_in = nc.dram_tensor("onesq", [128, 128], BF16, kind="ExternalInput").ap()

    y_sl = nc.dram_tensor("y_sl", [NTOK, D], F32, kind="ExternalOutput").ap()

    dbg = {}
    if debug:
        for nm, shp, dt in [("xn", [NTOK, D], F32), ("q0", [128, S], BF16),
                            ("k0", [128, S], BF16), ("v0", [128, S], BF16),
                            ("afm0", [128, 2 * S], BF16), ("s", [NTOK, D], BF16),
                            ("bcs", [128, 16], F32), ("xq1", [128, 8 * D], BF16),
                            ("h0", [128, HID], BF16), ("negm", [128, 1], F32),
                            ("rc", [128, 24], F32)]:
            dbg[nm] = nc.dram_tensor("dbg_" + nm, shp, dt, kind="ExternalOutput").ap()

    if dbg2:
        dbg["d2_xn"] = nc.dram_tensor("d2_xn", [NCORES * D, NTOK], BF16,
                                      kind="ExternalOutput").ap()
        dbg["d2_s"] = nc.dram_tensor("d2_s", [NTOK, D], BF16,
                                     kind="ExternalOutput").ap()
        dbg["d2_rsin"] = nc.dram_tensor("d2_rsin", [B * S, D], BF16,
                                        kind="ExternalOutput").ap()
        dbg["d2_xq"] = nc.dram_tensor("d2_xq", [128, 8 * D], BF16,
                                      kind="ExternalOutput").ap()
        dbg["d2_tern"] = nc.dram_tensor("d2_tern", [768, HID], BF16,
                                        kind="ExternalOutput").ap()
        dbg["d2_ri0"] = nc.dram_tensor("d2_ri0", [B * S, D], BF16,
                                       kind="ExternalOutput").ap()
        dbg["d2_h16"] = nc.dram_tensor("d2_h16", [128, 4 * HID], BF16,
                                       kind="ExternalOutput").ap()
        dbg["d2_rc"] = nc.dram_tensor("d2_rc", [128, 24], F32,
                                      kind="ExternalOutput").ap()
    ag_xn_in = nc.dram_tensor("ag_xn_in", [D, NTOK], BF16)
    ag_xn_out = nc.dram_tensor("ag_xn_out", [NCORES * D, NTOK], BF16)
    ar_in = nc.dram_tensor("ar_in", [128, 4], F32)
    ar_out = nc.dram_tensor("ar_out", [128, 4], F32)
    ag_t_in = nc.dram_tensor("ag_t_in", [768, HID], BF16)
    ag_t_out = nc.dram_tensor("ag_t_out", [NCORES * 768, HID], BF16)
    rs_in = nc.dram_tensor("rs_in", [B * S, D], BF16)
    rs_out = nc.dram_tensor("rs_out", [NTOK, D], BF16)
    xqf_d = [nc.dram_tensor(f"xqf_d{m}", [128, 8 * D], BF16) for m in range(2)]

    import os as _os
    es = contextlib.ExitStack()
    SEMS, CNT = {}, {}
    with es:
        def _sem(name):
            if name not in SEMS:
                SEMS[name] = es.enter_context(nc.semaphore(name))
                CNT[name] = 0
            return SEMS[name]

        for snm in ["pe", "act", "dve"]:
            _sem(snm)

        def inc(instr, name, k=1):
            _sem(name)
            if name in ("dve", "act"):
                return CNT[name]
            step = 1 if (name in ("pe", "pool") or name.startswith("cc")) \
                else 16
            instr.then_inc(SEMS[name], k * step)
            CNT[name] += k * step
            return CNT[name]

        def wt(eng, name, val):
            _sem(name)
            assert val <= CNT[name], f"wait on future {name}: {val} > {CNT[name]}"
            if val > 0:
                eng.wait_ge(SEMS[name], val)

        def wtt(eng, tup):
            if tup:
                wt(eng, tup[0], tup[1])

        def cum(name):
            _sem(name)
            return (name, CNT[name])

        IDN = es.enter_context(nc.sbuf_tensor("IDN", [128, 128], BF16))
        MSK = es.enter_context(nc.sbuf_tensor("MSK", [128, 128], F32))
        ONEC = es.enter_context(nc.sbuf_tensor("ONEC", [128, 1], F32))
        ONER = es.enter_context(nc.sbuf_tensor("ONER", [1, 128], F32))
        ONECB = es.enter_context(nc.sbuf_tensor("ONECB", [128, 1], BF16))
        ONERB = es.enter_context(nc.sbuf_tensor("ONERB", [1, 128], BF16))
        ONESQ = es.enter_context(nc.sbuf_tensor("ONESQ", [128, 128], BF16))
        EPS5 = es.enter_context(nc.sbuf_tensor("EPS5", [128, 1], F32))
        EPS8 = es.enter_context(nc.sbuf_tensor("EPS8", [128, 1], F32))
        BCS = es.enter_context(nc.sbuf_tensor("BCS", [128, 16], F32))
        PWS = es.enter_context(nc.sbuf_tensor("PWS", [128, 4], F32))
        ARS = es.enter_context(nc.sbuf_tensor("ARS", [128, 4], F32))
        AM12 = es.enter_context(nc.sbuf_tensor("AM12", [128, 16], F32))
        RC1 = es.enter_context(nc.sbuf_tensor("RC1", [128, 8], F32))
        RC2 = es.enter_context(nc.sbuf_tensor("RC2", [128, 8], F32))
        RC3 = es.enter_context(nc.sbuf_tensor("RC3", [128, 8], F32))
        TOT = es.enter_context(nc.sbuf_tensor("TOT", [1, 16], F32))
        ROW = es.enter_context(nc.sbuf_tensor("ROW", [1, 16], F32))

        import os as _os2
        _PRE = _os2.environ.get("KPRE", "cc")
        if _PRE == "all":
            for _nm in ["ccA", "ccB", "ccC", "ccD",
                        "sCONST", "sXT0", "sXT1", "sWFT0", "sWFT1", "sAGI",
                        "sARI", "sARO", "sG12", "sXQS", "sTRN0", "sTRN1", "sW2",
                        "sRHS0", "sRHS1", "sRSB0", "sRSB1", "sG3", "sXQQ",
                        "sW12_0", "sW12_1", "sW3_0", "sW3_1", "sFIN", "sOO0",
                        "sOO1", "oodbg"]:
                _sem(_nm)
        elif _PRE == "cc":
            for _nm in ["ccA", "ccB", "ccC", "ccD", "pool"]:
                _sem(_nm)

        block = es.enter_context(nc.Block())

        class Chain:
            def __init__(self, eng, name):
                self._e = eng
                self._n = name

            def wait_ge(self, sem, val):
                return self._e.wait_ge(sem, val)

            def __getattr__(self, attr):
                real = getattr(self._e, attr)
                if not callable(real):
                    return real
                nm = self._n

                def call(*a, **k):
                    if CNT[nm] > 0:
                        self._e.wait_ge(SEMS[nm], CNT[nm])
                    instr = real(*a, **k)
                    if instr is not None and hasattr(instr, "then_inc"):
                        instr.then_inc(SEMS[nm], 1)
                        CNT[nm] += 1
                    return instr
                return call

        def V_(f):
            block.vector(lambda v: f(Chain(v, "dve")))

        def A_(f):
            block.scalar(lambda a: f(Chain(a, "act")))

        S_, T_, G_ = block.sync, block.tensor, block.gpsimd

        def _finish_early():
            def f(gpsimd):
                for nm in list(SEMS):
                    wt(gpsimd, nm, CNT[nm])
            G_(f)


        esXN = contextlib.ExitStack()
        XN = esXN.enter_context(nc.sbuf_tensor("XN", [128, 8 * D], F32))
        esLN = contextlib.ExitStack()
        XT = [esLN.enter_context(nc.sbuf_tensor(f"XT{i}", [128, D], F32))
              for i in range(2)]
        XBF = esLN.enter_context(nc.sbuf_tensor("XBF", [128, D], BF16))
        SCR = esLN.enter_context(nc.sbuf_tensor("SCR1", [128, D], F32))
        ST1 = esLN.enter_context(nc.sbuf_tensor("ST1", [128, 1], F32))
        ST2 = esLN.enter_context(nc.sbuf_tensor("ST2", [128, 1], F32))
        ST3 = esLN.enter_context(nc.sbuf_tensor("ST3", [128, 1], F32))
        esTF = contextlib.ExitStack()
        XTF = esTF.enter_context(nc.sbuf_tensor("XTF", [128, DC * NTOK], BF16))

        if int(_os.environ.get("KCONSTS", "1")):
            def f(sync):
                inc(sync.dma_start(out=IDN[:], in_=ident_in), "sCONST")
                inc(sync.dma_start(out=MSK[:], in_=mask_in), "sCONST")
                inc(sync.dma_start(out=ONEC[:], in_=onec_in), "sCONST")
                inc(sync.dma_start(out=ONER[:], in_=oner_in), "sCONST")
                inc(sync.dma_start(out=ONECB[:], in_=onecb_in), "sCONST")
                inc(sync.dma_start(out=ONERB[:], in_=onerb_in), "sCONST")
                inc(sync.dma_start(out=ONESQ[:], in_=onesq_in), "sCONST")
            S_(f)
        d_consts = cum("sCONST")

        def f(vector):
            wtt(vector, d_consts)
            vector.memset(EPS5[:], 1e-5)
            vector.memset(EPS8[:], 1e-8)
            inc(vector.memset(PWS[:], 0.0), "dve")
        V_(f)

        esP = contextlib.ExitStack()
        PTB = esP.enter_context(nc.psum_tensor("PTB1", [128, 128], BF16))
        _NTILES = int(_os.environ.get("KNT", "8"))
        ln_done = [0] * 8
        tr_evac = [0]
        ln_tr_pe = [0]
        xdma = [None] * 8
        for i in range(_NTILES):
            def f(sync, i=i):
                if i >= 2:
                    wt(sync, "dve", ln_done[i - 2])
                inc(sync.dma_start(out=XT[i % 2][:],
                                   in_=x_sl[i * 128:(i + 1) * 128, :]), f"sXT{i % 2}")
                xdma[i] = cum(f"sXT{i % 2}")
            S_(f)

            def f(scalar, i=i):
                wtt(scalar, xdma[i])
                if i > 0:
                    wt(scalar, "dve", ln_done[i - 1])
                inc(scalar.activation(SCR[:], XT[i % 2][:], AF.Square,
                                      bias=0.0, scale=1.0, accum_out=ST2[:]), "act")
            A_(f)
            a_sq = CNT["act"]

            def f(vector, i=i, a_sq=a_sq):
                wtt(vector, xdma[i])
                inc(vector.reduce_sum(ST1[:], XT[i % 2][:], AX.X), "dve")
                inc(vector.tensor_scalar_mul(ST1[:], ST1[:], -1.0 / D), "dve")
                wt(vector, "act", a_sq)
                inc(vector.tensor_mul(ST3[:], ST1[:], ST1[:]), "dve")
                inc(vector.scalar_tensor_tensor(ST3[:], ST2[:], 1.0 / D, ST3[:],
                                                ALU.mult, ALU.subtract), "dve")
            V_(f)
            dve_var = CNT["dve"]

            def f(scalar, dve_var=dve_var):
                wt(scalar, "dve", dve_var)
                inc(scalar.activation(ST2[:], ST3[:], AF.Sqrt, bias=EPS5[:],
                                      scale=1.0), "act")
            A_(f)
            a_rstd = CNT["act"]

            def f(vector, i=i, a_rstd=a_rstd, ltp=ln_tr_pe[0]):
                wt(vector, "act", a_rstd)
                inc(vector.reciprocal(ST2[:], ST2[:]), "dve")
                inc(vector.tensor_scalar(XN[:, i * D:(i + 1) * D], XT[i % 2][:],
                                         ST1[:], ST2[:], ALU.add, ALU.mult), "dve")
                if ltp:
                    wt(vector, "pe", ltp)
                inc(vector.tensor_copy(XBF[:], XN[:, i * D:(i + 1) * D]), "dve")
            V_(f)
            ln_done[i] = CNT["dve"]

            for ic in range(DC):
                def f(tensor, i=i, ic=ic, g=ln_done[i], te=tr_evac[-1]):
                    wt(tensor, "dve", g)
                    wt(tensor, "dve", te)
                    wtt(tensor, d_consts)
                    inc(tensor.transpose(PTB[:], XBF[:, ic * 128:(ic + 1) * 128],
                                         IDN[:]), "pe")
                T_(f)
                pe_tr = CNT["pe"]

                def f(vector, i=i, ic=ic, pe_tr=pe_tr):
                    wt(vector, "pe", pe_tr)
                    inc(vector.tensor_copy(
                        XTF[:, ic * NTOK + i * 128: ic * NTOK + (i + 1) * 128],
                        PTB[:]), "dve")
                V_(f)
                tr_evac.append(CNT["dve"])
            ln_tr_pe[0] = CNT["pe"]
        act_xtf = CNT["act"]
        dve_xtf = CNT["dve"]

        _kag = int(_os.environ.get("KAG", "1"))
        if _kag:
            def f(gpsimd):
                wt(gpsimd, "dve", dve_xtf)
                if _kag in (1, 2):
                    for part in range(4):
                        inc(gpsimd.dma_start(
                            out=ag_xn_in.ap()[part * 512:(part + 1) * 512, :]
                            .rearrange("(i p) t -> p i t", p=128),
                            in_=XTF[:, part * 4 * NTOK:(part + 1) * 4 * NTOK]
                            .rearrange("p (i t) -> p i t", i=4)), "sAGI")
                    wt(gpsimd, "sAGI", CNT["sAGI"])
                if _kag in (1, 3):
                    inc(gpsimd.collective_compute(
                        "AllGather", ALU.bypass,
                        replica_groups=[list(range(NCORES))],
                        ins=[ag_xn_in.ap().opt()], outs=[ag_xn_out.ap().opt()]),
                        "ccA")
            G_(f)
        cc_agxn = cum("ccA")
        dG_agxn = cum("sAGI")
        esTF.close()
        if stop_after == "ln":
            _finish_early()
            esLN.close(); esXN.close(); esP.close()
            return nc

        if debug:
            def f(sync):
                wt(sync, "dve", ln_done[7])
                inc(sync.dma_start(out=dbg["xn"].rearrange("(i p) c -> p i c", p=128),
                                   in_=XN[:].rearrange("p (i c) -> p i c", i=8)), "oodbg")
            S_(f)

        esG = contextlib.ExitStack()
        G1B = esG.enter_context(nc.sbuf_tensor("G1B", [128, D], F32))
        G2B = esG.enter_context(nc.sbuf_tensor("G2B", [128, D], F32))
        YT = esG.enter_context(nc.sbuf_tensor("YT", [128, D], F32))
        YG = esG.enter_context(nc.sbuf_tensor("YGg", [128, D], F32))
        XQ8 = esG.enter_context(nc.sbuf_tensor("XQ8g", [128, D], F32))
        XQB = esG.enter_context(nc.sbuf_tensor("XQBg", [128, D], BF16))
        XSTG = esG.enter_context(nc.sbuf_tensor("XSTG", [128, D], BF16))
        AMX = esG.enter_context(nc.sbuf_tensor("AMXg", [128, 1], F32))
        SCLg = esG.enter_context(nc.sbuf_tensor("SCLg", [128, 1], F32))
        SSQ = esG.enter_context(nc.sbuf_tensor("SSQg", [128, 1], F32))
        RSTD = esG.enter_context(nc.sbuf_tensor("RSTDg", [128, 1], F32))
        esGP = contextlib.ExitStack()
        PTG = esGP.enter_context(nc.psum_tensor("PTG", [128, 128], BF16))

        def f(sync):
            wtt(sync, dG_agxn)
            inc(sync.dma_start(out=G1B[:], in_=g1b_in), "sG12")
            inc(sync.dma_start(out=G2B[:], in_=g2b_in), "sG12")
        S_(f)
        d_g12 = cum("sG12")

        xstg_dma = [None]
        last_tr_pe = [0]
        prev_dve_y = [0]
        for i in range(8):
            def f(scalar, i=i, pdy=prev_dve_y[0]):
                wt(scalar, "dve", ln_done[i])
                if i == 0:
                    wtt(scalar, d_g12)
                if pdy:
                    wt(scalar, "dve", pdy)
                inc(scalar.activation(SCR[:], XN[:, i * D:(i + 1) * D], AF.Square,
                                      bias=0.0, scale=1.0, accum_out=SSQ[:]), "act")
                inc(scalar.activation(RSTD[:], SSQ[:], AF.Sqrt, bias=EPS8[:],
                                      scale=1.0 / D), "act")
            A_(f)
            a_rst = CNT["act"]

            def f(vector, i=i, a_rst=a_rst):
                wt(vector, "act", a_rst)
                inc(vector.reciprocal(RSTD[:], RSTD[:]), "dve")
                inc(vector.tensor_scalar_mul(YT[:], XN[:, i * D:(i + 1) * D], RSTD[:]),
                    "dve")
            V_(f)
            dve_y = CNT["dve"]
            prev_dve_y[0] = dve_y

            for mi, GB in enumerate((G1B, G2B)):
                def f(vector, mi=mi, GB=GB, i=i, dve_y=dve_y):
                    wt(vector, "dve", dve_y)
                    inc(vector.tensor_mul(YG[:], YT[:], GB[:]), "dve")
                    inc(vector.reduce_max(AMX[:], YG[:], AX.X,
                                          apply_absolute_value=True), "dve")
                    inc(vector.tensor_scalar_max(AMX[:], AMX[:], 1e-4), "dve")
                    inc(vector.tensor_copy(AM12[:, mi * 8 + i: mi * 8 + i + 1],
                                           AMX[:]), "dve")
                V_(f)
                dve_am = CNT["dve"]

                def f(vector, dve_am=dve_am, ltr=last_tr_pe[0]):
                    wt(vector, "dve", dve_am)
                    if ltr:
                        wt(vector, "pe", ltr)
                    inc(vector.reciprocal(SCLg[:], AMX[:]), "dve")
                    inc(vector.tensor_scalar(XQ8[:], YG[:], SCLg[:], 127.0,
                                             ALU.mult, ALU.mult), "dve")
                    inc(vector.tensor_scalar(YG[:], XQ8[:], MAGIC, MAGIC,
                                             ALU.add, ALU.subtract), "dve")
                    inc(vector.tensor_scalar(XQB[:], YG[:], 127.0, -128.0,
                                             ALU.min, ALU.max), "dve")
                V_(f)
                dve_xq = CNT["dve"]

                stg_evac = [0]
                for ic in range(DC):
                    def f(tensor, ic=ic, dve_xq=dve_xq, se=stg_evac[-1],
                          xd=xstg_dma[-1]):
                        wt(tensor, "dve", dve_xq)
                        wt(tensor, "dve", se)
                        if ic == 0:
                            wtt(tensor, xd)
                        inc(tensor.transpose(PTG[:], XQB[:, ic * 128:(ic + 1) * 128],
                                             IDN[:]), "pe")
                    T_(f)
                    pe_t = CNT["pe"]

                    def f(vector, ic=ic, pe_t=pe_t):
                        wt(vector, "pe", pe_t)
                        inc(vector.tensor_copy(XSTG[:, ic * 128:(ic + 1) * 128],
                                               PTG[:]), "dve")
                    V_(f)
                    stg_evac.append(CNT["dve"])
                last_tr_pe[0] = CNT["pe"]

                def f(sync, mi=mi, i=i, se=stg_evac[-1]):
                    wt(sync, "dve", se)
                    inc(sync.dma_start(out=xqf_d[mi].ap()[:, i * D:(i + 1) * D],
                                       in_=XSTG[:]), "sXQS")
                    xstg_dma.append(cum("sXQS"))
                S_(f)
        act_xqf = CNT["act"]
        dve_g_end = CNT["dve"]
        dS_xqf = cum("sXQS")

        if debug:
            def f(sync):
                wtt(sync, dS_xqf)
                inc(sync.dma_start(out=dbg["xq1"], in_=xqf_d[0].ap()), "oodbg")
                wt(sync, "dve", CNT["dve"])
                inc(sync.dma_start(out=dbg["rc"][:, 0:16], in_=AM12[:]), "oodbg")
            S_(f)
        esGP.close()
        esG.close()
        if stop_after == "g":
            _finish_early()
            esLN.close(); esXN.close(); esP.close()
            return nc

        esP.close()
        esLN.close()
        esXN.close()

        es2 = contextlib.ExitStack()
        WFT = [es2.enter_context(nc.sbuf_tensor(f"WFT{i}", [128, 2048], F32))
               for i in range(2)]
        SG1 = es2.enter_context(nc.sbuf_tensor("SG1", [128, 2048], BF16))
        SG2 = es2.enter_context(nc.sbuf_tensor("SG2", [128, 2048], BF16))
        TRN = [es2.enter_context(nc.sbuf_tensor(f"TRN{i}", [128, 2048], BF16))
               for i in range(2)]
        ST3C = es2.enter_context(nc.sbuf_tensor("ST3C", [128, 1], F32))
        WQB = es2.enter_context(nc.sbuf_tensor("WQB", [128, DC * 768], BF16))
        WOB = es2.enter_context(nc.sbuf_tensor("WOB", [128, 2 * D], BF16))
        esWQ = contextlib.ExitStack()
        WQF = esWQ.enter_context(nc.sbuf_tensor("WQF", [128, DC * 768], F32))
        WOF = esWQ.enter_context(nc.sbuf_tensor("WOF", [128, 2 * D], F32))

        def f(sync):
            wt(sync, "act", act_xqf)
            wt(sync, "dve", dve_g_end)
            wtt(sync, dG_agxn)
            wtt(sync, dS_xqf)
            for part in range(4):
                inc(sync.dma_start(
                    out=WQF[:, part * 4 * 768:(part + 1) * 4 * 768]
                    .rearrange("p (i c) -> p i c", i=4),
                    in_=wqkv_sl[part * 512:(part + 1) * 512, :]
                    .rearrange("(i p) c -> p i c", p=128)), "sW2")
            inc(sync.dma_start(out=WOF[:].rearrange("p (i c) -> p i c", i=2),
                               in_=wo_sl.rearrange("(i p) c -> p i c", p=128)), "sW2")
        S_(f)
        d_w2 = cum("sW2")

        def f(vector):
            wtt(vector, d_w2)
            inc(vector.tensor_copy(WQB[:], WQF[:]), "dve")
            inc(vector.tensor_copy(WOB[:], WOF[:]), "dve")
        V_(f)
        dve_wcast = CNT["dve"]
        esWQ.close()

        QF = [es2.enter_context(nc.sbuf_tensor(f"QF{h}", [128, S], BF16))
              for h in range(2)]
        KF = [es2.enter_context(nc.sbuf_tensor(f"KF{h}", [128, S], BF16))
              for h in range(2)]
        VF = [es2.enter_context(nc.sbuf_tensor(f"VF{h}", [128, S], BF16))
              for h in range(2)]
        RHS = [es2.enter_context(nc.sbuf_tensor(f"RHS{i}", [128, DC * 512], BF16))
               for i in range(2)]
        AFM = es2.enter_context(nc.sbuf_tensor("AFM", [128, 2 * S], BF16))
        VT = es2.enter_context(nc.sbuf_tensor("VT", [128, 32 * 129], BF16))
        SQT = es2.enter_context(nc.sbuf_tensor("SQT", [128, S], BF16))
        MROW = es2.enter_context(nc.sbuf_tensor("MROW", [128, 32], F32))
        MCH = es2.enter_context(nc.sbuf_tensor("MCH", [128, 9], F32))
        MNEG = es2.enter_context(nc.sbuf_tensor("MNEG", [128, 32], F32))
        MTMP = es2.enter_context(nc.sbuf_tensor("MTMP", [128, 1], F32))
        DB = es2.enter_context(nc.sbuf_tensor("DB", [128, 512], BF16))
        ATM = [es2.enter_context(nc.sbuf_tensor(f"ATM{i}", [128, 128], BF16))
               for i in range(2)]
        RCT = [es2.enter_context(nc.sbuf_tensor(f"RCT{i}", [128, 1], F32))
               for i in range(2)]
        PRB = [es2.enter_context(nc.sbuf_tensor(f"PRB{i}", [128, 512], BF16))
               for i in range(3)]
        RSB = [es2.enter_context(nc.sbuf_tensor(f"RSB{i}", [128, D], BF16))
               for i in range(2)]
        NRM = es2.enter_context(nc.sbuf_tensor("NRM", [1, 4], F32))

        def qkv_phase(b, psum_act=None, psum_dve=0):
            esQ = contextlib.ExitStack()
            act_pre = CNT["act"] if psum_act is None else psum_act
            dve_pre0 = psum_dve
            PQ = [esQ.enter_context(nc.psum_tensor(f"PQ{b}_{ob}", [128, 512], F32))
                  for ob in range(6)]
            evac_acts = [0] * 6
            rhs_free = [0] * 8
            rhs_dma = [None] * 8
            for tg in range(8):
                def f(sync, tg=tg, b=b, ap_=act_pre):
                    if tg == 0:
                        wtt(sync, cc_agxn)
                        wt(sync, "dve", dve_wcast)
                    if tg < 2 and b == 1:
                        wt(sync, "act", ap_)
                    if tg >= 2:
                        wt(sync, "act", rhs_free[tg - 2])
                    r = 4 * b + tg // 2
                    c0 = (tg % 2) * 512
                    for part in range(4):
                        src = ag_xn_out.ap()[r * D + part * 512:
                                             r * D + (part + 1) * 512, c0:c0 + 512]
                        inc(sync.dma_start(
                            out=RHS[tg % 2][:, part * 4 * 512:(part + 1) * 4 * 512]
                            .rearrange("p (i t) -> p i t", i=4),
                            in_=src.rearrange("(i p) t -> p i t", p=128)),
                            f"sRHS{tg % 2}")
                    rhs_dma[tg] = cum(f"sRHS{tg % 2}")
                S_(f)

                def f(tensor, tg=tg, ea=list(evac_acts), ap_=act_pre,
                      dp_=dve_pre0):
                    wtt(tensor, rhs_dma[tg])
                    if tg == 0:
                        wt(tensor, "act", ap_)
                        if dp_:
                            wt(tensor, "dve", dp_)
                    for ob in range(6):
                        if ea[ob]:
                            wt(tensor, "act", ea[ob])
                        for ic in range(DC):
                            inc(tensor.matmul(
                                PQ[ob][:],
                                WQB[:, ic * 768 + ob * 128: ic * 768 + (ob + 1) * 128],
                                RHS[tg % 2][:, ic * 512:(ic + 1) * 512],
                                start=(ic == 0), stop=(ic == DC - 1)), "pe")
                T_(f)
                pe_tg = CNT["pe"]

                def f(scalar, tg=tg, pe_tg=pe_tg):
                    wt(scalar, "pe", pe_tg)
                    dsts = [(QF[0], QSC), (KF[0], None), (VF[0], None),
                            (QF[1], QSC), (KF[1], None), (VF[1], None)]
                    for ob, (dstt, sc) in enumerate(dsts):
                        dst = dstt[:, tg * 512:(tg + 1) * 512]
                        if sc is not None:
                            inc(scalar.activation(dst, PQ[ob][:], AF.Copy,
                                                  bias=0.0, scale=sc), "act")
                        else:
                            inc(scalar.copy(dst, PQ[ob][:]), "act")
                        evac_acts[ob] = CNT["act"]
                    rhs_free[tg] = CNT["act"]
                A_(f)
            esQ.close()
            return CNT["act"]

        PRO = {}
        tiles_c = []
        for m in (1, 2, 3):
            r0 = {1: 0, 2: 256, 3: 512}[m]
            for rr in range(r0, r0 + 256, 128):
                for cc_ in range(0, HID, 2048):
                    tiles_c.append((m, rr, cc_))

        def emit_c1():
            c1_red = [0] * len(tiles_c)
            wf1_dma = [None] * len(tiles_c)
            for t, (m, rr, cc_) in enumerate(tiles_c):
                def f(sync, t=t, rr=rr, cc_=cc_):
                    if t >= 2:
                        wt(sync, "dve", c1_red[t - 2])
                    inc(sync.dma_start(out=WFT[t % 2][:],
                                       in_=wf_sl[rr:rr + 128, cc_:cc_ + 2048]),
                        f"sWFT{t % 2}")
                    wf1_dma[t] = cum(f"sWFT{t % 2}")
                S_(f)

                def f(vector, t=t, m=m):
                    wtt(vector, wf1_dma[t])
                    inc(vector.reduce_sum(ST3C[:], WFT[t % 2][:], AX.X,
                                          apply_absolute_value=True), "dve")
                    inc(vector.tensor_add(PWS[:, m - 1:m], PWS[:, m - 1:m],
                                          ST3C[:]), "dve")
                    c1_red[t] = CNT["dve"]
                V_(f)
            PRO["dve_pw"] = CNT["dve"]
            PRO["c1_red"] = c1_red

        def emit_ar():
            def f(gpsimd, dve_pw=PRO["dve_pw"]):
                wt(gpsimd, "dve", dve_pw)
                inc(gpsimd.dma_start(out=ar_in.ap(), in_=PWS[:]), "sARI")
                wt(gpsimd, "sARI", CNT["sARI"])
                inc(gpsimd.collective_compute(
                    "AllReduce", ALU.add, replica_groups=[list(range(NCORES))],
                    ins=[ar_in.ap().opt()], outs=[ar_out.ap().opt()]), "ccB")
                wt(gpsimd, "ccB", CNT["ccB"])
                inc(gpsimd.dma_start(out=ARS[:], in_=ar_out.ap()), "sARO")
            G_(f)
            PRO["dG_ars"] = cum("sARO")

        def emit_c2(psum_act):
            esC = contextlib.ExitStack()
            PT1 = esC.enter_context(nc.psum_tensor("PT1", [1, 4], F32))
            PT2 = esC.enter_context(nc.psum_tensor("PT2", [128, 16], F32))

            def f(tensor, pa=psum_act):
                wtt(tensor, PRO["dG_ars"])
                wt(tensor, "act", pa)
                inc(tensor.matmul(PT1[:], ONEC[:], ARS[:], start=True, stop=True),
                    "pe")
            T_(f)
            pe_tot = CNT["pe"]

            def f(vector, pe_tot=pe_tot):
                wt(vector, "pe", pe_tot)
                vector.memset(ROW[:], 0.0)
                vector.memset(TOT[:], 0.0)
                inc(vector.tensor_copy(TOT[0:1, 0:4], PT1[:]), "dve")
                for m in (1, 2, 3):
                    inc(vector.tensor_scalar_mul(TOT[0:1, m - 1:m],
                                                 TOT[0:1, m - 1:m],
                                                 1.0 / NELW[m]), "dve")
                inc(vector.tensor_scalar(ROW[0:1, 0:3], TOT[0:1, 0:3], 1e-8,
                                         ATANH_HALF, ALU.add, ALU.mult), "dve")
                inc(vector.tensor_scalar_mul(ROW[0:1, 3:6], ROW[0:1, 0:3], -1.0),
                    "dve")
                inc(vector.tensor_scalar_add(ROW[0:1, 9:12], TOT[0:1, 0:3], 1.0),
                    "dve")
                inc(vector.tensor_scalar(ROW[0:1, 12:15], TOT[0:1, 0:3], -1.0, 1.0,
                                         ALU.mult, ALU.add), "dve")
                inc(vector.reciprocal(ROW[0:1, 12:15], ROW[0:1, 12:15]), "dve")
                inc(vector.tensor_mul(ROW[0:1, 9:12], ROW[0:1, 9:12],
                                      ROW[0:1, 12:15]), "dve")
            V_(f)
            dve_row = CNT["dve"]

            def f(scalar, dve_row=dve_row):
                wt(scalar, "dve", dve_row)
                inc(scalar.activation(ROW[0:1, 6:9], ROW[0:1, 9:12], AF.Ln,
                                      bias=0.0, scale=1.0), "act")
            A_(f)
            a_ln = CNT["act"]

            def f(vector, a_ln=a_ln):
                wt(vector, "act", a_ln)
                inc(vector.tensor_scalar_mul(ROW[0:1, 6:9], ROW[0:1, 6:9],
                                             0.25 / 127.0), "dve")
            V_(f)
            dve_row2 = CNT["dve"]

            def f(tensor, dve_row2=dve_row2):
                wt(tensor, "dve", dve_row2)
                inc(tensor.matmul(PT2[:], ONER[:], ROW[0:1, :], start=True,
                                  stop=True), "pe")
            T_(f)
            pe_bcs = CNT["pe"]

            def f(vector, pe_bcs=pe_bcs):
                wt(vector, "pe", pe_bcs)
                inc(vector.tensor_copy(BCS[:], PT2[:]), "dve")
                inc(vector.tensor_scalar_mul(RC1[:], AM12[:, 0:8], BCS[:, 6:7]),
                    "dve")
                inc(vector.tensor_scalar_mul(RC2[:], AM12[:, 8:16], BCS[:, 7:8]),
                    "dve")
            V_(f)
            PRO["dve_bcs"] = CNT["dve"]
            esC.close()

        c2b_state = {"wf2": [None] * len(tiles_c), "sgn": [0] * len(tiles_c),
                     "add": [0] * len(tiles_c), "tdma": [None] * len(tiles_c),
                     "next": 0, "ag": False}

        def _emit_c2b_tile(t):
            m, rr, cc_ = tiles_c[t]
            st = c2b_state

            def f(sync, t=t, rr=rr, cc_=cc_):
                if t == 0:
                    wt(sync, "dve", PRO["dve_bcs"])
                if t >= 2:
                    wt(sync, "act", st["sgn"][t - 2])
                inc(sync.dma_start(out=WFT[t % 2][:],
                                   in_=wf_sl[rr:rr + 128, cc_:cc_ + 2048]),
                    f"sWFT{t % 2}")
                st["wf2"][t] = cum(f"sWFT{t % 2}")
            S_(f)

            def f(scalar, t=t, m=m):
                wtt(scalar, st["wf2"][t])
                if t >= 2:
                    wt(scalar, "dve", st["add"][t - 2])
                elif t == 0:
                    wt(scalar, "dve", PRO["dve_bcs"])
                inc(scalar.activation(SG1[:], WFT[t % 2][:], AF.Sign,
                                      bias=BCS[:, 2 + m:3 + m], scale=1.0), "act")
                inc(scalar.activation(SG2[:], WFT[t % 2][:], AF.Sign,
                                      bias=BCS[:, m - 1:m], scale=1.0), "act")
                st["sgn"][t] = CNT["act"]
            A_(f)

            def f(vector, t=t):
                wt(vector, "act", st["sgn"][t])
                if t >= 2:
                    wtt(vector, st["tdma"][t - 2])
                inc(vector.tensor_add(TRN[t % 2][:], SG1[:], SG2[:]), "dve")
                st["add"][t] = CNT["dve"]
            V_(f)

            def f(gpsimd, t=t, rr=rr, cc_=cc_):
                wt(gpsimd, "dve", st["add"][t])
                inc(gpsimd.dma_start(out=ag_t_in.ap()[rr:rr + 128, cc_:cc_ + 2048],
                                     in_=TRN[t % 2][:]), f"sTRN{t % 2}")
                st["tdma"][t] = cum(f"sTRN{t % 2}")
            G_(f)

        def _emit_ag_tern():
            st = c2b_state

            def f(gpsimd):
                wtt(gpsimd, st["tdma"][-1])
                wtt(gpsimd, st["tdma"][-2])
                inc(gpsimd.collective_compute(
                    "AllGather", ALU.bypass, replica_groups=[list(range(NCORES))],
                    ins=[ag_t_in.ap().opt()], outs=[ag_t_out.ap().opt()]), "ccC")
            G_(f)
            PRO["cc_tern"] = cum("ccC")
            st["ag"] = True

        def c2b_filler(n=4):
            st = c2b_state
            if "dve_bcs" not in PRO:
                return
            while n > 0 and st["next"] < len(tiles_c):
                _emit_c2b_tile(st["next"])
                st["next"] += 1
                n -= 1
            if st["next"] == len(tiles_c) and not st["ag"]:
                _emit_ag_tern()

        def c2b_drain():
            c2b_filler(n=len(tiles_c))

        prev_sco_pe = [0]

        def attn_phase(b, qkv_done, filler=None):
            last_afm = 0
            for h in range(2):
                dve0 = CNT["dve"]
                act0 = qkv_done
                esA = contextlib.ExitStack()
                PTBv = esA.enter_context(nc.psum_tensor(f"PTv{b}{h}", [128, 128], BF16))
                PN = esA.enter_context(nc.psum_tensor(f"PN{b}{h}", [1, 512], F32))

                def f(vector, psp=prev_sco_pe[0]):
                    if psp:
                        wt(vector, "pe", psp)
                    inc(vector.memset(
                        VT[:].rearrange("p (k c) -> p k c", c=129)[:, :, 128:129],
                        1.0), "dve")
                V_(f)
                v_evac = [CNT["dve"]]
                for ki in range(32):
                    def f(tensor, ki=ki, h=h, act0=act0, ve=v_evac[-1]):
                        wt(tensor, "act", act0)
                        wt(tensor, "dve", ve)
                        inc(tensor.transpose(PTBv[:],
                                             VF[h][:, ki * 128:(ki + 1) * 128],
                                             IDN[:]), "pe")
                    T_(f)
                    pe_v = CNT["pe"]

                    def f(vector, ki=ki, pe_v=pe_v):
                        wt(vector, "pe", pe_v)
                        inc(vector.tensor_copy(VT[:, ki * 129:ki * 129 + 128],
                                               PTBv[:]), "dve")
                    V_(f)
                    v_evac.append(CNT["dve"])

                PMX = [esA.enter_context(
                    nc.psum_tensor(f"PMX{b}{h}{i}", [128, 512], F32))
                    for i in range(2)]
                mx_red = {0: 0, 1: 0}
                for qt in range(32):
                    nfull = qt // 4
                    plen = (qt % 4) * 128 + 128
                    chunks = [(kc * 512, 512, False) for kc in range(nfull)]
                    chunks.append((nfull * 512, plen, True))
                    for idx, (k0, klen, partial) in enumerate(chunks):
                        par = idx % 2

                        def f(tensor, qt=qt, k0=k0, klen=klen, par=par, h=h,
                              act0=act0, mr=mx_red[par], dve0=dve0):
                            wt(tensor, "act", act0)
                            wt(tensor, "dve", mr if mr else dve0)
                            inc(tensor.matmul(
                                PMX[par][:, 0:klen],
                                QF[h][:, qt * 128:(qt + 1) * 128],
                                KF[h][:, k0:k0 + klen],
                                start=True, stop=True), "pe")
                        T_(f)
                        pe_mx = CNT["pe"]

                        def f(vector, qt=qt, klen=klen, par=par, idx=idx,
                              partial=partial, pe_mx=pe_mx, nch=len(chunks)):
                            wt(vector, "pe", pe_mx)
                            if partial:
                                inc(vector.tensor_add(
                                    PMX[par][:, klen - 128:klen],
                                    PMX[par][:, klen - 128:klen], MSK[:]), "dve")
                            inc(vector.reduce_max(MCH[:, idx:idx + 1],
                                                  PMX[par][:, 0:klen], AX.X),
                                "dve")
                            mx_red[par] = CNT["dve"]
                            if idx == nch - 1:
                                inc(vector.reduce_max(MROW[:, qt:qt + 1],
                                                      MCH[:, 0:nch], AX.X), "dve")
                        V_(f)

                def f(vector):
                    inc(vector.tensor_scalar(MNEG[:], MROW[:], -1.0, -2.0,
                                             ALU.mult, ALU.add), "dve")
                V_(f)
                dve_negm = CNT["dve"]
                esA.close()


                esA2 = contextlib.ExitStack()
                PSS = [esA2.enter_context(
                    nc.psum_tensor(f"PS{b}{h}{i}", [128, 512], F32))
                    for i in range(3)]
                PSA = [esA2.enter_context(
                    nc.psum_tensor(f"PA{b}{h}{i}", [128, 129], F32))
                    for i in range(4)]
                PTR = esA2.enter_context(nc.psum_tensor(f"PR{b}{h}", [128, 128], BF16))

                exp_last = {0: 0, 1: 0, 2: 0}
                pv_last = {0: 0, 1: 0, 2: 0}
                resc_dve = {}
                tr_done = {0: prev_sco_pe[0], 1: prev_sco_pe[0]}
                afm_acts = [0]
                db_pe = [prev_sco_pe[0]]
                prev_resc0 = CNT["dve"]
                for qg in range(8):
                    def f(vector, qg=qg, dn=dve_negm, dp=db_pe[0]):
                        wt(vector, "dve", dn)
                        if dp:
                            wt(vector, "pe", dp)
                        for j in range(4):
                            qt = 4 * qg + j
                            inc(vector.tensor_scalar_mul(
                                DB[:, j * 128:(j + 1) * 128], IDN[:],
                                MNEG[:, qt:qt + 1]), "dve")
                    V_(f)
                    dve_db = CNT["dve"]

                    nki = 4 * qg + 4
                    for ki in range(nki):
                        par = ki % 3
                        diag = ki - 4 * qg

                        def f(tensor, qg=qg, ki=ki, par=par, h=h,
                              dve_db=dve_db, vl=v_evac[-1], el=exp_last[par]):
                            if el:
                                wt(tensor, "act", el)
                            else:
                                wt(tensor, "dve", vl)
                            wt(tensor, "dve", dve_db)
                            inc(tensor.matmul(PSS[par][:],
                                              KF[h][:, ki * 128:(ki + 1) * 128],
                                              QF[h][:, qg * 512:(qg + 1) * 512],
                                              start=True, stop=False), "pe")
                            inc(tensor.matmul(PSS[par][:], ONESQ[:], DB[:],
                                              start=False, stop=True), "pe")
                        T_(f)
                        pe_sco = CNT["pe"]

                        dve_pre = 0
                        if diag >= 0:
                            def f(vector, par=par, diag=diag, pe_sco=pe_sco,
                                  el=exp_last[par], pl=pv_last[par]):
                                wt(vector, "pe", pe_sco)
                                if el:
                                    wt(vector, "act", el)
                                if pl:
                                    wt(vector, "pe", pl)
                                inc(vector.tensor_add(
                                    PSS[par][:, diag * 128:(diag + 1) * 128],
                                    PSS[par][:, diag * 128:(diag + 1) * 128],
                                    MSK[:]), "dve")
                            V_(f)
                            dve_pre = CNT["dve"]

                        def f(scalar, par=par, diag=diag, pe_sco=pe_sco,
                              dve_pre=dve_pre, pl=pv_last[par]):
                            wt(scalar, "pe", pe_sco)
                            if dve_pre:
                                wt(scalar, "dve", dve_pre)
                            if pl:
                                wt(scalar, "pe", pl)
                            vf = max(0, diag) * 128
                            inc(scalar.activation(PRB[par][:, vf:512],
                                                  PSS[par][:, vf:512],
                                                  AF.Exp, bias=0.0, scale=1.0),
                                "act")
                        A_(f)
                        exp_last[par] = CNT["act"]

                        def f(tensor, qg=qg, ki=ki, par=par, diag=diag,
                              ea=exp_last[par], pr0=prev_resc0):
                            wt(tensor, "act", ea)
                            for qi in range(max(0, diag), 4):
                                if ki == 0:
                                    wt(tensor, "dve", resc_dve.get(qi, pr0))
                                inc(tensor.matmul(
                                    PSA[qi][:],
                                    PRB[par][:, qi * 128:(qi + 1) * 128],
                                    VT[:, ki * 129:(ki + 1) * 129],
                                    start=(ki == 0),
                                    stop=(ki == 4 * qg + qi)), "pe")
                        T_(f)
                        pv_last[par] = CNT["pe"]
                        pv_done_ki = CNT["pe"]
                        db_pe[0] = pe_sco

                        if diag >= 0:
                            qi = diag
                            qt = 4 * qg + qi
                            apar = qt % 2

                            def f(vector, qi=qi, apar=apar, pvd=pv_done_ki,
                                  td=tr_done.get(apar, 0)):
                                wt(vector, "pe", pvd)
                                if td:
                                    wt(vector, "pe", td)
                                inc(vector.reciprocal(RCT[apar][:],
                                                      PSA[qi][:, 128:129]), "dve")
                                inc(vector.tensor_scalar_mul(ATM[apar][:],
                                                             PSA[qi][:, 0:128],
                                                             RCT[apar][:]), "dve")
                            V_(f)
                            resc_dve[qi] = CNT["dve"]

                            def f(tensor, qi=qi, apar=apar, rd=resc_dve[qi],
                                  aa=afm_acts[-1]):
                                wt(tensor, "dve", rd)
                                wt(tensor, "dve", aa)
                                inc(tensor.transpose(PTR[:], ATM[apar][:], IDN[:]),
                                    "pe")
                            T_(f)
                            tr_done[apar] = CNT["pe"]

                            def f(vector, qt=qt, h=h, td=tr_done[apar]):
                                wt(vector, "pe", td)
                                inc(vector.tensor_copy(
                                    AFM[:, h * S + qt * 128: h * S + (qt + 1) * 128],
                                    PTR[:]), "dve")
                            V_(f)
                            afm_acts.append(CNT["dve"])
                    if filler is not None:
                        filler()
                esA2.close()
                prev_sco_pe[0] = CNT["pe"]
                last_afm = afm_acts[-1]
            return last_afm

        def wo_phase(b, afm_done, prev_rs=None):
            esW = contextlib.ExitStack()
            PWO = [esW.enter_context(nc.psum_tensor(f"PW{b}{i}", [128, D], F32))
                   for i in range(2)]
            evac_for = [0] * 32
            rs_dma = [None] * 32
            for tt in range(32):
                buf = tt % 2

                def f(tensor, tt=tt, buf=buf, afm_done=afm_done,
                      ef=(evac_for[tt - 2] if tt >= 2 else 0)):
                    wt(tensor, "dve", afm_done)
                    if ef:
                        wt(tensor, "dve", ef)
                    for hc in range(2):
                        for oc in range(4):
                            inc(tensor.matmul(
                                PWO[buf][:, oc * 512:(oc + 1) * 512],
                                AFM[:, hc * S + tt * 128: hc * S + (tt + 1) * 128],
                                WOB[:, hc * D + oc * 512: hc * D + (oc + 1) * 512],
                                start=(hc == 0), stop=(hc == 1)), "pe")
                T_(f)
                pe_wo = CNT["pe"]

                def f(vector, tt=tt, buf=buf, pe_wo=pe_wo, prev_rs=prev_rs,
                      rg=(rs_dma[tt - 2] if tt >= 2 else None)):
                    wt(vector, "pe", pe_wo)
                    if rg:
                        wtt(vector, rg)
                    elif prev_rs:
                        wtt(vector, prev_rs[buf])
                    inc(vector.tensor_copy(RSB[buf][:], PWO[buf][:]), "dve")
                    evac_for[tt] = CNT["dve"]
                V_(f)

                def f(gpsimd, tt=tt, buf=buf, b=b):
                    wt(gpsimd, "dve", evac_for[tt])
                    inc(gpsimd.dma_start(
                        out=rs_in.ap()[b * S + tt * 128: b * S + (tt + 1) * 128, :],
                        in_=RSB[buf][:]), f"sRSB{buf}")
                    rs_dma[tt] = cum(f"sRSB{buf}")
                G_(f)
            esW.close()
            PRO["wo_evac_dve"] = CNT["dve"]
            return (rs_dma[30], rs_dma[31])

        act_qkv0 = qkv_phase(0, psum_act=act_xqf, psum_dve=dve_g_end)
        emit_c1()
        emit_ar()
        emit_c2(act_qkv0)
        if stop_after == "qkv0":
            _finish_early()
            es2.close()
            return nc
        if debug:
            def f(sync):
                wt(sync, "act", act_qkv0)
                inc(sync.dma_start(out=dbg["q0"], in_=QF[0][:]), "oodbg")
                inc(sync.dma_start(out=dbg["k0"], in_=KF[0][:]), "oodbg")
                inc(sync.dma_start(out=dbg["v0"], in_=VF[0][:]), "oodbg")
            S_(f)
        afm0 = attn_phase(0, act_qkv0, filler=c2b_filler)
        if stop_after == "attn0":
            _finish_early()
            es2.close()
            return nc
        if debug:
            def f(sync, afm0=afm0):
                wt(sync, "dve", afm0)
                inc(sync.dma_start(out=dbg["afm0"], in_=AFM[:]), "oodbg")
            S_(f)
        wo0 = wo_phase(0, afm0)
        if stop_after == "wo0":
            _finish_early()
            es2.close()
            return nc
        act_qkv1 = qkv_phase(1, psum_dve=PRO["wo_evac_dve"])
        if stop_after == "qkv1":
            _finish_early()
            es2.close()
            return nc
        afm1 = attn_phase(1, act_qkv1, filler=c2b_filler)
        if stop_after == "attn1":
            _finish_early()
            es2.close()
            return nc
        c2b_drain()
        wo1 = wo_phase(1, afm1, prev_rs=wo0)
        stage2_act = CNT["act"]
        cc_tern = PRO["cc_tern"]
        if stop_after == "wo1":
            _finish_early()
            es2.close()
            return nc

        def f(gpsimd, wo1=wo1):
            wtt(gpsimd, wo1[0])
            wtt(gpsimd, wo1[1])
            if dbg2:
                inc(gpsimd.dma_start(out=dbg["d2_ri0"], in_=rs_in.ap()), "oodbg")
                wt(gpsimd, "oodbg", CNT["oodbg"])
            inc(gpsimd.collective_compute(
                "ReduceScatter", ALU.add, replica_groups=[list(range(NCORES))],
                ins=[rs_in.ap().opt()], outs=[rs_out.ap().opt()]), "ccD")
        G_(f)
        cc_rs = cum("ccD")
        if stop_after == "rs":
            _finish_early()
            es2.close()
            return nc

        if debug:
            def f(sync):
                wtt(sync, cc_rs)
                inc(sync.dma_start(out=dbg["s"], in_=rs_out.ap()), "oodbg")
            S_(f)

        es2.close()

        esH = contextlib.ExitStack()
        G3B = esH.enter_context(nc.sbuf_tensor("G3B", [128, HID], F32))

        def f(sync):
            wt(sync, "act", stage2_act)
            wtt(sync, wo1[0])
            wtt(sync, wo1[1])
            inc(sync.dma_start(out=G3B[:], in_=g3b_in), "sG3")
        S_(f)
        d_g3 = cum("sG3")

        def w12_src_ap(m, parity, hc, uh):
            m_off = 0 if m == 0 else 256
            off = (m_off + parity * 128 + uh * 4 * 768) * HID + hc * 512
            return mkap(ag_t_out, off, [(HID, 128), (768 * HID, 4), (1, 512)])

        out_dma = []
        PRV = {"mm_last": 0, "w3pe": 0, "h_act": 0, "ob_last": 0,
               "xq_used": 0}
        for qd in range(2):
            esQ = contextlib.ExitStack()
            H16 = esQ.enter_context(nc.sbuf_tensor(f"H16_{qd}",
                                                   [128, 4 * HID], BF16))

            def HT(th):
                return H16[:, th * HID:(th + 1) * HID]

            def XQ3F(th, j):
                return H16[:, th * HID + j * 128: th * HID + (j + 1) * 128]

            esW12 = contextlib.ExitStack()
            XQ12Q = [esW12.enter_context(
                nc.sbuf_tensor(f"XQQ{qd}{m}", [128, 4 * D], BF16))
                for m in range(2)]
            P12 = [[esW12.enter_context(
                nc.psum_tensor(f"P12_{qd}_{m}_{th}", [128, 512], F32))
                for th in range(4)] for m in range(2)]
            W12B = [[esW12.enter_context(
                nc.sbuf_tensor(f"W12B{qd}{m}{i}", [128, 16 * 512], BF16))
                for i in range(2)] for m in range(2)]
            SW = esW12.enter_context(nc.sbuf_tensor(f"SW{qd}", [128, 512], BF16))
            XV = esW12.enter_context(nc.sbuf_tensor(f"XV{qd}", [128, 512], BF16))

            def f(sync, qd=qd, xu=PRV["xq_used"]):
                if qd == 0:
                    wtt(sync, dS_xqf)
                    wt(sync, "act", stage2_act)
                    wtt(sync, wo1[0])
                    wtt(sync, wo1[1])
                if xu:
                    wt(sync, "pe", xu)
                for m in range(2):
                    for hf in range(2):
                        inc(sync.dma_start(
                            out=XQ12Q[m][:, hf * 2 * D:(hf + 1) * 2 * D],
                            in_=xqf_d[m].ap()[:, (qd * 4 + hf * 2) * D:
                                              (qd * 4 + hf * 2 + 2) * D]),
                            "sXQQ")
            S_(f)
            d_xqq = cum("sXQQ")
            if stop_after == "h_xq" and qd == 0:
                _finish_early()
                esW12.close(); esQ.close(); esH.close()
                return nc

            w12_dma = [None] * 16
            h_done = [[0] * 16 for _ in range(4)]
            mm_done = [[0] * 16 for _ in range(4)]
            for hc in range(16):
                def f(sync, qd=qd, hc=hc, pmm=PRV["mm_last"],
                      ob=PRV["ob_last"]):
                    if qd == 0 and hc == 0:
                        wtt(sync, cc_tern)
                    if qd == 0 and hc < 2:
                        wt(sync, "act", stage2_act)
                        wtt(sync, wo1[0])
                        wtt(sync, wo1[1])
                    if hc >= 2:
                        wt(sync, "pe", mm_done[3][hc - 2])
                    elif pmm:
                        wt(sync, "pe", pmm)
                        if ob:
                            wt(sync, "dve", ob)
                    for m in range(2):
                        for parity in range(2):
                            for uh in range(2):
                                outap = W12B[m][hc % 2][:].rearrange(
                                    "p (u q c) -> p u q c", q=2, c=512)[
                                    :, uh * 4:(uh + 1) * 4, parity, :]
                                inc(sync.dma_start(
                                    out=outap,
                                    in_=w12_src_ap(m, parity, hc, uh)),
                                    f"sW12_{hc % 2}")
                    w12_dma[hc] = cum(f"sW12_{hc % 2}")
                S_(f)

                for th in range(4):
                    for m in range(2):
                        def f(tensor, qd=qd, hc=hc, th=th, m=m,
                              hg=(h_done[th][hc - 2] if hc >= 2 else 0), dq=d_xqq):
                            wtt(tensor, w12_dma[hc])
                            if hc == 0 and th == 0 and m == 0:
                                wtt(tensor, dq)
                            if hg:
                                wt(tensor, "dve", hg)
                            for ic in range(DC):
                                inc(tensor.matmul(
                                    P12[m][th][:],
                                    XQ12Q[m][:, (th * DC + ic) * 128:
                                             (th * DC + ic + 1) * 128],
                                    W12B[m][hc % 2][:, ic * 512:(ic + 1) * 512],
                                    start=(ic == 0), stop=(ic == DC - 1)), "pe")
                        T_(f)
                        mm_done[th][hc] = CNT["pe"]

                    def f(scalar, qd=qd, hc=hc, th=th, mmd=mm_done[th][hc],
                          hprev=PRV["h_act"]):
                        wt(scalar, "pe", mmd)
                        if hprev:
                            wt(scalar, "dve", hprev)
                        inc(scalar.activation(
                            SW[:], P12[0][th][:], AF.Silu, bias=0.0,
                            scale=RC1[:, qd * 4 + th:qd * 4 + th + 1]), "act")
                    A_(f)
                    a_sw = CNT["act"]

                    def f(vector, qd=qd, hc=hc, th=th, a_sw=a_sw,
                          mmd=mm_done[th][hc]):
                        wt(vector, "pe", mmd)
                        inc(vector.tensor_scalar_mul(
                            XV[:], P12[1][th][:],
                            RC2[:, qd * 4 + th:qd * 4 + th + 1]), "dve")
                        wt(vector, "act", a_sw)
                        inc(vector.tensor_mul(HT(th)[:, hc * 512:(hc + 1) * 512],
                                              SW[:], XV[:]), "dve")
                        h_done[th][hc] = CNT["dve"]
                        PRV["h_act"] = CNT["dve"]
                    V_(f)
            PRV["mm_last"] = mm_done[3][15]
            esW12.close()
            if stop_after == "h_w12" and qd == 0:
                _finish_early()
                esQ.close(); esH.close()
                return nc

            if debug and qd == 0:
                def f(sync, hd=h_done[0][15]):
                    wt(sync, "dve", hd)
                    inc(sync.dma_start(out=dbg["h0"], in_=HT(0)[:]), "oodbg")
                S_(f)

            esQu = contextlib.ExitStack()
            YGH = esQu.enter_context(nc.sbuf_tensor(f"YGH{qd}", [128, HID], F32))
            YH2 = esQu.enter_context(nc.sbuf_tensor(f"YH2{qd}", [128, HID], F32))
            XQ8H = esQu.enter_context(nc.sbuf_tensor(f"XQ8H{qd}", [128, HID], BF16))
            AMX3 = esQu.enter_context(nc.sbuf_tensor(f"AMX3{qd}", [128, 1], F32))
            SSQ3 = esQu.enter_context(nc.sbuf_tensor(f"SSQ3{qd}", [128, 1], F32))
            RST3 = esQu.enter_context(nc.sbuf_tensor(f"RST3{qd}", [128, 1], F32))
            SCL3 = esQu.enter_context(nc.sbuf_tensor(f"SCL3{qd}", [128, 1], F32))
            esTP = contextlib.ExitStack()
            PTH = esTP.enter_context(nc.psum_tensor(f"PTH{qd}", [128, 128], BF16))

            q3_evac = [0]
            q3_done = [0] * 4
            prev_q8 = [0]
            for th in range(4):
                def f(scalar, th=th, hd=h_done[th][15], dg3=d_g3, qd=qd,
                      pq8=prev_q8[0]):
                    wt(scalar, "dve", hd)
                    if pq8:
                        wt(scalar, "dve", pq8)
                    if th == 0 and qd == 0:
                        wtt(scalar, dg3)
                    inc(scalar.activation(YGH[:], HT(th)[:], AF.Square,
                                          bias=0.0, scale=1.0, accum_out=SSQ3[:]),
                        "act")
                    inc(scalar.activation(RST3[:], SSQ3[:], AF.Sqrt, bias=EPS8[:],
                                          scale=1.0 / HID), "act")
                A_(f)
                a_r3 = CNT["act"]

                def f(vector, th=th, a_r3=a_r3, qd=qd):
                    wt(vector, "act", a_r3)
                    inc(vector.reciprocal(RST3[:], RST3[:]), "dve")
                    inc(vector.scalar_tensor_tensor(YGH[:], HT(th)[:], RST3[:],
                                                    G3B[:], ALU.mult, ALU.mult),
                        "dve")
                    inc(vector.reduce_max(AMX3[:], YGH[:], AX.X,
                                          apply_absolute_value=True), "dve")
                    inc(vector.tensor_scalar_max(AMX3[:], AMX3[:], 1e-4), "dve")
                    inc(vector.tensor_mul(RC3[:, qd * 4 + th:qd * 4 + th + 1],
                                          AMX3[:], BCS[:, 8:9]), "dve")
                V_(f)
                dve_a3 = CNT["dve"]

                def f(vector, dve_a3=dve_a3):
                    wt(vector, "dve", dve_a3)
                    inc(vector.reciprocal(SCL3[:], AMX3[:]), "dve")
                    inc(vector.tensor_scalar(YH2[:], YGH[:], SCL3[:], 127.0,
                                             ALU.mult, ALU.mult), "dve")
                    inc(vector.tensor_scalar(YGH[:], YH2[:], MAGIC, MAGIC,
                                             ALU.add, ALU.subtract), "dve")
                    inc(vector.tensor_scalar(XQ8H[:], YGH[:], 127.0, -128.0,
                                             ALU.min, ALU.max), "dve")
                V_(f)
                dve_q8 = CNT["dve"]
                prev_q8[0] = dve_q8

                for j in range(64):
                    def f(tensor, j=j, dve_q8=dve_q8, qe=q3_evac[-1]):
                        wt(tensor, "dve", dve_q8)
                        wt(tensor, "dve", qe)
                        inc(tensor.transpose(PTH[:],
                                             XQ8H[:, j * 128:(j + 1) * 128],
                                             IDN[:]), "pe")
                    T_(f)
                    pe_th = CNT["pe"]

                    def f(vector, th=th, j=j, pt=pe_th):
                        wt(vector, "pe", pt)
                        inc(vector.tensor_copy(XQ3F(th, j), PTH[:]), "dve")
                    V_(f)
                    q3_evac.append(CNT["dve"])
                q3_done[th] = CNT["dve"]
            if dbg2 and qd == 0:
                def f(sync, q3=q3_done[3]):
                    wt(sync, "dve", q3)
                    for pp in range(4):
                        inc(sync.dma_start(
                            out=dbg["d2_h16"][:, pp * HID:(pp + 1) * HID],
                            in_=H16[:, pp * HID:(pp + 1) * HID]), "oodbg")
                    inc(sync.dma_start(out=dbg["d2_rc"][:, 0:8], in_=RC1[:]),
                        "oodbg")
                    inc(sync.dma_start(out=dbg["d2_rc"][:, 8:16], in_=RC2[:]),
                        "oodbg")
                    inc(sync.dma_start(out=dbg["d2_rc"][:, 16:24], in_=RC3[:]),
                        "oodbg")
                S_(f)
            esTP.close()
            esQu.close()
            if stop_after == "h_q3" and qd == 0:
                _finish_early()
                esQ.close(); esH.close()
                return nc

            esW3 = contextlib.ExitStack()
            PO = [esW3.enter_context(nc.psum_tensor(f"PO{qd}{th}", [128, 1024], F32))
                  for th in range(4)]
            W3B = [esW3.enter_context(
                nc.sbuf_tensor(f"W3B{qd}{i}", [128, 4 * 1024], BF16))
                for i in range(2)]
            FO4 = [esW3.enter_context(nc.sbuf_tensor(f"FO4_{qd}{th}", [128, D], F32))
                   for th in range(4)]
            GT = esW3.enter_context(nc.sbuf_tensor(f"GT{qd}", [128, D], F32))
            SBF = esW3.enter_context(nc.sbuf_tensor(f"SBF{qd}", [128, D], BF16))
            SF32 = esW3.enter_context(nc.sbuf_tensor(f"SF32{qd}", [128, D], F32))
            XT2 = esW3.enter_context(nc.sbuf_tensor(f"XT2{qd}", [128, D], F32))
            OBUF = [esW3.enter_context(nc.sbuf_tensor(f"OB{qd}{i}", [128, D], F32))
                    for i in range(2)]

            w3_dma = [None] * 32
            w3_used = {}
            po_free = [0]
            for och in range(2):
                for rh in range(16):
                    t3 = och * 16 + rh
                    r, hh = rh // 2, rh % 2

                    def f(gpsimd, qd=qd, t3=t3, r=r, hh=hh, och=och,
                          q3=q3_done[3], g=PRV["w3pe"],
                          wg=(w3_used.get(t3 - 2, 0) if t3 >= 2 else 0)):
                        if t3 == 0:
                            wt(gpsimd, "dve", q3)
                        if wg:
                            wt(gpsimd, "pe", wg)
                        elif g:
                            wt(gpsimd, "pe", g)
                        for jh in range(2):
                            off = ((r * 768 + 512) * HID
                                   + (hh * 4 + jh * 2) * 32 * HID + och * 1024)
                            src = mkap(ag_t_out, off,
                                       [(HID, 32), (D, 4), (32 * HID, 2), (1, 1024)])
                            inc(gpsimd.dma_start(
                                out=W3B[t3 % 2][:, jh * 2 * 1024:(jh + 1) * 2 * 1024]
                                .rearrange("p (j c) -> p j c", j=2),
                                in_=src), f"sW3_{t3 % 2}")
                        w3_dma[t3] = cum(f"sW3_{t3 % 2}")
                    G_(f)

                    def f(tensor, qd=qd, t3=t3, rh=rh, r=r, hh=hh, och=och,
                          pf=po_free[0]):
                        wtt(tensor, w3_dma[t3])
                        if rh == 0 and pf:
                            wt(tensor, "dve", pf)
                        for js in range(4):
                            j = r * 8 + hh * 4 + js
                            for th in range(4):
                                for oc in range(2):
                                    inc(tensor.matmul(
                                        PO[th][:, oc * 512:(oc + 1) * 512],
                                        XQ3F(th, j),
                                        W3B[t3 % 2][:, js * 1024 + oc * 512:
                                                    js * 1024 + (oc + 1) * 512],
                                        start=(j == 0), stop=(j == 63)), "pe")
                    T_(f)
                    w3_used[t3] = CNT["pe"]
                pe_w3o = CNT["pe"]

                for th in range(4):
                    def f(vector, th=th, och=och, qd=qd, pw=pe_w3o):
                        wt(vector, "pe", pw)
                        inc(vector.tensor_scalar_mul(
                            FO4[th][:, och * 1024:(och + 1) * 1024], PO[th][:],
                            RC3[:, qd * 4 + th:qd * 4 + th + 1]), "dve")
                    V_(f)
                po_free[0] = CNT["dve"]
            pe_w3 = CNT["pe"]
            PRV["w3pe"] = pe_w3
            dve_fo4 = po_free[0]
            if stop_after == "h_w3" and qd == 0:
                _finish_early()
                esW3.close(); esQ.close(); esH.close()
                return nc

            for th in range(4):
                ti = qd * 4 + th

                def f(sync, ti=ti, g=PRV["ob_last"], q3=q3_done[3]):
                    wtt(sync, cc_rs)
                    wt(sync, "dve", q3)
                    if g:
                        wt(sync, "dve", g)
                    inc(sync.dma_start(
                        out=SBF[:], in_=rs_out.ap()[ti * 128:(ti + 1) * 128, :]),
                        "sFIN")
                    inc(sync.dma_start(out=XT2[:],
                                       in_=x_sl[ti * 128:(ti + 1) * 128, :]), "sFIN")
                S_(f)
                d_fin = cum("sFIN")

                def f(scalar, th=th, dve_fo4=dve_fo4, g=PRV["ob_last"]):
                    wt(scalar, "dve", dve_fo4)
                    if g:
                        wt(scalar, "dve", g)
                    inc(scalar.activation(GT[:], FO4[th][:], AF.Silu, bias=0.0,
                                          scale=1.0),
                        "act")
                A_(f)
                a_g = CNT["act"]

                def f(vector, ti=ti, th=th, a_g=a_g, d_fin=d_fin,
                      g=(out_dma[ti - 2] if ti >= 2 else None)):
                    wt(vector, "act", a_g)
                    wtt(vector, d_fin)
                    if g:
                        wtt(vector, g)
                    inc(vector.tensor_copy(SF32[:], SBF[:]), "dve")
                    inc(vector.tensor_mul(FO4[th][:], GT[:], SF32[:]), "dve")
                    inc(vector.tensor_add(OBUF[ti % 2][:], FO4[th][:], XT2[:]),
                        "dve")
                V_(f)
                ob_last = CNT["dve"]
                PRV["ob_last"] = ob_last

                def f(gpsimd, ti=ti, ou=ob_last):
                    wt(gpsimd, "dve", ou)
                    inc(gpsimd.dma_start(out=y_sl[ti * 128:(ti + 1) * 128, :],
                                         in_=OBUF[ti % 2][:]), f"sOO{ti % 2}")
                    out_dma.append(cum(f"sOO{ti % 2}"))
                G_(f)
            PRV["xq_used"] = pe_w3
            esW3.close()
            esQ.close()
            if stop_after == "h_fin" and qd == 0:
                _finish_early()
                esH.close()
                return nc
        esH.close()

        if dbg2:
            def f(sync):
                for nm in list(SEMS):
                    wt(sync, nm, CNT[nm])
                inc(sync.dma_start(out=dbg["d2_xn"], in_=ag_xn_out.ap()), "oodbg")
                inc(sync.dma_start(out=dbg["d2_s"], in_=rs_out.ap()), "oodbg")
                inc(sync.dma_start(out=dbg["d2_rsin"], in_=rs_in.ap()), "oodbg")
                inc(sync.dma_start(out=dbg["d2_xq"], in_=xqf_d[0].ap()), "oodbg")
                inc(sync.dma_start(out=dbg["d2_tern"],
                                   in_=ag_t_out.ap()[0:768, :]), "oodbg")
            S_(f)

        def f(gpsimd):
            for nm in list(SEMS):
                if nm.startswith("sOO") or nm == "oodbg":
                    wt(gpsimd, nm, CNT[nm])
        G_(f)

    return nc



_CACHED = {}


def _get_nc(debug=False):
    key = bool(debug)
    if key not in _CACHED:
        _CACHED[key] = build(debug=key)
    return _CACHED[key]


def make_in_maps(L0_attn_hidden_states, L0_attn_w_qkv, L0_attn_w_o,
                 L0_ffn_w1, L0_ffn_g1, L0_ffn_w2, L0_ffn_g2, L0_ffn_w3, L0_ffn_g3):
    bf = ml_dtypes.bfloat16
    x = np.ascontiguousarray(np.asarray(L0_attn_hidden_states, np.float32)
                             .reshape(B * S, D))
    wqkvT = np.ascontiguousarray(np.asarray(L0_attn_w_qkv, np.float32).T)
    woT = np.ascontiguousarray(np.asarray(L0_attn_w_o, np.float32).T)
    w1T = np.ascontiguousarray(np.asarray(L0_ffn_w1, np.float32).T)
    w2T = np.ascontiguousarray(np.asarray(L0_ffn_w2, np.float32).T)
    w3T = np.ascontiguousarray(np.asarray(L0_ffn_w3, np.float32).T)

    g1b = np.ascontiguousarray(np.tile(np.asarray(L0_ffn_g1, np.float32)[None, :],
                                       (128, 1)))
    g2b = np.ascontiguousarray(np.tile(np.asarray(L0_ffn_g2, np.float32)[None, :],
                                       (128, 1)))
    g3b = np.ascontiguousarray(np.tile(np.asarray(L0_ffn_g3, np.float32)[None, :],
                                       (128, 1)))
    ident = np.eye(128, dtype=np.float32).astype(bf)
    kk, qq = np.meshgrid(np.arange(128), np.arange(128), indexing="ij")
    maskt = np.where(kk <= qq, 0.0, -1e9).astype(np.float32)
    onec = np.ones((128, 1), np.float32)
    oner = np.ones((1, 128), np.float32)
    onecb = np.ones((128, 1), np.float32).astype(bf)
    onerb = np.ones((1, 128), np.float32).astype(bf)
    onesq = np.ones((128, 128), np.float32).astype(bf)

    in_maps = []
    for c in range(NCORES):
        h0 = 2 * c
        cols = []
        for h in (h0, h0 + 1):
            for off in (0, D, 2 * D):
                cols.append(wqkvT[:, off + h * 128: off + (h + 1) * 128])
        wqkv_sl_a = np.ascontiguousarray(np.concatenate(cols, axis=1))
        wo_sl_a = np.ascontiguousarray(woT[h0 * 128: h0 * 128 + 256, :])
        wf_sl_a = np.ascontiguousarray(np.concatenate([
            w1T[c * 256:(c + 1) * 256, :],
            w2T[c * 256:(c + 1) * 256, :],
            w3T[c * NTOK:(c + 1) * NTOK, :].reshape(256, HID)], axis=0))
        in_maps.append({
            "x_sl": np.ascontiguousarray(x[c * NTOK:(c + 1) * NTOK, :]),
            "wqkv_sl": wqkv_sl_a,
            "wo_sl": wo_sl_a,
            "wf_sl": wf_sl_a,
            "g1b": g1b, "g2b": g2b, "g3b": g3b,
            "ident": ident, "maskt": maskt,
            "onec": onec, "oner": oner, "onecb": onecb, "onerb": onerb,
            "onesq": onesq,
        })
    return in_maps


def kernel(L0_attn_hidden_states, L0_attn_w_qkv, L0_attn_w_o,
           L0_ffn_w1, L0_ffn_g1, L0_ffn_w2, L0_ffn_g2, L0_ffn_w3, L0_ffn_g3):
    in_maps = make_in_maps(L0_attn_hidden_states, L0_attn_w_qkv, L0_attn_w_o,
                           L0_ffn_w1, L0_ffn_g1, L0_ffn_w2, L0_ffn_g2,
                           L0_ffn_w3, L0_ffn_g3)
    nc = _get_nc(debug=False)
    res = run_bass_kernel_spmd(nc, in_maps, core_ids=list(range(NCORES)))
    out = np.concatenate([res.results[c]["y_sl"] for c in range(NCORES)], axis=0)
    return out.reshape(B, S, D)


# version 15
# speedup vs baseline: 1.0351x; 1.0028x over previous
import contextlib
import numpy as np
import ml_dtypes

import concourse.bass as bass
import concourse.mybir as mybir
from concourse.bass_types import AP
from concourse.bass_utils import run_bass_kernel_spmd

F32 = mybir.dt.float32
BF16 = mybir.dt.bfloat16
I8 = mybir.dt.int8
AF = mybir.ActivationFunctionType
ALU = mybir.AluOpType
AX = mybir.AxisListType

NCORES = 8
D = 2048
DC = 16
S = 4096
B = 2
NTOK = 1024
HID = 8192
QSC = 1.0 / float(np.sqrt(np.float32(128.0)))
ATANH_HALF = 0.5493061443340549
MAGIC = 12582912.0
NELW = {1: 8192 * 2048, 2: 8192 * 2048, 3: 2048 * 8192}


def mkap(t, offset, dims):
    base = t.ap()
    return AP(base.tensor, offset, [[st, cn] for (st, cn) in dims])


def build(debug=False, stop_after="full", dbg2=False):
    nc = bass.Bass("TRN2", target_bir_lowering=False,
                   detect_race_conditions=False)

    x_sl = nc.dram_tensor("x_sl", [NTOK, D], F32, kind="ExternalInput").ap()
    wqkv_sl = nc.dram_tensor("wqkv_sl", [D, 768], F32, kind="ExternalInput").ap()
    wo_sl = nc.dram_tensor("wo_sl", [256, D], F32, kind="ExternalInput").ap()
    wf_sl = nc.dram_tensor("wf_sl", [768, HID], F32, kind="ExternalInput").ap()
    g1b_in = nc.dram_tensor("g1b", [128, D], F32, kind="ExternalInput").ap()
    g2b_in = nc.dram_tensor("g2b", [128, D], F32, kind="ExternalInput").ap()
    g3b_in = nc.dram_tensor("g3b", [128, HID], F32, kind="ExternalInput").ap()
    ident_in = nc.dram_tensor("ident", [128, 128], BF16, kind="ExternalInput").ap()
    mask_in = nc.dram_tensor("maskt", [128, 128], F32, kind="ExternalInput").ap()
    onec_in = nc.dram_tensor("onec", [128, 1], F32, kind="ExternalInput").ap()
    oner_in = nc.dram_tensor("oner", [1, 128], F32, kind="ExternalInput").ap()
    onecb_in = nc.dram_tensor("onecb", [128, 1], BF16, kind="ExternalInput").ap()
    onerb_in = nc.dram_tensor("onerb", [1, 128], BF16, kind="ExternalInput").ap()
    onesq_in = nc.dram_tensor("onesq", [128, 128], BF16, kind="ExternalInput").ap()

    y_sl = nc.dram_tensor("y_sl", [NTOK, D], F32, kind="ExternalOutput").ap()

    dbg = {}
    if debug:
        for nm, shp, dt in [("xn", [NTOK, D], F32), ("q0", [128, S], BF16),
                            ("k0", [128, S], BF16), ("v0", [128, S], BF16),
                            ("afm0", [128, 2 * S], BF16), ("s", [NTOK, D], BF16),
                            ("bcs", [128, 16], F32), ("xq1", [128, 8 * D], BF16),
                            ("h0", [128, HID], BF16), ("negm", [128, 1], F32),
                            ("rc", [128, 24], F32)]:
            dbg[nm] = nc.dram_tensor("dbg_" + nm, shp, dt, kind="ExternalOutput").ap()

    if dbg2:
        dbg["d2_xn"] = nc.dram_tensor("d2_xn", [NCORES * D, NTOK], BF16,
                                      kind="ExternalOutput").ap()
        dbg["d2_s"] = nc.dram_tensor("d2_s", [NTOK, D], BF16,
                                     kind="ExternalOutput").ap()
        dbg["d2_rsin"] = nc.dram_tensor("d2_rsin", [B * S, D], BF16,
                                        kind="ExternalOutput").ap()
        dbg["d2_xq"] = nc.dram_tensor("d2_xq", [128, 8 * D], BF16,
                                      kind="ExternalOutput").ap()
        dbg["d2_tern"] = nc.dram_tensor("d2_tern", [768, HID], BF16,
                                        kind="ExternalOutput").ap()
        dbg["d2_ri0"] = nc.dram_tensor("d2_ri0", [B * S, D], BF16,
                                       kind="ExternalOutput").ap()
        dbg["d2_h16"] = nc.dram_tensor("d2_h16", [128, 4 * HID], BF16,
                                       kind="ExternalOutput").ap()
        dbg["d2_rc"] = nc.dram_tensor("d2_rc", [128, 24], F32,
                                      kind="ExternalOutput").ap()
    ag_xn_in = nc.dram_tensor("ag_xn_in", [D, NTOK], BF16)
    ag_xn_out = nc.dram_tensor("ag_xn_out", [NCORES * D, NTOK], BF16)
    ar_in = nc.dram_tensor("ar_in", [128, 4], F32)
    ar_out = nc.dram_tensor("ar_out", [128, 4], F32)
    ag_t_in = nc.dram_tensor("ag_t_in", [768, HID], BF16)
    ag_t_out = nc.dram_tensor("ag_t_out", [NCORES * 768, HID], BF16)
    rs_in = nc.dram_tensor("rs_in", [B * S, D], BF16)
    rs_out = nc.dram_tensor("rs_out", [NTOK, D], BF16)
    xqf_d = [nc.dram_tensor(f"xqf_d{m}", [128, 8 * D], BF16) for m in range(2)]

    import os as _os
    es = contextlib.ExitStack()
    SEMS, CNT = {}, {}
    with es:
        def _sem(name):
            if name not in SEMS:
                SEMS[name] = es.enter_context(nc.semaphore(name))
                CNT[name] = 0
            return SEMS[name]

        for snm in ["pe", "act", "dve"]:
            _sem(snm)

        def inc(instr, name, k=1):
            _sem(name)
            if name in ("dve", "act"):
                return CNT[name]
            step = 1 if (name in ("pe", "pool") or name.startswith("cc")) \
                else 16
            instr.then_inc(SEMS[name], k * step)
            CNT[name] += k * step
            return CNT[name]

        def wt(eng, name, val):
            _sem(name)
            assert val <= CNT[name], f"wait on future {name}: {val} > {CNT[name]}"
            if val > 0:
                eng.wait_ge(SEMS[name], val)

        def wtt(eng, tup):
            if tup:
                wt(eng, tup[0], tup[1])

        def cum(name):
            _sem(name)
            return (name, CNT[name])

        IDN = es.enter_context(nc.sbuf_tensor("IDN", [128, 128], BF16))
        MSK = es.enter_context(nc.sbuf_tensor("MSK", [128, 128], F32))
        ONEC = es.enter_context(nc.sbuf_tensor("ONEC", [128, 1], F32))
        ONER = es.enter_context(nc.sbuf_tensor("ONER", [1, 128], F32))
        ONECB = es.enter_context(nc.sbuf_tensor("ONECB", [128, 1], BF16))
        ONERB = es.enter_context(nc.sbuf_tensor("ONERB", [1, 128], BF16))
        ONESQ = es.enter_context(nc.sbuf_tensor("ONESQ", [128, 128], BF16))
        EPS5 = es.enter_context(nc.sbuf_tensor("EPS5", [128, 1], F32))
        EPS8 = es.enter_context(nc.sbuf_tensor("EPS8", [128, 1], F32))
        BCS = es.enter_context(nc.sbuf_tensor("BCS", [128, 16], F32))
        PWS = es.enter_context(nc.sbuf_tensor("PWS", [128, 4], F32))
        ARS = es.enter_context(nc.sbuf_tensor("ARS", [128, 4], F32))
        AM12 = es.enter_context(nc.sbuf_tensor("AM12", [128, 16], F32))
        RC1 = es.enter_context(nc.sbuf_tensor("RC1", [128, 8], F32))
        RC2 = es.enter_context(nc.sbuf_tensor("RC2", [128, 8], F32))
        RC3 = es.enter_context(nc.sbuf_tensor("RC3", [128, 8], F32))
        TOT = es.enter_context(nc.sbuf_tensor("TOT", [1, 16], F32))
        ROW = es.enter_context(nc.sbuf_tensor("ROW", [1, 16], F32))

        import os as _os2
        _PRE = _os2.environ.get("KPRE", "cc")
        if _PRE == "all":
            for _nm in ["ccA", "ccB", "ccC", "ccD",
                        "sCONST", "sXT0", "sXT1", "sWFT0", "sWFT1", "sAGI",
                        "sARI", "sARO", "sG12", "sXQS", "sTRN0", "sTRN1", "sW2",
                        "sRHS0", "sRHS1", "sRSB0", "sRSB1", "sG3", "sXQQ",
                        "sW12_0", "sW12_1", "sW3_0", "sW3_1", "sFIN", "sOO0",
                        "sOO1", "oodbg"]:
                _sem(_nm)
        elif _PRE == "cc":
            for _nm in ["ccA", "ccB", "ccC", "ccD", "pool"]:
                _sem(_nm)

        block = es.enter_context(nc.Block())

        class Chain:
            def __init__(self, eng, name):
                self._e = eng
                self._n = name

            def wait_ge(self, sem, val):
                return self._e.wait_ge(sem, val)

            def __getattr__(self, attr):
                real = getattr(self._e, attr)
                if not callable(real):
                    return real
                nm = self._n

                def call(*a, **k):
                    if CNT[nm] > 0:
                        self._e.wait_ge(SEMS[nm], CNT[nm])
                    instr = real(*a, **k)
                    if instr is not None and hasattr(instr, "then_inc"):
                        instr.then_inc(SEMS[nm], 1)
                        CNT[nm] += 1
                    return instr
                return call

        def V_(f):
            block.vector(lambda v: f(Chain(v, "dve")))

        def A_(f):
            block.scalar(lambda a: f(Chain(a, "act")))

        S_, T_, G_ = block.sync, block.tensor, block.gpsimd

        def _finish_early():
            def f(gpsimd):
                for nm in list(SEMS):
                    wt(gpsimd, nm, CNT[nm])
            G_(f)


        esXN = contextlib.ExitStack()
        XN = esXN.enter_context(nc.sbuf_tensor("XN", [128, 8 * D], F32))
        esLN = contextlib.ExitStack()
        XT = [esLN.enter_context(nc.sbuf_tensor(f"XT{i}", [128, D], F32))
              for i in range(2)]
        XBF = esLN.enter_context(nc.sbuf_tensor("XBF", [128, D], BF16))
        SCR = esLN.enter_context(nc.sbuf_tensor("SCR1", [128, D], F32))
        ST1 = esLN.enter_context(nc.sbuf_tensor("ST1", [128, 1], F32))
        ST2 = esLN.enter_context(nc.sbuf_tensor("ST2", [128, 2], F32))
        ST3 = esLN.enter_context(nc.sbuf_tensor("ST3", [128, 1], F32))
        ST4 = esLN.enter_context(nc.sbuf_tensor("ST4", [128, 2], F32))
        esTF = contextlib.ExitStack()
        XTF = esTF.enter_context(nc.sbuf_tensor("XTF", [128, DC * NTOK], BF16))

        if int(_os.environ.get("KCONSTS", "1")):
            def f(sync):
                inc(sync.dma_start(out=IDN[:], in_=ident_in), "sCONST")
                inc(sync.dma_start(out=MSK[:], in_=mask_in), "sCONST")
                inc(sync.dma_start(out=ONEC[:], in_=onec_in), "sCONST")
                inc(sync.dma_start(out=ONER[:], in_=oner_in), "sCONST")
                inc(sync.dma_start(out=ONECB[:], in_=onecb_in), "sCONST")
                inc(sync.dma_start(out=ONERB[:], in_=onerb_in), "sCONST")
                inc(sync.dma_start(out=ONESQ[:], in_=onesq_in), "sCONST")
            S_(f)
        d_consts = cum("sCONST")

        def f(vector):
            wtt(vector, d_consts)
            vector.memset(EPS5[:], 1e-5)
            vector.memset(EPS8[:], 1e-8)
            inc(vector.memset(PWS[:], 0.0), "dve")
        V_(f)

        esP = contextlib.ExitStack()
        PTB = esP.enter_context(nc.psum_tensor("PTB1", [128, 128], BF16))
        _NTILES = int(_os.environ.get("KNT", "8"))
        ln_done = [0] * 8
        tr_evac = [0]
        ln_tr_pe = [0]
        xdma = [None] * 8
        for i in range(_NTILES):
            def f(sync, i=i):
                if i >= 2:
                    wt(sync, "dve", ln_done[i - 2])
                inc(sync.dma_start(out=XT[i % 2][:],
                                   in_=x_sl[i * 128:(i + 1) * 128, :]), f"sXT{i % 2}")
                xdma[i] = cum(f"sXT{i % 2}")
            S_(f)

            def f(scalar, i=i):
                wtt(scalar, xdma[i])
                if i > 1:
                    wt(scalar, "dve", ln_done[i - 2])
                inc(scalar.activation(SCR[:], XT[i % 2][:], AF.Square,
                                      bias=0.0, scale=1.0,
                                      accum_out=ST2[:, i % 2:i % 2 + 1]), "act")
            A_(f)
            a_sq = CNT["act"]

            def f(vector, i=i, a_sq=a_sq):
                wtt(vector, xdma[i])
                inc(vector.reduce_sum(ST1[:], XT[i % 2][:], AX.X), "dve")
                inc(vector.tensor_scalar_mul(ST1[:], ST1[:], -1.0 / D), "dve")
                wt(vector, "act", a_sq)
                inc(vector.tensor_mul(ST3[:], ST1[:], ST1[:]), "dve")
                inc(vector.scalar_tensor_tensor(ST3[:], ST2[:, i % 2:i % 2 + 1],
                                                1.0 / D, ST3[:],
                                                ALU.mult, ALU.subtract), "dve")
            V_(f)
            dve_var = CNT["dve"]

            def f(scalar, i=i, dve_var=dve_var):
                wt(scalar, "dve", dve_var)
                inc(scalar.activation(ST4[:, i % 2:i % 2 + 1], ST3[:], AF.Sqrt,
                                      bias=EPS5[:], scale=1.0), "act")
            A_(f)
            a_rstd = CNT["act"]

            def f(vector, i=i, a_rstd=a_rstd, ltp=ln_tr_pe[0]):
                wt(vector, "act", a_rstd)
                inc(vector.reciprocal(ST4[:, i % 2:i % 2 + 1],
                                      ST4[:, i % 2:i % 2 + 1]), "dve")
                inc(vector.tensor_scalar(XN[:, i * D:(i + 1) * D], XT[i % 2][:],
                                         ST1[:], ST4[:, i % 2:i % 2 + 1],
                                         ALU.add, ALU.mult), "dve")
                if ltp:
                    wt(vector, "pe", ltp)
                inc(vector.tensor_copy(XBF[:], XN[:, i * D:(i + 1) * D]), "dve")
            V_(f)
            ln_done[i] = CNT["dve"]

            for ic in range(DC):
                def f(tensor, i=i, ic=ic, g=ln_done[i], te=tr_evac[-1]):
                    wt(tensor, "dve", g)
                    wt(tensor, "dve", te)
                    wtt(tensor, d_consts)
                    inc(tensor.transpose(PTB[:], XBF[:, ic * 128:(ic + 1) * 128],
                                         IDN[:]), "pe")
                T_(f)
                pe_tr = CNT["pe"]

                def f(vector, i=i, ic=ic, pe_tr=pe_tr):
                    wt(vector, "pe", pe_tr)
                    inc(vector.tensor_copy(
                        XTF[:, ic * NTOK + i * 128: ic * NTOK + (i + 1) * 128],
                        PTB[:]), "dve")
                V_(f)
                tr_evac.append(CNT["dve"])
            ln_tr_pe[0] = CNT["pe"]
        act_xtf = CNT["act"]
        dve_xtf = CNT["dve"]

        _kag = int(_os.environ.get("KAG", "1"))
        if _kag:
            def f(gpsimd):
                wt(gpsimd, "dve", dve_xtf)
                if _kag in (1, 2):
                    for part in range(4):
                        inc(gpsimd.dma_start(
                            out=ag_xn_in.ap()[part * 512:(part + 1) * 512, :]
                            .rearrange("(i p) t -> p i t", p=128),
                            in_=XTF[:, part * 4 * NTOK:(part + 1) * 4 * NTOK]
                            .rearrange("p (i t) -> p i t", i=4)), "sAGI")
                    wt(gpsimd, "sAGI", CNT["sAGI"])
                if _kag in (1, 3):
                    inc(gpsimd.collective_compute(
                        "AllGather", ALU.bypass,
                        replica_groups=[list(range(NCORES))],
                        ins=[ag_xn_in.ap().opt()], outs=[ag_xn_out.ap().opt()]),
                        "ccA")
            G_(f)
        cc_agxn = cum("ccA")
        dG_agxn = cum("sAGI")
        esTF.close()
        if stop_after == "ln":
            _finish_early()
            esLN.close(); esXN.close(); esP.close()
            return nc

        if debug:
            def f(sync):
                wt(sync, "dve", ln_done[7])
                inc(sync.dma_start(out=dbg["xn"].rearrange("(i p) c -> p i c", p=128),
                                   in_=XN[:].rearrange("p (i c) -> p i c", i=8)), "oodbg")
            S_(f)

        esG = contextlib.ExitStack()
        G1B = esG.enter_context(nc.sbuf_tensor("G1B", [128, D], F32))
        G2B = esG.enter_context(nc.sbuf_tensor("G2B", [128, D], F32))
        YT = esG.enter_context(nc.sbuf_tensor("YT", [128, D], F32))
        YG = esG.enter_context(nc.sbuf_tensor("YGg", [128, D], F32))
        XQ8 = esG.enter_context(nc.sbuf_tensor("XQ8g", [128, D], F32))
        XQB = esG.enter_context(nc.sbuf_tensor("XQBg", [128, D], BF16))
        XSTG = esG.enter_context(nc.sbuf_tensor("XSTG", [128, D], BF16))
        AMX = esG.enter_context(nc.sbuf_tensor("AMXg", [128, 1], F32))
        SCLg = esG.enter_context(nc.sbuf_tensor("SCLg", [128, 1], F32))
        SSQ = esG.enter_context(nc.sbuf_tensor("SSQg", [128, 2], F32))
        RSTD = esG.enter_context(nc.sbuf_tensor("RSTDg", [128, 2], F32))
        esGP = contextlib.ExitStack()
        PTG = esGP.enter_context(nc.psum_tensor("PTG", [128, 128], BF16))

        def f(sync):
            wtt(sync, dG_agxn)
            inc(sync.dma_start(out=G1B[:], in_=g1b_in), "sG12")
            inc(sync.dma_start(out=G2B[:], in_=g2b_in), "sG12")
        S_(f)
        d_g12 = cum("sG12")

        xstg_dma = [None]
        last_tr_pe = [0]
        prev_dve_y = [0, 0]
        for i in range(8):
            def f(scalar, i=i, pdy=prev_dve_y[i % 2]):
                wt(scalar, "dve", ln_done[i])
                if i == 0:
                    wtt(scalar, d_g12)
                if pdy:
                    wt(scalar, "dve", pdy)
                inc(scalar.activation(SCR[:], XN[:, i * D:(i + 1) * D], AF.Square,
                                      bias=0.0, scale=1.0,
                                      accum_out=SSQ[:, i % 2:i % 2 + 1]), "act")
                inc(scalar.activation(RSTD[:, i % 2:i % 2 + 1],
                                      SSQ[:, i % 2:i % 2 + 1], AF.Sqrt,
                                      bias=EPS8[:], scale=1.0 / D), "act")
            A_(f)
            a_rst = CNT["act"]

            def f(vector, i=i, a_rst=a_rst):
                wt(vector, "act", a_rst)
                inc(vector.reciprocal(RSTD[:, i % 2:i % 2 + 1],
                                      RSTD[:, i % 2:i % 2 + 1]), "dve")
                inc(vector.tensor_scalar_mul(YT[:], XN[:, i * D:(i + 1) * D],
                                             RSTD[:, i % 2:i % 2 + 1]),
                    "dve")
            V_(f)
            dve_y = CNT["dve"]
            prev_dve_y[i % 2] = dve_y

            for mi, GB in enumerate((G1B, G2B)):
                def f(vector, mi=mi, GB=GB, i=i, dve_y=dve_y):
                    wt(vector, "dve", dve_y)
                    inc(vector.tensor_mul(YG[:], YT[:], GB[:]), "dve")
                    inc(vector.reduce_max(AMX[:], YG[:], AX.X,
                                          apply_absolute_value=True), "dve")
                    inc(vector.tensor_scalar_max(AMX[:], AMX[:], 1e-4), "dve")
                    inc(vector.tensor_copy(AM12[:, mi * 8 + i: mi * 8 + i + 1],
                                           AMX[:]), "dve")
                V_(f)
                dve_am = CNT["dve"]

                def f(vector, dve_am=dve_am, ltr=last_tr_pe[0]):
                    wt(vector, "dve", dve_am)
                    if ltr:
                        wt(vector, "pe", ltr)
                    inc(vector.reciprocal(SCLg[:], AMX[:]), "dve")
                    inc(vector.tensor_scalar(XQ8[:], YG[:], SCLg[:], 127.0,
                                             ALU.mult, ALU.mult), "dve")
                    inc(vector.tensor_scalar(YG[:], XQ8[:], MAGIC, MAGIC,
                                             ALU.add, ALU.subtract), "dve")
                    inc(vector.tensor_scalar(XQB[:], YG[:], 127.0, -128.0,
                                             ALU.min, ALU.max), "dve")
                V_(f)
                dve_xq = CNT["dve"]

                stg_evac = [0]
                for ic in range(DC):
                    def f(tensor, ic=ic, dve_xq=dve_xq, se=stg_evac[-1],
                          xd=xstg_dma[-1]):
                        wt(tensor, "dve", dve_xq)
                        wt(tensor, "dve", se)
                        if ic == 0:
                            wtt(tensor, xd)
                        inc(tensor.transpose(PTG[:], XQB[:, ic * 128:(ic + 1) * 128],
                                             IDN[:]), "pe")
                    T_(f)
                    pe_t = CNT["pe"]

                    def f(vector, ic=ic, pe_t=pe_t):
                        wt(vector, "pe", pe_t)
                        inc(vector.tensor_copy(XSTG[:, ic * 128:(ic + 1) * 128],
                                               PTG[:]), "dve")
                    V_(f)
                    stg_evac.append(CNT["dve"])
                last_tr_pe[0] = CNT["pe"]

                def f(sync, mi=mi, i=i, se=stg_evac[-1]):
                    wt(sync, "dve", se)
                    inc(sync.dma_start(out=xqf_d[mi].ap()[:, i * D:(i + 1) * D],
                                       in_=XSTG[:]), "sXQS")
                    xstg_dma.append(cum("sXQS"))
                S_(f)
        act_xqf = CNT["act"]
        dve_g_end = CNT["dve"]
        dS_xqf = cum("sXQS")

        if debug:
            def f(sync):
                wtt(sync, dS_xqf)
                inc(sync.dma_start(out=dbg["xq1"], in_=xqf_d[0].ap()), "oodbg")
                wt(sync, "dve", CNT["dve"])
                inc(sync.dma_start(out=dbg["rc"][:, 0:16], in_=AM12[:]), "oodbg")
            S_(f)
        esGP.close()
        esG.close()
        if stop_after == "g":
            _finish_early()
            esLN.close(); esXN.close(); esP.close()
            return nc

        esP.close()
        esLN.close()
        esXN.close()

        es2 = contextlib.ExitStack()
        WFT = [es2.enter_context(nc.sbuf_tensor(f"WFT{i}", [128, 2048], F32))
               for i in range(2)]
        SG1 = es2.enter_context(nc.sbuf_tensor("SG1", [128, 2048], BF16))
        SG2 = es2.enter_context(nc.sbuf_tensor("SG2", [128, 2048], BF16))
        TRN = [es2.enter_context(nc.sbuf_tensor(f"TRN{i}", [128, 2048], BF16))
               for i in range(2)]
        ST3C = es2.enter_context(nc.sbuf_tensor("ST3C", [128, 1], F32))
        WQB = es2.enter_context(nc.sbuf_tensor("WQB", [128, DC * 768], BF16))
        WOB = es2.enter_context(nc.sbuf_tensor("WOB", [128, 2 * D], BF16))
        esWQ = contextlib.ExitStack()
        WQF = esWQ.enter_context(nc.sbuf_tensor("WQF", [128, DC * 768], F32))
        WOF = esWQ.enter_context(nc.sbuf_tensor("WOF", [128, 2 * D], F32))

        def f(sync):
            wt(sync, "act", act_xqf)
            wt(sync, "dve", dve_g_end)
            wtt(sync, dG_agxn)
            wtt(sync, dS_xqf)
            for part in range(4):
                inc(sync.dma_start(
                    out=WQF[:, part * 4 * 768:(part + 1) * 4 * 768]
                    .rearrange("p (i c) -> p i c", i=4),
                    in_=wqkv_sl[part * 512:(part + 1) * 512, :]
                    .rearrange("(i p) c -> p i c", p=128)), "sW2")
            inc(sync.dma_start(out=WOF[:].rearrange("p (i c) -> p i c", i=2),
                               in_=wo_sl.rearrange("(i p) c -> p i c", p=128)), "sW2")
        S_(f)
        d_w2 = cum("sW2")

        def f(vector):
            wtt(vector, d_w2)
            inc(vector.tensor_copy(WQB[:], WQF[:]), "dve")
            inc(vector.tensor_copy(WOB[:], WOF[:]), "dve")
        V_(f)
        dve_wcast = CNT["dve"]
        esWQ.close()

        QF = [es2.enter_context(nc.sbuf_tensor(f"QF{h}", [128, S], BF16))
              for h in range(2)]
        KF = [es2.enter_context(nc.sbuf_tensor(f"KF{h}", [128, S], BF16))
              for h in range(2)]
        VF = [es2.enter_context(nc.sbuf_tensor(f"VF{h}", [128, S], BF16))
              for h in range(2)]
        RHS = [es2.enter_context(nc.sbuf_tensor(f"RHS{i}", [128, DC * 512], BF16))
               for i in range(2)]
        AFM = es2.enter_context(nc.sbuf_tensor("AFM", [128, 2 * S], BF16))
        VT = es2.enter_context(nc.sbuf_tensor("VT", [128, 32 * 129], BF16))
        SQT = es2.enter_context(nc.sbuf_tensor("SQT", [128, S], BF16))
        MROW = es2.enter_context(nc.sbuf_tensor("MROW", [128, 32], F32))
        MCH = es2.enter_context(nc.sbuf_tensor("MCH", [128, 9], F32))
        MNEG = es2.enter_context(nc.sbuf_tensor("MNEG", [128, 32], F32))
        MTMP = es2.enter_context(nc.sbuf_tensor("MTMP", [128, 1], F32))
        DB = es2.enter_context(nc.sbuf_tensor("DB", [128, 512], BF16))
        ATM = [es2.enter_context(nc.sbuf_tensor(f"ATM{i}", [128, 128], BF16))
               for i in range(2)]
        RCT = [es2.enter_context(nc.sbuf_tensor(f"RCT{i}", [128, 1], F32))
               for i in range(2)]
        PRB = [es2.enter_context(nc.sbuf_tensor(f"PRB{i}", [128, 512], BF16))
               for i in range(3)]
        RSB = [es2.enter_context(nc.sbuf_tensor(f"RSB{i}", [128, D], BF16))
               for i in range(2)]
        NRM = es2.enter_context(nc.sbuf_tensor("NRM", [1, 4], F32))

        def qkv_phase(b, psum_act=None, psum_dve=0):
            esQ = contextlib.ExitStack()
            act_pre = CNT["act"] if psum_act is None else psum_act
            dve_pre0 = psum_dve
            PQ = [esQ.enter_context(nc.psum_tensor(f"PQ{b}_{ob}", [128, 512], F32))
                  for ob in range(6)]
            evac_acts = [0] * 6
            rhs_free = [0] * 8
            rhs_dma = [None] * 8
            for tg in range(8):
                def f(sync, tg=tg, b=b, ap_=act_pre):
                    if tg == 0:
                        wtt(sync, cc_agxn)
                        wt(sync, "dve", dve_wcast)
                    if tg < 2 and b == 1:
                        wt(sync, "act", ap_)
                    if tg >= 2:
                        wt(sync, "act", rhs_free[tg - 2])
                    r = 4 * b + tg // 2
                    c0 = (tg % 2) * 512
                    for part in range(4):
                        src = ag_xn_out.ap()[r * D + part * 512:
                                             r * D + (part + 1) * 512, c0:c0 + 512]
                        inc(sync.dma_start(
                            out=RHS[tg % 2][:, part * 4 * 512:(part + 1) * 4 * 512]
                            .rearrange("p (i t) -> p i t", i=4),
                            in_=src.rearrange("(i p) t -> p i t", p=128)),
                            f"sRHS{tg % 2}")
                    rhs_dma[tg] = cum(f"sRHS{tg % 2}")
                S_(f)

                def f(tensor, tg=tg, ea=list(evac_acts), ap_=act_pre,
                      dp_=dve_pre0):
                    wtt(tensor, rhs_dma[tg])
                    if tg == 0:
                        wt(tensor, "act", ap_)
                        if dp_:
                            wt(tensor, "dve", dp_)
                    for ob in range(6):
                        if ea[ob]:
                            wt(tensor, "act", ea[ob])
                        for ic in range(DC):
                            inc(tensor.matmul(
                                PQ[ob][:],
                                WQB[:, ic * 768 + ob * 128: ic * 768 + (ob + 1) * 128],
                                RHS[tg % 2][:, ic * 512:(ic + 1) * 512],
                                start=(ic == 0), stop=(ic == DC - 1)), "pe")
                T_(f)
                pe_tg = CNT["pe"]

                def f(scalar, tg=tg, pe_tg=pe_tg):
                    wt(scalar, "pe", pe_tg)
                    dsts = [(QF[0], QSC), (KF[0], None), (VF[0], None),
                            (QF[1], QSC), (KF[1], None), (VF[1], None)]
                    for ob, (dstt, sc) in enumerate(dsts):
                        dst = dstt[:, tg * 512:(tg + 1) * 512]
                        if sc is not None:
                            inc(scalar.activation(dst, PQ[ob][:], AF.Copy,
                                                  bias=0.0, scale=sc), "act")
                        else:
                            inc(scalar.copy(dst, PQ[ob][:]), "act")
                        evac_acts[ob] = CNT["act"]
                    rhs_free[tg] = CNT["act"]
                A_(f)
            esQ.close()
            return CNT["act"]

        PRO = {}
        tiles_c = []
        for m in (1, 2, 3):
            r0 = {1: 0, 2: 256, 3: 512}[m]
            for rr in range(r0, r0 + 256, 128):
                for cc_ in range(0, HID, 2048):
                    tiles_c.append((m, rr, cc_))

        def emit_c1():
            c1_red = [0] * len(tiles_c)
            wf1_dma = [None] * len(tiles_c)
            for t, (m, rr, cc_) in enumerate(tiles_c):
                def f(sync, t=t, rr=rr, cc_=cc_):
                    if t >= 2:
                        wt(sync, "dve", c1_red[t - 2])
                    inc(sync.dma_start(out=WFT[t % 2][:],
                                       in_=wf_sl[rr:rr + 128, cc_:cc_ + 2048]),
                        f"sWFT{t % 2}")
                    wf1_dma[t] = cum(f"sWFT{t % 2}")
                S_(f)

                def f(vector, t=t, m=m):
                    wtt(vector, wf1_dma[t])
                    inc(vector.reduce_sum(ST3C[:], WFT[t % 2][:], AX.X,
                                          apply_absolute_value=True), "dve")
                    inc(vector.tensor_add(PWS[:, m - 1:m], PWS[:, m - 1:m],
                                          ST3C[:]), "dve")
                    c1_red[t] = CNT["dve"]
                V_(f)
            PRO["dve_pw"] = CNT["dve"]
            PRO["c1_red"] = c1_red

        def emit_ar():
            def f(gpsimd, dve_pw=PRO["dve_pw"]):
                wt(gpsimd, "dve", dve_pw)
                inc(gpsimd.dma_start(out=ar_in.ap(), in_=PWS[:]), "sARI")
                wt(gpsimd, "sARI", CNT["sARI"])
                inc(gpsimd.collective_compute(
                    "AllReduce", ALU.add, replica_groups=[list(range(NCORES))],
                    ins=[ar_in.ap().opt()], outs=[ar_out.ap().opt()]), "ccB")
                wt(gpsimd, "ccB", CNT["ccB"])
                inc(gpsimd.dma_start(out=ARS[:], in_=ar_out.ap()), "sARO")
            G_(f)
            PRO["dG_ars"] = cum("sARO")

        def emit_c2(psum_act):
            esC = contextlib.ExitStack()
            PT1 = esC.enter_context(nc.psum_tensor("PT1", [1, 4], F32))
            PT2 = esC.enter_context(nc.psum_tensor("PT2", [128, 16], F32))

            def f(tensor, pa=psum_act):
                wtt(tensor, PRO["dG_ars"])
                wt(tensor, "act", pa)
                inc(tensor.matmul(PT1[:], ONEC[:], ARS[:], start=True, stop=True),
                    "pe")
            T_(f)
            pe_tot = CNT["pe"]

            def f(vector, pe_tot=pe_tot):
                wt(vector, "pe", pe_tot)
                vector.memset(ROW[:], 0.0)
                vector.memset(TOT[:], 0.0)
                inc(vector.tensor_copy(TOT[0:1, 0:4], PT1[:]), "dve")
                for m in (1, 2, 3):
                    inc(vector.tensor_scalar_mul(TOT[0:1, m - 1:m],
                                                 TOT[0:1, m - 1:m],
                                                 1.0 / NELW[m]), "dve")
                inc(vector.tensor_scalar(ROW[0:1, 0:3], TOT[0:1, 0:3], 1e-8,
                                         ATANH_HALF, ALU.add, ALU.mult), "dve")
                inc(vector.tensor_scalar_mul(ROW[0:1, 3:6], ROW[0:1, 0:3], -1.0),
                    "dve")
                inc(vector.tensor_scalar_add(ROW[0:1, 9:12], TOT[0:1, 0:3], 1.0),
                    "dve")
                inc(vector.tensor_scalar(ROW[0:1, 12:15], TOT[0:1, 0:3], -1.0, 1.0,
                                         ALU.mult, ALU.add), "dve")
                inc(vector.reciprocal(ROW[0:1, 12:15], ROW[0:1, 12:15]), "dve")
                inc(vector.tensor_mul(ROW[0:1, 9:12], ROW[0:1, 9:12],
                                      ROW[0:1, 12:15]), "dve")
            V_(f)
            dve_row = CNT["dve"]

            def f(scalar, dve_row=dve_row):
                wt(scalar, "dve", dve_row)
                inc(scalar.activation(ROW[0:1, 6:9], ROW[0:1, 9:12], AF.Ln,
                                      bias=0.0, scale=1.0), "act")
            A_(f)
            a_ln = CNT["act"]

            def f(vector, a_ln=a_ln):
                wt(vector, "act", a_ln)
                inc(vector.tensor_scalar_mul(ROW[0:1, 6:9], ROW[0:1, 6:9],
                                             0.25 / 127.0), "dve")
            V_(f)
            dve_row2 = CNT["dve"]

            def f(tensor, dve_row2=dve_row2):
                wt(tensor, "dve", dve_row2)
                inc(tensor.matmul(PT2[:], ONER[:], ROW[0:1, :], start=True,
                                  stop=True), "pe")
            T_(f)
            pe_bcs = CNT["pe"]

            def f(vector, pe_bcs=pe_bcs):
                wt(vector, "pe", pe_bcs)
                inc(vector.tensor_copy(BCS[:], PT2[:]), "dve")
                inc(vector.tensor_scalar_mul(RC1[:], AM12[:, 0:8], BCS[:, 6:7]),
                    "dve")
                inc(vector.tensor_scalar_mul(RC2[:], AM12[:, 8:16], BCS[:, 7:8]),
                    "dve")
            V_(f)
            PRO["dve_bcs"] = CNT["dve"]
            esC.close()

        c2b_state = {"wf2": [None] * len(tiles_c), "sgn": [0] * len(tiles_c),
                     "add": [0] * len(tiles_c), "tdma": [None] * len(tiles_c),
                     "next": 0, "ag": False}

        def _emit_c2b_tile(t):
            m, rr, cc_ = tiles_c[t]
            st = c2b_state

            def f(sync, t=t, rr=rr, cc_=cc_):
                if t == 0:
                    wt(sync, "dve", PRO["dve_bcs"])
                if t >= 2:
                    wt(sync, "act", st["sgn"][t - 2])
                inc(sync.dma_start(out=WFT[t % 2][:],
                                   in_=wf_sl[rr:rr + 128, cc_:cc_ + 2048]),
                    f"sWFT{t % 2}")
                st["wf2"][t] = cum(f"sWFT{t % 2}")
            S_(f)

            def f(scalar, t=t, m=m):
                wtt(scalar, st["wf2"][t])
                if t >= 2:
                    wt(scalar, "dve", st["add"][t - 2])
                elif t == 0:
                    wt(scalar, "dve", PRO["dve_bcs"])
                inc(scalar.activation(SG1[:], WFT[t % 2][:], AF.Sign,
                                      bias=BCS[:, 2 + m:3 + m], scale=1.0), "act")
                inc(scalar.activation(SG2[:], WFT[t % 2][:], AF.Sign,
                                      bias=BCS[:, m - 1:m], scale=1.0), "act")
                st["sgn"][t] = CNT["act"]
            A_(f)

            def f(vector, t=t):
                wt(vector, "act", st["sgn"][t])
                if t >= 2:
                    wtt(vector, st["tdma"][t - 2])
                inc(vector.tensor_add(TRN[t % 2][:], SG1[:], SG2[:]), "dve")
                st["add"][t] = CNT["dve"]
            V_(f)

            def f(gpsimd, t=t, rr=rr, cc_=cc_):
                wt(gpsimd, "dve", st["add"][t])
                inc(gpsimd.dma_start(out=ag_t_in.ap()[rr:rr + 128, cc_:cc_ + 2048],
                                     in_=TRN[t % 2][:]), f"sTRN{t % 2}")
                st["tdma"][t] = cum(f"sTRN{t % 2}")
            G_(f)

        def _emit_ag_tern():
            st = c2b_state

            def f(gpsimd):
                wtt(gpsimd, st["tdma"][-1])
                wtt(gpsimd, st["tdma"][-2])
                inc(gpsimd.collective_compute(
                    "AllGather", ALU.bypass, replica_groups=[list(range(NCORES))],
                    ins=[ag_t_in.ap().opt()], outs=[ag_t_out.ap().opt()]), "ccC")
            G_(f)
            PRO["cc_tern"] = cum("ccC")
            st["ag"] = True

        def c2b_filler(n=4):
            st = c2b_state
            if "dve_bcs" not in PRO:
                return
            while n > 0 and st["next"] < len(tiles_c):
                _emit_c2b_tile(st["next"])
                st["next"] += 1
                n -= 1
            if st["next"] == len(tiles_c) and not st["ag"]:
                _emit_ag_tern()

        def c2b_drain():
            c2b_filler(n=len(tiles_c))

        prev_sco_pe = [0]

        def attn_phase(b, qkv_done, filler=None):
            last_afm = 0
            for h in range(2):
                dve0 = CNT["dve"]
                act0 = qkv_done
                esA = contextlib.ExitStack()
                PTBv = esA.enter_context(nc.psum_tensor(f"PTv{b}{h}", [128, 128], BF16))
                PN = esA.enter_context(nc.psum_tensor(f"PN{b}{h}", [1, 512], F32))

                def f(vector, psp=prev_sco_pe[0]):
                    if psp:
                        wt(vector, "pe", psp)
                    inc(vector.memset(
                        VT[:].rearrange("p (k c) -> p k c", c=129)[:, :, 128:129],
                        1.0), "dve")
                V_(f)
                v_evac = [CNT["dve"]]
                for ki in range(32):
                    def f(tensor, ki=ki, h=h, act0=act0, ve=v_evac[-1]):
                        wt(tensor, "act", act0)
                        wt(tensor, "dve", ve)
                        inc(tensor.transpose(PTBv[:],
                                             VF[h][:, ki * 128:(ki + 1) * 128],
                                             IDN[:]), "pe")
                    T_(f)
                    pe_v = CNT["pe"]

                    def f(vector, ki=ki, pe_v=pe_v):
                        wt(vector, "pe", pe_v)
                        inc(vector.tensor_copy(VT[:, ki * 129:ki * 129 + 128],
                                               PTBv[:]), "dve")
                    V_(f)
                    v_evac.append(CNT["dve"])

                PMX = [esA.enter_context(
                    nc.psum_tensor(f"PMX{b}{h}{i}", [128, 512], F32))
                    for i in range(2)]
                mx_red = {0: 0, 1: 0}
                for qt in range(32):
                    nfull = qt // 4
                    plen = (qt % 4) * 128 + 128
                    chunks = [(kc * 512, 512, False) for kc in range(nfull)]
                    chunks.append((nfull * 512, plen, True))
                    for idx, (k0, klen, partial) in enumerate(chunks):
                        par = idx % 2

                        def f(tensor, qt=qt, k0=k0, klen=klen, par=par, h=h,
                              act0=act0, mr=mx_red[par], dve0=dve0):
                            wt(tensor, "act", act0)
                            wt(tensor, "dve", mr if mr else dve0)
                            inc(tensor.matmul(
                                PMX[par][:, 0:klen],
                                QF[h][:, qt * 128:(qt + 1) * 128],
                                KF[h][:, k0:k0 + klen],
                                start=True, stop=True), "pe")
                        T_(f)
                        pe_mx = CNT["pe"]

                        def f(vector, qt=qt, klen=klen, par=par, idx=idx,
                              partial=partial, pe_mx=pe_mx, nch=len(chunks)):
                            wt(vector, "pe", pe_mx)
                            if partial:
                                inc(vector.tensor_add(
                                    PMX[par][:, klen - 128:klen],
                                    PMX[par][:, klen - 128:klen], MSK[:]), "dve")
                            inc(vector.reduce_max(MCH[:, idx:idx + 1],
                                                  PMX[par][:, 0:klen], AX.X),
                                "dve")
                            mx_red[par] = CNT["dve"]
                            if idx == nch - 1:
                                inc(vector.reduce_max(MROW[:, qt:qt + 1],
                                                      MCH[:, 0:nch], AX.X), "dve")
                        V_(f)

                def f(vector):
                    inc(vector.tensor_scalar(MNEG[:], MROW[:], -1.0, -2.0,
                                             ALU.mult, ALU.add), "dve")
                V_(f)
                dve_negm = CNT["dve"]
                esA.close()


                esA2 = contextlib.ExitStack()
                PSS = [esA2.enter_context(
                    nc.psum_tensor(f"PS{b}{h}{i}", [128, 512], F32))
                    for i in range(3)]
                PSA = [esA2.enter_context(
                    nc.psum_tensor(f"PA{b}{h}{i}", [128, 129], F32))
                    for i in range(4)]
                PTR = esA2.enter_context(nc.psum_tensor(f"PR{b}{h}", [128, 128], BF16))

                exp_last = {0: 0, 1: 0, 2: 0}
                pv_last = {0: 0, 1: 0, 2: 0}
                resc_dve = {}
                tr_done = {0: prev_sco_pe[0], 1: prev_sco_pe[0]}
                afm_acts = [0]
                db_pe = [prev_sco_pe[0]]
                prev_resc0 = CNT["dve"]
                for qg in range(8):
                    def f(vector, qg=qg, dn=dve_negm, dp=db_pe[0]):
                        wt(vector, "dve", dn)
                        if dp:
                            wt(vector, "pe", dp)
                        for j in range(4):
                            qt = 4 * qg + j
                            inc(vector.tensor_scalar_mul(
                                DB[:, j * 128:(j + 1) * 128], IDN[:],
                                MNEG[:, qt:qt + 1]), "dve")
                    V_(f)
                    dve_db = CNT["dve"]

                    nki = 4 * qg + 4
                    for ki in range(nki):
                        par = ki % 3
                        diag = ki - 4 * qg

                        def f(tensor, qg=qg, ki=ki, par=par, h=h,
                              dve_db=dve_db, vl=v_evac[-1], el=exp_last[par]):
                            if el:
                                wt(tensor, "act", el)
                            else:
                                wt(tensor, "dve", vl)
                            wt(tensor, "dve", dve_db)
                            inc(tensor.matmul(PSS[par][:],
                                              KF[h][:, ki * 128:(ki + 1) * 128],
                                              QF[h][:, qg * 512:(qg + 1) * 512],
                                              start=True, stop=False), "pe")
                            inc(tensor.matmul(PSS[par][:], ONESQ[:], DB[:],
                                              start=False, stop=True), "pe")
                        T_(f)
                        pe_sco = CNT["pe"]

                        dve_pre = 0
                        if diag >= 0:
                            def f(vector, par=par, diag=diag, pe_sco=pe_sco,
                                  el=exp_last[par], pl=pv_last[par]):
                                wt(vector, "pe", pe_sco)
                                if el:
                                    wt(vector, "act", el)
                                if pl:
                                    wt(vector, "pe", pl)
                                inc(vector.tensor_add(
                                    PSS[par][:, diag * 128:(diag + 1) * 128],
                                    PSS[par][:, diag * 128:(diag + 1) * 128],
                                    MSK[:]), "dve")
                            V_(f)
                            dve_pre = CNT["dve"]

                        def f(scalar, par=par, diag=diag, pe_sco=pe_sco,
                              dve_pre=dve_pre, pl=pv_last[par]):
                            wt(scalar, "pe", pe_sco)
                            if dve_pre:
                                wt(scalar, "dve", dve_pre)
                            if pl:
                                wt(scalar, "pe", pl)
                            vf = max(0, diag) * 128
                            inc(scalar.activation(PRB[par][:, vf:512],
                                                  PSS[par][:, vf:512],
                                                  AF.Exp, bias=0.0, scale=1.0),
                                "act")
                        A_(f)
                        exp_last[par] = CNT["act"]

                        def f(tensor, qg=qg, ki=ki, par=par, diag=diag,
                              ea=exp_last[par], pr0=prev_resc0):
                            wt(tensor, "act", ea)
                            for qi in range(max(0, diag), 4):
                                if ki == 0:
                                    wt(tensor, "dve", resc_dve.get(qi, pr0))
                                inc(tensor.matmul(
                                    PSA[qi][:],
                                    PRB[par][:, qi * 128:(qi + 1) * 128],
                                    VT[:, ki * 129:(ki + 1) * 129],
                                    start=(ki == 0),
                                    stop=(ki == 4 * qg + qi)), "pe")
                        T_(f)
                        pv_last[par] = CNT["pe"]
                        pv_done_ki = CNT["pe"]
                        db_pe[0] = pe_sco

                        if diag >= 0:
                            qi = diag
                            qt = 4 * qg + qi
                            apar = qt % 2

                            def f(vector, qi=qi, apar=apar, pvd=pv_done_ki,
                                  td=tr_done.get(apar, 0)):
                                wt(vector, "pe", pvd)
                                if td:
                                    wt(vector, "pe", td)
                                inc(vector.reciprocal(RCT[apar][:],
                                                      PSA[qi][:, 128:129]), "dve")
                                inc(vector.tensor_scalar_mul(ATM[apar][:],
                                                             PSA[qi][:, 0:128],
                                                             RCT[apar][:]), "dve")
                            V_(f)
                            resc_dve[qi] = CNT["dve"]

                            def f(tensor, qi=qi, apar=apar, rd=resc_dve[qi],
                                  aa=afm_acts[-1]):
                                wt(tensor, "dve", rd)
                                wt(tensor, "dve", aa)
                                inc(tensor.transpose(PTR[:], ATM[apar][:], IDN[:]),
                                    "pe")
                            T_(f)
                            tr_done[apar] = CNT["pe"]

                            def f(vector, qt=qt, h=h, td=tr_done[apar]):
                                wt(vector, "pe", td)
                                inc(vector.tensor_copy(
                                    AFM[:, h * S + qt * 128: h * S + (qt + 1) * 128],
                                    PTR[:]), "dve")
                            V_(f)
                            afm_acts.append(CNT["dve"])
                    if filler is not None:
                        filler()
                esA2.close()
                prev_sco_pe[0] = CNT["pe"]
                last_afm = afm_acts[-1]
            return last_afm

        def wo_phase(b, afm_done, prev_rs=None):
            esW = contextlib.ExitStack()
            PWO = [esW.enter_context(nc.psum_tensor(f"PW{b}{i}", [128, D], F32))
                   for i in range(2)]
            evac_for = [0] * 32
            rs_dma = [None] * 32
            for tt in range(32):
                buf = tt % 2

                def f(tensor, tt=tt, buf=buf, afm_done=afm_done,
                      ef=(evac_for[tt - 2] if tt >= 2 else 0)):
                    wt(tensor, "dve", afm_done)
                    if ef:
                        wt(tensor, "dve", ef)
                    for hc in range(2):
                        for oc in range(4):
                            inc(tensor.matmul(
                                PWO[buf][:, oc * 512:(oc + 1) * 512],
                                AFM[:, hc * S + tt * 128: hc * S + (tt + 1) * 128],
                                WOB[:, hc * D + oc * 512: hc * D + (oc + 1) * 512],
                                start=(hc == 0), stop=(hc == 1)), "pe")
                T_(f)
                pe_wo = CNT["pe"]

                def f(vector, tt=tt, buf=buf, pe_wo=pe_wo, prev_rs=prev_rs,
                      rg=(rs_dma[tt - 2] if tt >= 2 else None)):
                    wt(vector, "pe", pe_wo)
                    if rg:
                        wtt(vector, rg)
                    elif prev_rs:
                        wtt(vector, prev_rs[buf])
                    inc(vector.tensor_copy(RSB[buf][:], PWO[buf][:]), "dve")
                    evac_for[tt] = CNT["dve"]
                V_(f)

                def f(gpsimd, tt=tt, buf=buf, b=b):
                    wt(gpsimd, "dve", evac_for[tt])
                    inc(gpsimd.dma_start(
                        out=rs_in.ap()[b * S + tt * 128: b * S + (tt + 1) * 128, :],
                        in_=RSB[buf][:]), f"sRSB{buf}")
                    rs_dma[tt] = cum(f"sRSB{buf}")
                G_(f)
            esW.close()
            PRO["wo_evac_dve"] = CNT["dve"]
            return (rs_dma[30], rs_dma[31])

        act_qkv0 = qkv_phase(0, psum_act=act_xqf, psum_dve=dve_g_end)
        emit_c1()
        emit_ar()
        emit_c2(act_qkv0)
        if stop_after == "qkv0":
            _finish_early()
            es2.close()
            return nc
        if debug:
            def f(sync):
                wt(sync, "act", act_qkv0)
                inc(sync.dma_start(out=dbg["q0"], in_=QF[0][:]), "oodbg")
                inc(sync.dma_start(out=dbg["k0"], in_=KF[0][:]), "oodbg")
                inc(sync.dma_start(out=dbg["v0"], in_=VF[0][:]), "oodbg")
            S_(f)
        afm0 = attn_phase(0, act_qkv0, filler=c2b_filler)
        if stop_after == "attn0":
            _finish_early()
            es2.close()
            return nc
        if debug:
            def f(sync, afm0=afm0):
                wt(sync, "dve", afm0)
                inc(sync.dma_start(out=dbg["afm0"], in_=AFM[:]), "oodbg")
            S_(f)
        wo0 = wo_phase(0, afm0)
        if stop_after == "wo0":
            _finish_early()
            es2.close()
            return nc
        act_qkv1 = qkv_phase(1, psum_dve=PRO["wo_evac_dve"])
        if stop_after == "qkv1":
            _finish_early()
            es2.close()
            return nc
        afm1 = attn_phase(1, act_qkv1, filler=c2b_filler)
        if stop_after == "attn1":
            _finish_early()
            es2.close()
            return nc
        c2b_drain()
        wo1 = wo_phase(1, afm1, prev_rs=wo0)
        stage2_act = CNT["act"]
        cc_tern = PRO["cc_tern"]
        if stop_after == "wo1":
            _finish_early()
            es2.close()
            return nc

        def f(gpsimd, wo1=wo1):
            wtt(gpsimd, wo1[0])
            wtt(gpsimd, wo1[1])
            if dbg2:
                inc(gpsimd.dma_start(out=dbg["d2_ri0"], in_=rs_in.ap()), "oodbg")
                wt(gpsimd, "oodbg", CNT["oodbg"])
            inc(gpsimd.collective_compute(
                "ReduceScatter", ALU.add, replica_groups=[list(range(NCORES))],
                ins=[rs_in.ap().opt()], outs=[rs_out.ap().opt()]), "ccD")
        G_(f)
        cc_rs = cum("ccD")
        if stop_after == "rs":
            _finish_early()
            es2.close()
            return nc

        if debug:
            def f(sync):
                wtt(sync, cc_rs)
                inc(sync.dma_start(out=dbg["s"], in_=rs_out.ap()), "oodbg")
            S_(f)

        es2.close()

        esH = contextlib.ExitStack()
        G3B = esH.enter_context(nc.sbuf_tensor("G3B", [128, HID], F32))

        def f(sync):
            wt(sync, "act", stage2_act)
            wtt(sync, wo1[0])
            wtt(sync, wo1[1])
            inc(sync.dma_start(out=G3B[:], in_=g3b_in), "sG3")
        S_(f)
        d_g3 = cum("sG3")

        def w12_src_ap(m, parity, hc, uh):
            m_off = 0 if m == 0 else 256
            off = (m_off + parity * 128 + uh * 4 * 768) * HID + hc * 512
            return mkap(ag_t_out, off, [(HID, 128), (768 * HID, 4), (1, 512)])

        out_dma = []
        PRV = {"mm_last": 0, "w3pe": 0, "h_act": 0, "ob_last": 0,
               "xq_used": 0}
        for qd in range(2):
            esQ = contextlib.ExitStack()
            H16 = esQ.enter_context(nc.sbuf_tensor(f"H16_{qd}",
                                                   [128, 4 * HID], BF16))

            def HT(th):
                return H16[:, th * HID:(th + 1) * HID]

            def XQ3F(th, j):
                return H16[:, th * HID + j * 128: th * HID + (j + 1) * 128]

            esW12 = contextlib.ExitStack()
            XQ12Q = [esW12.enter_context(
                nc.sbuf_tensor(f"XQQ{qd}{m}", [128, 4 * D], BF16))
                for m in range(2)]
            P12 = [[esW12.enter_context(
                nc.psum_tensor(f"P12_{qd}_{m}_{th}", [128, 512], F32))
                for th in range(4)] for m in range(2)]
            W12B = [[esW12.enter_context(
                nc.sbuf_tensor(f"W12B{qd}{m}{i}", [128, 16 * 512], BF16))
                for i in range(2)] for m in range(2)]
            SW = esW12.enter_context(nc.sbuf_tensor(f"SW{qd}", [128, 512], BF16))
            XV = esW12.enter_context(nc.sbuf_tensor(f"XV{qd}", [128, 512], BF16))

            def f(sync, qd=qd, xu=PRV["xq_used"]):
                if qd == 0:
                    wtt(sync, dS_xqf)
                    wt(sync, "act", stage2_act)
                    wtt(sync, wo1[0])
                    wtt(sync, wo1[1])
                if xu:
                    wt(sync, "pe", xu)
                for m in range(2):
                    for hf in range(2):
                        inc(sync.dma_start(
                            out=XQ12Q[m][:, hf * 2 * D:(hf + 1) * 2 * D],
                            in_=xqf_d[m].ap()[:, (qd * 4 + hf * 2) * D:
                                              (qd * 4 + hf * 2 + 2) * D]),
                            "sXQQ")
            S_(f)
            d_xqq = cum("sXQQ")
            if stop_after == "h_xq" and qd == 0:
                _finish_early()
                esW12.close(); esQ.close(); esH.close()
                return nc

            w12_dma = [None] * 16
            h_done = [[0] * 16 for _ in range(4)]
            mm_done = [[0] * 16 for _ in range(4)]
            for hc in range(16):
                def f(sync, qd=qd, hc=hc, pmm=PRV["mm_last"],
                      ob=PRV["ob_last"]):
                    if qd == 0 and hc == 0:
                        wtt(sync, cc_tern)
                    if qd == 0 and hc < 2:
                        wt(sync, "act", stage2_act)
                        wtt(sync, wo1[0])
                        wtt(sync, wo1[1])
                    if hc >= 2:
                        wt(sync, "pe", mm_done[3][hc - 2])
                    elif pmm:
                        wt(sync, "pe", pmm)
                        if ob:
                            wt(sync, "dve", ob)
                    for m in range(2):
                        for parity in range(2):
                            for uh in range(2):
                                outap = W12B[m][hc % 2][:].rearrange(
                                    "p (u q c) -> p u q c", q=2, c=512)[
                                    :, uh * 4:(uh + 1) * 4, parity, :]
                                inc(sync.dma_start(
                                    out=outap,
                                    in_=w12_src_ap(m, parity, hc, uh)),
                                    f"sW12_{hc % 2}")
                    w12_dma[hc] = cum(f"sW12_{hc % 2}")
                S_(f)

                for th in range(4):
                    for m in range(2):
                        def f(tensor, qd=qd, hc=hc, th=th, m=m,
                              hg=(h_done[th][hc - 2] if hc >= 2 else 0), dq=d_xqq):
                            wtt(tensor, w12_dma[hc])
                            if hc == 0 and th == 0 and m == 0:
                                wtt(tensor, dq)
                            if hg:
                                wt(tensor, "dve", hg)
                            for ic in range(DC):
                                inc(tensor.matmul(
                                    P12[m][th][:],
                                    XQ12Q[m][:, (th * DC + ic) * 128:
                                             (th * DC + ic + 1) * 128],
                                    W12B[m][hc % 2][:, ic * 512:(ic + 1) * 512],
                                    start=(ic == 0), stop=(ic == DC - 1)), "pe")
                        T_(f)
                        mm_done[th][hc] = CNT["pe"]

                    def f(scalar, qd=qd, hc=hc, th=th, mmd=mm_done[th][hc],
                          hprev=PRV["h_act"]):
                        wt(scalar, "pe", mmd)
                        if hprev:
                            wt(scalar, "dve", hprev)
                        inc(scalar.activation(
                            SW[:], P12[0][th][:], AF.Silu, bias=0.0,
                            scale=RC1[:, qd * 4 + th:qd * 4 + th + 1]), "act")
                    A_(f)
                    a_sw = CNT["act"]

                    def f(vector, qd=qd, hc=hc, th=th, a_sw=a_sw,
                          mmd=mm_done[th][hc]):
                        wt(vector, "pe", mmd)
                        inc(vector.tensor_scalar_mul(
                            XV[:], P12[1][th][:],
                            RC2[:, qd * 4 + th:qd * 4 + th + 1]), "dve")
                        wt(vector, "act", a_sw)
                        inc(vector.tensor_mul(HT(th)[:, hc * 512:(hc + 1) * 512],
                                              SW[:], XV[:]), "dve")
                        h_done[th][hc] = CNT["dve"]
                        PRV["h_act"] = CNT["dve"]
                    V_(f)
            PRV["mm_last"] = mm_done[3][15]
            esW12.close()
            if stop_after == "h_w12" and qd == 0:
                _finish_early()
                esQ.close(); esH.close()
                return nc

            if debug and qd == 0:
                def f(sync, hd=h_done[0][15]):
                    wt(sync, "dve", hd)
                    inc(sync.dma_start(out=dbg["h0"], in_=HT(0)[:]), "oodbg")
                S_(f)

            esQu = contextlib.ExitStack()
            YGH = esQu.enter_context(nc.sbuf_tensor(f"YGH{qd}", [128, HID], F32))
            YH2 = esQu.enter_context(nc.sbuf_tensor(f"YH2{qd}", [128, HID], F32))
            XQ8H = esQu.enter_context(nc.sbuf_tensor(f"XQ8H{qd}", [128, HID], BF16))
            AMX3 = esQu.enter_context(nc.sbuf_tensor(f"AMX3{qd}", [128, 1], F32))
            SSQ3 = esQu.enter_context(nc.sbuf_tensor(f"SSQ3{qd}", [128, 1], F32))
            RST3 = esQu.enter_context(nc.sbuf_tensor(f"RST3{qd}", [128, 1], F32))
            SCL3 = esQu.enter_context(nc.sbuf_tensor(f"SCL3{qd}", [128, 1], F32))
            esTP = contextlib.ExitStack()
            PTH = esTP.enter_context(nc.psum_tensor(f"PTH{qd}", [128, 128], BF16))

            q3_evac = [0]
            q3_done = [0] * 4
            prev_q8 = [0]
            for th in range(4):
                def f(scalar, th=th, hd=h_done[th][15], dg3=d_g3, qd=qd,
                      pq8=prev_q8[0]):
                    wt(scalar, "dve", hd)
                    if pq8:
                        wt(scalar, "dve", pq8)
                    if th == 0 and qd == 0:
                        wtt(scalar, dg3)
                    inc(scalar.activation(YGH[:], HT(th)[:], AF.Square,
                                          bias=0.0, scale=1.0, accum_out=SSQ3[:]),
                        "act")
                    inc(scalar.activation(RST3[:], SSQ3[:], AF.Sqrt, bias=EPS8[:],
                                          scale=1.0 / HID), "act")
                A_(f)
                a_r3 = CNT["act"]

                def f(vector, th=th, a_r3=a_r3, qd=qd):
                    wt(vector, "act", a_r3)
                    inc(vector.reciprocal(RST3[:], RST3[:]), "dve")
                    inc(vector.scalar_tensor_tensor(YGH[:], HT(th)[:], RST3[:],
                                                    G3B[:], ALU.mult, ALU.mult),
                        "dve")
                    inc(vector.reduce_max(AMX3[:], YGH[:], AX.X,
                                          apply_absolute_value=True), "dve")
                    inc(vector.tensor_scalar_max(AMX3[:], AMX3[:], 1e-4), "dve")
                    inc(vector.tensor_mul(RC3[:, qd * 4 + th:qd * 4 + th + 1],
                                          AMX3[:], BCS[:, 8:9]), "dve")
                V_(f)
                dve_a3 = CNT["dve"]

                def f(vector, dve_a3=dve_a3):
                    wt(vector, "dve", dve_a3)
                    inc(vector.reciprocal(SCL3[:], AMX3[:]), "dve")
                    inc(vector.tensor_scalar(YH2[:], YGH[:], SCL3[:], 127.0,
                                             ALU.mult, ALU.mult), "dve")
                    inc(vector.tensor_scalar(YGH[:], YH2[:], MAGIC, MAGIC,
                                             ALU.add, ALU.subtract), "dve")
                    inc(vector.tensor_scalar(XQ8H[:], YGH[:], 127.0, -128.0,
                                             ALU.min, ALU.max), "dve")
                V_(f)
                dve_q8 = CNT["dve"]
                prev_q8[0] = dve_q8

                for j in range(64):
                    def f(tensor, j=j, dve_q8=dve_q8, qe=q3_evac[-1]):
                        wt(tensor, "dve", dve_q8)
                        wt(tensor, "dve", qe)
                        inc(tensor.transpose(PTH[:],
                                             XQ8H[:, j * 128:(j + 1) * 128],
                                             IDN[:]), "pe")
                    T_(f)
                    pe_th = CNT["pe"]

                    def f(vector, th=th, j=j, pt=pe_th):
                        wt(vector, "pe", pt)
                        inc(vector.tensor_copy(XQ3F(th, j), PTH[:]), "dve")
                    V_(f)
                    q3_evac.append(CNT["dve"])
                q3_done[th] = CNT["dve"]
            if dbg2 and qd == 0:
                def f(sync, q3=q3_done[3]):
                    wt(sync, "dve", q3)
                    for pp in range(4):
                        inc(sync.dma_start(
                            out=dbg["d2_h16"][:, pp * HID:(pp + 1) * HID],
                            in_=H16[:, pp * HID:(pp + 1) * HID]), "oodbg")
                    inc(sync.dma_start(out=dbg["d2_rc"][:, 0:8], in_=RC1[:]),
                        "oodbg")
                    inc(sync.dma_start(out=dbg["d2_rc"][:, 8:16], in_=RC2[:]),
                        "oodbg")
                    inc(sync.dma_start(out=dbg["d2_rc"][:, 16:24], in_=RC3[:]),
                        "oodbg")
                S_(f)
            esTP.close()
            esQu.close()
            if stop_after == "h_q3" and qd == 0:
                _finish_early()
                esQ.close(); esH.close()
                return nc

            esW3 = contextlib.ExitStack()
            PO = [esW3.enter_context(nc.psum_tensor(f"PO{qd}{th}", [128, 1024], F32))
                  for th in range(4)]
            W3B = [esW3.enter_context(
                nc.sbuf_tensor(f"W3B{qd}{i}", [128, 4 * 1024], BF16))
                for i in range(2)]
            FO4 = [esW3.enter_context(nc.sbuf_tensor(f"FO4_{qd}{th}", [128, D], F32))
                   for th in range(4)]
            GT = esW3.enter_context(nc.sbuf_tensor(f"GT{qd}", [128, D], F32))
            SBF = esW3.enter_context(nc.sbuf_tensor(f"SBF{qd}", [128, D], BF16))
            SF32 = esW3.enter_context(nc.sbuf_tensor(f"SF32{qd}", [128, D], F32))
            XT2 = esW3.enter_context(nc.sbuf_tensor(f"XT2{qd}", [128, D], F32))
            OBUF = [esW3.enter_context(nc.sbuf_tensor(f"OB{qd}{i}", [128, D], F32))
                    for i in range(2)]

            w3_dma = [None] * 32
            w3_used = {}
            po_free = [0]
            for och in range(2):
                for rh in range(16):
                    t3 = och * 16 + rh
                    r, hh = rh // 2, rh % 2

                    def f(gpsimd, qd=qd, t3=t3, r=r, hh=hh, och=och,
                          q3=q3_done[3], g=PRV["w3pe"],
                          wg=(w3_used.get(t3 - 2, 0) if t3 >= 2 else 0)):
                        if t3 == 0:
                            wt(gpsimd, "dve", q3)
                        if wg:
                            wt(gpsimd, "pe", wg)
                        elif g:
                            wt(gpsimd, "pe", g)
                        for jh in range(2):
                            off = ((r * 768 + 512) * HID
                                   + (hh * 4 + jh * 2) * 32 * HID + och * 1024)
                            src = mkap(ag_t_out, off,
                                       [(HID, 32), (D, 4), (32 * HID, 2), (1, 1024)])
                            inc(gpsimd.dma_start(
                                out=W3B[t3 % 2][:, jh * 2 * 1024:(jh + 1) * 2 * 1024]
                                .rearrange("p (j c) -> p j c", j=2),
                                in_=src), f"sW3_{t3 % 2}")
                        w3_dma[t3] = cum(f"sW3_{t3 % 2}")
                    G_(f)

                    def f(tensor, qd=qd, t3=t3, rh=rh, r=r, hh=hh, och=och,
                          pf=po_free[0]):
                        wtt(tensor, w3_dma[t3])
                        if rh == 0 and pf:
                            wt(tensor, "dve", pf)
                        for js in range(4):
                            j = r * 8 + hh * 4 + js
                            for th in range(4):
                                for oc in range(2):
                                    inc(tensor.matmul(
                                        PO[th][:, oc * 512:(oc + 1) * 512],
                                        XQ3F(th, j),
                                        W3B[t3 % 2][:, js * 1024 + oc * 512:
                                                    js * 1024 + (oc + 1) * 512],
                                        start=(j == 0), stop=(j == 63)), "pe")
                    T_(f)
                    w3_used[t3] = CNT["pe"]
                pe_w3o = CNT["pe"]

                for th in range(4):
                    def f(vector, th=th, och=och, qd=qd, pw=pe_w3o):
                        wt(vector, "pe", pw)
                        inc(vector.tensor_scalar_mul(
                            FO4[th][:, och * 1024:(och + 1) * 1024], PO[th][:],
                            RC3[:, qd * 4 + th:qd * 4 + th + 1]), "dve")
                    V_(f)
                po_free[0] = CNT["dve"]
            pe_w3 = CNT["pe"]
            PRV["w3pe"] = pe_w3
            dve_fo4 = po_free[0]
            if stop_after == "h_w3" and qd == 0:
                _finish_early()
                esW3.close(); esQ.close(); esH.close()
                return nc

            for th in range(4):
                ti = qd * 4 + th

                def f(sync, ti=ti, g=PRV["ob_last"], q3=q3_done[3]):
                    wtt(sync, cc_rs)
                    wt(sync, "dve", q3)
                    if g:
                        wt(sync, "dve", g)
                    inc(sync.dma_start(
                        out=SBF[:], in_=rs_out.ap()[ti * 128:(ti + 1) * 128, :]),
                        "sFIN")
                    inc(sync.dma_start(out=XT2[:],
                                       in_=x_sl[ti * 128:(ti + 1) * 128, :]), "sFIN")
                S_(f)
                d_fin = cum("sFIN")

                def f(scalar, th=th, dve_fo4=dve_fo4, g=PRV["ob_last"]):
                    wt(scalar, "dve", dve_fo4)
                    if g:
                        wt(scalar, "dve", g)
                    inc(scalar.activation(GT[:], FO4[th][:], AF.Silu, bias=0.0,
                                          scale=1.0),
                        "act")
                A_(f)
                a_g = CNT["act"]

                def f(vector, ti=ti, th=th, a_g=a_g, d_fin=d_fin,
                      g=(out_dma[ti - 2] if ti >= 2 else None)):
                    wt(vector, "act", a_g)
                    wtt(vector, d_fin)
                    if g:
                        wtt(vector, g)
                    inc(vector.tensor_copy(SF32[:], SBF[:]), "dve")
                    inc(vector.tensor_mul(FO4[th][:], GT[:], SF32[:]), "dve")
                    inc(vector.tensor_add(OBUF[ti % 2][:], FO4[th][:], XT2[:]),
                        "dve")
                V_(f)
                ob_last = CNT["dve"]
                PRV["ob_last"] = ob_last

                def f(gpsimd, ti=ti, ou=ob_last):
                    wt(gpsimd, "dve", ou)
                    inc(gpsimd.dma_start(out=y_sl[ti * 128:(ti + 1) * 128, :],
                                         in_=OBUF[ti % 2][:]), f"sOO{ti % 2}")
                    out_dma.append(cum(f"sOO{ti % 2}"))
                G_(f)
            PRV["xq_used"] = pe_w3
            esW3.close()
            esQ.close()
            if stop_after == "h_fin" and qd == 0:
                _finish_early()
                esH.close()
                return nc
        esH.close()

        if dbg2:
            def f(sync):
                for nm in list(SEMS):
                    wt(sync, nm, CNT[nm])
                inc(sync.dma_start(out=dbg["d2_xn"], in_=ag_xn_out.ap()), "oodbg")
                inc(sync.dma_start(out=dbg["d2_s"], in_=rs_out.ap()), "oodbg")
                inc(sync.dma_start(out=dbg["d2_rsin"], in_=rs_in.ap()), "oodbg")
                inc(sync.dma_start(out=dbg["d2_xq"], in_=xqf_d[0].ap()), "oodbg")
                inc(sync.dma_start(out=dbg["d2_tern"],
                                   in_=ag_t_out.ap()[0:768, :]), "oodbg")
            S_(f)

        def f(gpsimd):
            for nm in list(SEMS):
                if nm.startswith("sOO") or nm == "oodbg":
                    wt(gpsimd, nm, CNT[nm])
        G_(f)

    return nc



_CACHED = {}


def _get_nc(debug=False):
    key = bool(debug)
    if key not in _CACHED:
        _CACHED[key] = build(debug=key)
    return _CACHED[key]


def make_in_maps(L0_attn_hidden_states, L0_attn_w_qkv, L0_attn_w_o,
                 L0_ffn_w1, L0_ffn_g1, L0_ffn_w2, L0_ffn_g2, L0_ffn_w3, L0_ffn_g3):
    bf = ml_dtypes.bfloat16
    x = np.ascontiguousarray(np.asarray(L0_attn_hidden_states, np.float32)
                             .reshape(B * S, D))
    wqkvT = np.ascontiguousarray(np.asarray(L0_attn_w_qkv, np.float32).T)
    woT = np.ascontiguousarray(np.asarray(L0_attn_w_o, np.float32).T)
    w1T = np.ascontiguousarray(np.asarray(L0_ffn_w1, np.float32).T)
    w2T = np.ascontiguousarray(np.asarray(L0_ffn_w2, np.float32).T)
    w3T = np.ascontiguousarray(np.asarray(L0_ffn_w3, np.float32).T)

    g1b = np.ascontiguousarray(np.tile(np.asarray(L0_ffn_g1, np.float32)[None, :],
                                       (128, 1)))
    g2b = np.ascontiguousarray(np.tile(np.asarray(L0_ffn_g2, np.float32)[None, :],
                                       (128, 1)))
    g3b = np.ascontiguousarray(np.tile(np.asarray(L0_ffn_g3, np.float32)[None, :],
                                       (128, 1)))
    ident = np.eye(128, dtype=np.float32).astype(bf)
    kk, qq = np.meshgrid(np.arange(128), np.arange(128), indexing="ij")
    maskt = np.where(kk <= qq, 0.0, -1e9).astype(np.float32)
    onec = np.ones((128, 1), np.float32)
    oner = np.ones((1, 128), np.float32)
    onecb = np.ones((128, 1), np.float32).astype(bf)
    onerb = np.ones((1, 128), np.float32).astype(bf)
    onesq = np.ones((128, 128), np.float32).astype(bf)

    in_maps = []
    for c in range(NCORES):
        h0 = 2 * c
        cols = []
        for h in (h0, h0 + 1):
            for off in (0, D, 2 * D):
                cols.append(wqkvT[:, off + h * 128: off + (h + 1) * 128])
        wqkv_sl_a = np.ascontiguousarray(np.concatenate(cols, axis=1))
        wo_sl_a = np.ascontiguousarray(woT[h0 * 128: h0 * 128 + 256, :])
        wf_sl_a = np.ascontiguousarray(np.concatenate([
            w1T[c * 256:(c + 1) * 256, :],
            w2T[c * 256:(c + 1) * 256, :],
            w3T[c * NTOK:(c + 1) * NTOK, :].reshape(256, HID)], axis=0))
        in_maps.append({
            "x_sl": np.ascontiguousarray(x[c * NTOK:(c + 1) * NTOK, :]),
            "wqkv_sl": wqkv_sl_a,
            "wo_sl": wo_sl_a,
            "wf_sl": wf_sl_a,
            "g1b": g1b, "g2b": g2b, "g3b": g3b,
            "ident": ident, "maskt": maskt,
            "onec": onec, "oner": oner, "onecb": onecb, "onerb": onerb,
            "onesq": onesq,
        })
    return in_maps


def kernel(L0_attn_hidden_states, L0_attn_w_qkv, L0_attn_w_o,
           L0_ffn_w1, L0_ffn_g1, L0_ffn_w2, L0_ffn_g2, L0_ffn_w3, L0_ffn_g3):
    in_maps = make_in_maps(L0_attn_hidden_states, L0_attn_w_qkv, L0_attn_w_o,
                           L0_ffn_w1, L0_ffn_g1, L0_ffn_w2, L0_ffn_g2,
                           L0_ffn_w3, L0_ffn_g3)
    nc = _get_nc(debug=False)
    res = run_bass_kernel_spmd(nc, in_maps, core_ids=list(range(NCORES)))
    out = np.concatenate([res.results[c]["y_sl"] for c in range(NCORES)], axis=0)
    return out.reshape(B, S, D)
